# version 27
# baseline (speedup 1.0000x reference)
import numpy as np
import ml_dtypes
import contextlib
import concourse.bass as bass
import concourse.mybir as mybir
from concourse.bass_utils import run_bass_kernel_spmd

bf16 = ml_dtypes.bfloat16
F32 = mybir.dt.float32
BF16 = mybir.dt.bfloat16
OP = mybir.AluOpType
AF = mybir.ActivationFunctionType
AX = mybir.AxisListType
AP = bass.AP

T, D, H, HD, I = 2048, 1024, 16, 64, 2816
NT = 16
EPS, CPc = 0.05, 1.05

_CACHE = {}


class Sem:
    def __init__(self, h):
        self.h = h
        self.n = 0
        self.max_wait = 0
        self.drained = 0

    def inc(self, k=1):
        self.n += k
        return self.n


def build_program():
    nc = bass.Bass("TRN2", target_bir_lowering=False)

    def ein(name, shape, dt=F32):
        return nc.dram_tensor(name, shape, dt, kind="ExternalInput").ap()

    x_in = ein("x_s", [T, D])
    wqT_i = ein("wqT", [D, D]); wkT_i = ein("wkT", [D, D])
    wvT_i = ein("wvT", [D, D]); wgT_i = ein("wgT", [D, D])
    woT_i = ein("woT", [D, D])
    wbfsT_i = ein("wbfsT", [D, 48])
    bias_in = ein("bias_bfs", [1, 48])
    wg_sh = ein("wg_sh", [384, D])
    wu_sh = ein("wu_sh", [384, D])
    wd_sh = ein("wd_sh", [128, I])
    c_mask = ein("c_mask", [128, 512], BF16)
    c_eye = ein("c_eye", [128, 128])
    c_ones = ein("c_ones", [1, 128], BF16)
    c_qtop0 = ein("c_qtop0", [128, 32])
    c_qbot0 = ein("c_qbot0", [128, 32])
    c_lwz = ein("c_lwz", [32, 8192])
    c_rws = ein("c_rws", [32, 2176])
    c_zb16 = ein("c_zb16", [1, 32768], BF16)
    c_zmask = ein("c_zmask", [16, 1])
    out_t = nc.dram_tensor("out", [T, D], F32, kind="ExternalOutput").ap()

    def scr(name, shape, dt):
        return nc.dram_tensor(name, shape, dt)

    d_xnb = scr("d_xnb", [T, D], BF16)
    d_xq = scr("d_xq", [T, D], BF16)
    d_u = scr("d_u", [T, D], BF16)
    d_q = scr("d_q", [T, D], BF16)
    d_g = scr("d_g", [T, D], BF16)
    d_fs = scr("d_fs", [48, T], F32)
    d_og = scr("d_og", [T, D], BF16)
    d_hq = scr("d_hq", [T, I], BF16)
    d_ds = scr("d_ds", [T, D], BF16)
    d_tg = scr("d_tg", [2, 32768], F32)
    d_th = scr("d_th", [2, 32768], F32)
    d_tr = scr("d_tr", [2, 32768], F32)
    d_tHfA = scr("d_tHfA", [2, 16384], F32)
    d_trBt = scr("d_trBt", [2, 16384], F32)
    d_trfl = scr("d_trfl", [2, 32768], BF16)
    d_q32 = scr("d_q32", [128, 32], F32)
    d_Braw = scr("d_Braw", [2, 65536], F32)
    d_zc = scr("d_zc", [2, 16 * 65 * 66], BF16)
    d_wqin = scr("d_wqin", [1056, 1024], BF16)
    d_wqout = scr("d_wqout", [8448, 1024], BF16)
    d_zin = scr("d_zin", [32, 64], BF16)
    d_zout = scr("d_zout", [64, 64], BF16)
    d_wgf = scr("d_wgf", [I, D], BF16)
    d_wuf = scr("d_wuf", [I, D], BF16)
    d_wdf = scr("d_wdf", [D, I], BF16)
    d_LWf = scr("d_LWf", [32, 32768], F32)
    d_RWf = scr("d_RWf", [32, 34816], F32)
    d_L2f = scr("d_L2f", [8, 32768], BF16)
    d_R2f = scr("d_R2f", [8, 16384], BF16)

    plan = {k: [] for k in ["sync", "tensor", "vector", "scalar", "gpsimd"]}
    stack0 = contextlib.ExitStack()

    sems = {s: Sem(stack0.enter_context(nc.semaphore(s)))
            for s in ["qS", "qG", "qA", "qV", "sPE", "sDV", "sAC", "sCC"]}
    qS, qG, qA, qV = sems["qS"], sems["qG"], sems["qA"], sems["qV"]
    sPE, sDV, sAC, sCC = sems["sPE"], sems["sDV"], sems["sAC"], sems["sCC"]

    def E(eng, fn):
        if eng == "vector" or eng == "scalar":
            s = sDV if eng == "vector" else sAC
            v = s.n
            plan[eng].append(lambda e, ss=s.h, vv=v: e.wait_ge(ss, vv))
            tgt = s.inc(1)
            plan[eng].append(lambda e, f=fn, ss=s.h: f(e).then_inc(ss, 1))
            return tgt
        plan[eng].append(fn)
        return None

    def dma(eng, q, dst, src, transpose=False, sp=True):
        tgt = q.inc(16)
        if transpose:
            plan[eng].append(
                lambda e, d=dst, s=src: e.dma_start(d, s, transpose=True)
                .then_inc(q.h, 16))
        else:
            plan[eng].append(lambda e, d=dst, s=src: e.dma_start(d, s)
                             .then_inc(q.h, 16))
        if sp:
            plan[eng].append(lambda e, s=q.h, vv=tgt: e.wait_ge(s, vv))
            q.drained = tgt
        return tgt

    def wait(eng, sem, val):
        sem.max_wait = max(sem.max_wait, val)
        plan[eng].append(lambda e, s=sem.h, v=val: e.wait_ge(s, v))

    def op(eng, fn, sem=None):
        if eng == "vector" or eng == "scalar":
            return E(eng, fn)
        if sem is None:
            E(eng, fn)
            return None
        tgt = sem.inc(1)
        plan[eng].append(lambda e, f=fn, s=sem.h: f(e).then_inc(s, 1))
        return tgt

    def mm(psum_ap, lhsT_ap, rhs_ap, start, stop, sem=None, skip=False):
        if sem is None:
            E("tensor", lambda e, o=psum_ap, l=lhsT_ap, r=rhs_ap, st=start,
              sp=stop, sk=skip: e.matmul(o, l, r, start=st, stop=sp,
                                         skip_group_check=sk))
            return None
        tgt = sem.inc(1)
        E("tensor", lambda e, o=psum_ap, l=lhsT_ap, r=rhs_ap, st=start,
          sp=stop, s=sem.h, sk=skip: e.matmul(
              o, l, r, start=st, stop=sp, skip_group_check=sk).then_inc(s, 1))
        return tgt

    def TTo(o, a, b, alu, sem=None):
        return op("vector", lambda e, oo=o, aa=a, bb=b, al=alu:
                  e.tensor_tensor(oo, aa, bb, al), sem)

    def STT(o, a, s, b, op0, op1, sem=None):
        return op("vector", lambda e, oo=o, aa=a, ss=s, bb=b, o0=op0, o1=op1:
                  e.scalar_tensor_tensor(oo, aa, ss, bb, o0, o1), sem)

    def TS(o, a, s1, s2, op0, op1=None, sem=None):
        if op1 is None:
            return op("vector", lambda e, oo=o, aa=a, a1=s1, o0=op0:
                      e.tensor_scalar(oo, aa, a1, None, o0), sem)
        return op("vector", lambda e, oo=o, aa=a, a1=s1, a2=s2, o0=op0,
                  o1=op1: e.tensor_scalar(oo, aa, a1, a2, o0, o1), sem)

    def CP(o, a, sem=None):
        return op("vector", lambda e, oo=o, aa=a: e.tensor_copy(oo, aa), sem)

    def sb(st, name, shape, dt):
        return st.enter_context(nc.sbuf_tensor(name, shape, dt))

    g0 = stack0
    xt_f = [sb(g0, f"xt{i}", [128, D], F32) for i in range(2)]
    xn_f = sb(g0, "xn_f", [128, D], F32)
    kvb = [sb(g0, f"kvb{i}", [128, D], BF16) for i in range(4)]
    res_f = sb(g0, "res_f", [128, D], F32)
    out_f = [sb(g0, f"outf{i}", [128, D], F32) for i in range(2)]
    ds_rd = sb(g0, "ds_rd", [128, D], F32)
    wsh_f = sb(g0, "wsh_f", [128, 1024], F32)
    col_a = sb(g0, "col_a", [128, 1], F32)
    col_b = sb(g0, "col_b", [128, 1], F32)
    col_c = sb(g0, "col_c", [128, 1], F32)
    col_d = sb(g0, "col_d", [128, 1], F32)
    qy = sb(g0, "qy", [128, 1024], F32)
    qv = sb(g0, "qv", [128, 1024], F32)
    qsg = sb(g0, "qsg", [128, 1024], BF16)
    qa_ = sb(g0, "qa_", [128, 1024], BF16)
    qb_ = sb(g0, "qb_", [128, 1024], BF16)
    qrB = sb(g0, "qrB", [128, 1024], BF16)
    qu2 = sb(g0, "qu2", [128, 1024], BF16)
    qz_ = sb(g0, "qz_", [128, 1024], BF16)
    qout = sb(g0, "qout", [128, 2048], BF16)
    amax = sb(g0, "amax", [128, 64], F32)
    scl = sb(g0, "scl", [128, 64], F32)
    rscl = sb(g0, "rscl", [128, 64], F32)
    scl16 = sb(g0, "scl16", [128, 64], BF16)
    mask_s = sb(g0, "mask_s", [128, 512], BF16)
    eye_s = sb(g0, "eye_s", [128, 128], F32)
    ones_s = sb(g0, "ones_s", [1, 128], BF16)
    bias_f = sb(g0, "bias_f", [1, 48], F32)
    bias_s = sb(g0, "bias_s", [1, 48], BF16)
    zmask_s = sb(g0, "zmask_s", [16, 1], F32)
    q32ch = sb(g0, "q32ch", [16, 256], F32)

    PS = [stack0.enter_context(nc.psum_tensor(f"PS{i}", [128, 512], F32))
          for i in range(8)]

    dma("sync", qS, eye_s[:], c_eye, sp=False)
    dma("sync", qS, mask_s[:], c_mask, sp=False)
    dma("sync", qS, ones_s[:], c_ones, sp=False)
    dma("sync", qS, bias_f[:], bias_in, sp=False)
    dma("sync", qS, zmask_s[:], c_zmask)
    t_consts0 = qS.n
    wait("sync", qS, t_consts0)
    xload = [dma("sync", qS, xt_f[0][:], x_in[0:128, :])]

    quant_state = {}

    def quant_tile(xin_ap, w, oq):
        nblk = w // 16
        x3 = xin_ap.rearrange("p (n k) -> p n k", k=16)
        am, sc_ = amax[:, 0:nblk], scl[:, 0:nblk]
        rs_, s16 = rscl[:, 0:nblk], scl16[:, 0:nblk]
        E("vector", lambda e, a=x3, b=am: e.tensor_reduce(
            b, a, AX.X, OP.max, apply_absolute_value=True))
        TS(sc_, am, 1.0 / 6.0, 1e-12, OP.mult, OP.max)
        E("vector", lambda e, a=sc_, b=rs_: e.reciprocal(b, a))
        rs_b = AP(rscl, 0, [[64, 128], [1, nblk], [0, 16]])
        sc_b = AP(scl, 0, [[64, 128], [1, nblk], [0, 16]])
        y3 = qy[:, 0:w].rearrange("p (n k) -> p n k", k=16)
        dvm = TTo(y3, x3, rs_b, OP.mult, sem=sDV)
        quant_state["y_mark"] = dvm
        yw, vw, sg = qy[:, 0:w], qv[:, 0:w], qsg[:, 0:w]
        wait("scalar", sDV, dvm)
        absm = op("scalar", lambda e, a=yw, o=vw: e.activation(o, a, AF.Abs),
                  sAC)
        acm = op("scalar", lambda e, a=yw, o=sg: e.activation(o, a, AF.Sign),
                 sAC)
        wait("vector", sAC, absm)
        aw, bw, zw = qa_[:, 0:w], qb_[:, 0:w], qz_[:, 0:w]
        TS(aw, vw, 96.0, None, OP.add)
        TS(bw, aw, -96.0, 2.0, OP.add, OP.min)
        STT(aw, vw, 2.5, bw, OP.is_gt, OP.add)
        STT(bw, vw, 3.5, aw, OP.is_gt, OP.add)
        STT(aw, vw, 5.0, bw, OP.is_gt, OP.add)
        STT(bw, vw, 5.0, aw, OP.is_gt, OP.add)
        wait("vector", sAC, acm)
        TTo(zw, bw, sg, OP.mult)
        z3 = qz_[:, 0:w].rearrange("p (n k) -> p n k", k=16)
        qo3 = qout[:, oq:oq + w].rearrange("p (n k) -> p n k", k=16)
        return TTo(qo3, z3, sc_b, OP.mult, sem=sDV)

    prev_qm = None
    wq_w = []
    shi = 0
    jobs = []
    for ti in range(3):
        jobs.append((wg_sh[128 * ti:128 * ti + 128, :], 128 * ti, 1024,
                     96 if ti == 2 else 128))
    for ti in range(3):
        jobs.append((wu_sh[128 * ti:128 * ti + 128, :], 352 + 128 * ti, 1024,
                     96 if ti == 2 else 128))
    for cc3 in range(3):
        cw3 = 768 if cc3 == 2 else 1024
        jobs.append((wd_sh[:, 1024 * cc3:1024 * cc3 + cw3],
                     ("wd", cc3, cw3), cw3, 128))
    for (srcap, dsti, wc, nrow) in jobs:
        if prev_qm is not None:
            wait("scalar", sDV, quant_state["y_mark"])
        t = dma("scalar", qA, wsh_f[:, 0:wc], srcap)
        wait("vector", qA, t)
        oq = (shi % 2) * 1024
        if shi >= 2:
            wait("vector", qS, wq_w[shi - 2])
        qm = quant_tile(wsh_f[:, 0:wc], wc, oq)
        prev_qm = qm
        wait("sync", sDV, qm)
        if isinstance(dsti, tuple):
            _, cc3, cw3 = dsti
            wq_w.append(dma("sync", qS,
                            AP(d_wqin, 704 * 1024 + 1024 * cc3,
                               [[2816, 128], [1, cw3]]),
                            qout[:, oq:oq + cw3]))
        else:
            wq_w.append(dma("sync", qS,
                            AP(d_wqin, dsti * 1024, [[1024, nrow], [1, 1024]]),
                            qout[0:nrow, oq:oq + 1024]))
        shi += 1
    t_wqin = qS.n

    gp = stack0.enter_context(contextlib.ExitStack())
    xn_fm = sb(gp, "xn_fm", [128, 8 * T], BF16)
    wmovA = sb(gp, "wmovA", [128, 8 * D], BF16)
    wmovB = sb(gp, "wmovB", [128, 8 * D], BF16)
    wbfs_s = sb(gp, "wbfs_s", [128, 8 * 48], BF16)
    bfs_tm = sb(gp, "bfs_tm", [128, NT * 48], F32)
    fs_st = sb(gp, "fs_st", [48, T], F32)
    wtmp = [sb(gp, f"wtmp{i}", [128, D], F32) for i in range(4)]
    ustg = sb(gp, "ustg", [128, 2048], BF16)
    qstg = sb(gp, "qstg", [128, 2048], BF16)
    gstg = sb(gp, "gstg", [128, 2048], BF16)
    f_sc = sb(gp, "f_sc", [128, 256], F32)
    s_sc = sb(gp, "s_sc", [128, 256], F32)
    QTop = sb(gp, "QTop", [128, 544], F32)
    QBot = sb(gp, "QBot", [128, 544], F32)
    SC0 = sb(gp, "SC0", [128, 544], F32)
    SC1 = sb(gp, "SC1", [128, 544], F32)
    t_g = sb(gp, "t_g", [128, 512], F32)
    t_h = sb(gp, "t_h", [128, 512], F32)
    t_r = sb(gp, "t_r", [128, 512], F32)
    t_HfA = sb(gp, "t_HfA", [128, 256], F32)
    t_rBt = sb(gp, "t_rBt", [128, 256], F32)
    t_rfl = sb(gp, "t_rfl", [128, 512], F32)
    t_det = sb(gp, "t_det", [128, 256], F32)
    t_rdet = sb(gp, "t_rdet", [128, 256], F32)
    t_w1 = sb(gp, "t_w1", [128, 544], F32)
    t_w2 = sb(gp, "t_w2", [128, 544], F32)
    t_bf = sb(gp, "t_bf", [128, 512], BF16)
    q32 = sb(gp, "q32", [128, 32], F32)

    def qslot(tsr, j):
        return AP(tsr, j, [[544, 128], [272, 2], [17, 16]])

    def qhalf(tsr, j, pair):
        return AP(tsr, j + 272 * pair, [[544, 128], [17, 16]])

    E("vector", lambda e: e.memset(qhalf(QTop, 0, 0), 1.0))
    E("vector", lambda e: e.memset(qhalf(QTop, 0, 1), 0.0))
    E("vector", lambda e: e.memset(qhalf(QBot, 0, 0), 0.0))
    E("vector", lambda e: e.memset(qhalf(QBot, 0, 1), 1.0))
    E("vector", lambda e: e.memset(qhalf(SC0, 16, 0), 1.0))
    E("vector", lambda e: e.memset(qhalf(SC0, 16, 1), 0.0))
    E("vector", lambda e: e.memset(qhalf(SC1, 16, 0), 0.0))
    E("vector", lambda e: e.memset(qhalf(SC1, 16, 1), 1.0))

    wjobs = []
    for src_, dst, ncol in [(wkT_i, wmovA, D), (wvT_i, wmovB, D),
                            (wbfsT_i, wbfs_s, 48)]:
        for b in range(8):
            wjobs.append((src_, dst, ncol, b))
    for src_, dst, ncol in [(wqT_i, wmovA, D), (wgT_i, wmovB, D)]:
        for b in range(8):
            wjobs.append((src_, dst, ncol, b))
    NW = len(wjobs)
    wload_t, wcast_t = [], []

    def emit_wjob(j):
        src_, dst, ncol, b = wjobs[j]
        if j >= 4:
            wait("gpsimd", sDV, wcast_t[j - 4])
        if ncol == D:
            t = dma("gpsimd", qG, wtmp[j % 4][:],
                    src_[128 * b:128 * b + 128, :])
        else:
            t = dma("gpsimd", qG, wtmp[j % 4][:, 0:48],
                    src_[128 * b:128 * b + 128, :])
        wload_t.append(t)
        wait("vector", qG, max(wload_t[min(3, j)], t))
        if ncol == D:
            wcast_t.append(CP(dst[:, D * b:D * b + D], wtmp[j % 4][:],
                              sem=sDV))
        else:
            wcast_t.append(CP(dst[:, 48 * b:48 * b + 48],
                              wtmp[j % 4][:, 0:48], sem=sDV))

    for j in range(24):
        emit_wjob(j)
    wait("vector", qS, t_consts0)
    bias_cast = CP(bias_s[:], bias_f[:])
    wait("gpsimd", qS, t_wqin)
    cc1 = sCC.inc(1)
    E("gpsimd", lambda e: e.collective_compute(
        "AllGather", OP.bypass, replica_groups=[[0, 1, 2, 3, 4, 5, 6, 7]],
        ins=[d_wqin.ap().opt()], outs=[d_wqout.ap().opt()]).then_inc(sCC.h, 1))
    wait("gpsimd", sCC, cc1)
    for r in range(8):
        dma("gpsimd", qG, AP(d_wgf, 352 * 1024 * r, [[1024, 352], [1, 1024]]),
            AP(d_wqout, 1056 * 1024 * r, [[1024, 352], [1, 1024]]), sp=False)
        dma("gpsimd", qG, AP(d_wuf, 352 * 1024 * r, [[1024, 352], [1, 1024]]),
            AP(d_wqout, (1056 * r + 352) * 1024, [[1024, 352], [1, 1024]]),
            sp=False)
        dma("gpsimd", qG, AP(d_wdf, 128 * 2816 * r, [[2816, 128], [1, 2816]]),
            AP(d_wqout, (1056 * r + 704) * 1024, [[2816, 128], [1, 2816]]),
            sp=(r == 7))
    t_repack = qG.n

    ph1_marks = []
    xnb_w, xq_w = [], []
    for i in range(NT + 1):
        if i < NT:
            xt = xt_f[i % 2]
            xbuf = xn_f if i % 2 == 0 else ds_rd
            cb = col_b if i % 2 == 0 else col_d
            wait("vector", qS, xload[i])
            E("vector", lambda e, a=xt: e.tensor_reduce(
                col_a[:], a[:], AX.X, OP.add))
            TS(col_a[:], col_a[:], -1.0 / D, None, OP.mult)
            dv_mean = TS(xbuf[:], xt[:], col_a[:], None, OP.add, sem=sDV)
            wait("scalar", sDV, dv_mean)
            ac_sq = op("scalar", lambda e, xb=xbuf, c=cb: e.activation(
                qrB[:], xb[:], AF.Square, accum_out=c[:]), sAC)
            ph1_marks.append((dv_mean, ac_sq))
            if i + 1 < NT:
                if i >= 1:
                    wait("sync", sDV, ph1_marks[i - 1][0])
                xload.append(dma("sync", qS, xt_f[(i + 1) % 2][:],
                                 x_in[128 * (i + 1):128 * (i + 2), :]))
        if i >= 1:
            j = i - 1
            xbufj = xn_f if j % 2 == 0 else ds_rd
            cbj = col_b if j % 2 == 0 else col_d
            wait("vector", sAC, ph1_marks[j][1])
            dvm = TS(cbj[:], cbj[:], 1.0 / D, 1e-5, OP.mult, OP.add,
                     sem=sDV)
            wait("scalar", sDV, dvm)
            E("scalar", lambda e, c=cbj: e.activation(col_c[:], c[:], AF.Ln))
            acm = op("scalar", lambda e: e.activation(
                col_c[:], col_c[:], AF.Exp, scale=-0.5), sAC)
            wait("vector", sAC, acm)
            TS(xbufj[:], xbufj[:], col_c[:], None, OP.mult)
            oh = (j % 2) * 1024
            if j >= 1:
                wait("vector", qS, xnb_w[j - 1])
            dvm = CP(qu2[:], xbufj[:], sem=sDV)
            wait("sync", sDV, dvm)
            xnb_w.append(dma("sync", qS,
                             d_xnb.ap()[128 * j:128 * j + 128, :], qu2[:]))
            if j >= 2:
                wait("vector", qS, xq_w[j - 2])
            else:
                wait("vector", qS, wq_w[7 + j])
            qm = quant_tile(xbufj[:], D, oh)
            wait("sync", sDV, qm)
            xq_w.append(dma("sync", qS,
                            d_xq.ap()[128 * j:128 * j + 128, :],
                            qout[:, oh:oh + 1024]))
    ph1_qs = qS.n



    wait("sync", qS, ph1_qs)
    for b in range(8):
        dma("sync", qS, xn_fm[:, T * b:T * b + T],
            AP(d_xnb, 128 * b, [[1024, 2048], [1, 128]]), transpose=True,
            sp=True)
    t_xnfm = qS.n

    wait("tensor", qS, t_xnfm)
    wait("tensor", sDV, bias_cast)
    bfs_ac = []
    for i in range(NT):
        pb_ = PS[4] if i % 2 == 0 else PS[5]
        if i >= 2:
            wait("tensor", sAC, bfs_ac[i - 2])
        for b in range(8):
            lhs = xn_fm[:, T * b + 128 * i: T * b + 128 * i + 128]
            mm(pb_[:, 0:48], lhs, wbfs_s[:, 48 * b:48 * b + 48],
               b == 0, False, skip=True)
        bfs_pe = mm(pb_[:, 0:48], ones_s[:], bias_s[:], False,
                    True, sem=sPE, skip=True)
        wait("scalar", sPE, bfs_pe)
        bac = op("scalar", lambda e, ii=i, pp=pb_: e.activation(
            bfs_tm[:, 48 * ii:48 * ii + 48], pp[:, 0:48], AF.Tanh,
            scale=0.5), sAC)
        bfs_ac.append(bac)
    wait("vector", sAC, bfs_ac[-1])
    bdv = TS(bfs_tm[:], bfs_tm[:], 0.5, 0.5, OP.mult, OP.add, sem=sDV)
    fs_dv = []
    wait("tensor", sDV, bdv)
    for i in range(NT):
        pt_ = PS[6] if i % 2 == 0 else PS[7]
        if i >= 2:
            wait("tensor", sDV, fs_dv[i - 2])
        trm = op("tensor", lambda e, ii=i, pp=pt_: e.transpose(
            pp[0:48, 0:128], bfs_tm[:, 48 * ii:48 * ii + 48],
            eye_s[:]), sPE)
        wait("vector", sPE, trm)
        dvm = CP(fs_st[:, 128 * i:128 * i + 128], pt_[0:48, 0:128],
                 sem=sDV)
        fs_dv.append(dvm)
    wait("sync", sDV, fs_dv[-1])
    dma("sync", qS, d_fs.ap()[:, :], fs_st[:])
    wait("sync", qS, qS.n)
    for hh in range(16):
        dma("sync", qS, f_sc[8 * hh:8 * hh + 8, :],
            AP(d_fs, (16 + hh) * 2048, [[256, 8], [1, 256]]), sp=False)
        dma("sync", qS, s_sc[8 * hh:8 * hh + 8, :],
            AP(d_fs, (32 + hh) * 2048, [[256, 8], [1, 256]]), sp=(hh == 15))
    t_fsc = qS.n
    wait("tensor", sAC, bfs_ac[-1])
    wait("tensor", sDV, fs_dv[-1])
    u_w, u_dv, kv_ac = [], [], []
    for i in range(NT):
        pk = (PS[0], PS[1]) if i % 2 == 0 else (PS[4], PS[5])
        pv = (PS[2], PS[3]) if i % 2 == 0 else (PS[6], PS[7])
        if i >= 2:
            wait("tensor", sAC, kv_ac[i - 2])
        for b in range(8):
            lhs = xn_fm[:, T * b + 128 * i: T * b + 128 * i + 128]
            mm(pk[0][:], lhs, wmovA[:, D * b:D * b + 512], b == 0, b == 7)
            k_pe = mm(pk[1][:], lhs, wmovA[:, D * b + 512:D * b + 1024],
                      b == 0, b == 7, sem=sPE if b == 7 else None)
        for b in range(8):
            lhs = xn_fm[:, T * b + 128 * i: T * b + 128 * i + 128]
            mm(pv[0][:], lhs, wmovB[:, D * b:D * b + 512], b == 0, b == 7)
            v_pe = mm(pv[1][:], lhs, wmovB[:, D * b + 512:D * b + 1024],
                      b == 0, b == 7, sem=sPE if b == 7 else None)
        kb_, vb_ = kvb[i % 2], kvb[2 + i % 2]
        if i >= 2:
            wait("scalar", sDV, u_dv[i - 2])
        wait("scalar", sPE, k_pe)
        E("scalar", lambda e, kk=kb_, p0=pk[0]: e.activation(
            kk[:, 0:512], p0[:], AF.Silu))
        E("scalar", lambda e, kk=kb_, p1=pk[1]: e.activation(
            kk[:, 512:1024], p1[:], AF.Silu))
        wait("scalar", sPE, v_pe)
        E("scalar", lambda e, vv=vb_, p0=pv[0]: e.activation(
            vv[:, 0:512], p0[:], AF.Silu))
        vac = op("scalar", lambda e, vv=vb_, p1=pv[1]: e.activation(
            vv[:, 512:1024], p1[:], AF.Silu), sAC)
        kv_ac.append(vac)
        wait("vector", sAC, vac)
        if i >= 2:
            wait("vector", qS, u_w[i - 2])
        bbc = AP(bfs_tm, 48 * i, [[NT * 48, 128], [1, 16], [0, 64]])
        k3 = kb_[:].rearrange("p (h d) -> p h d", d=64)
        v3 = vb_[:].rearrange("p (h d) -> p h d", d=64)
        oh = (i % 2) * 1024
        u3 = ustg[:, oh:oh + 1024].rearrange("p (h d) -> p h d", d=64)
        y3d = qy[:].rearrange("p (h d) -> p h d", d=64)
        TTo(y3d, k3, bbc, OP.mult)
        udv = TTo(u3, y3d, v3, OP.mult, sem=sDV)
        u_dv.append(udv)
        wait("sync", sDV, udv)
        u_w.append(dma("sync", qS, d_u.ap()[128 * i:128 * i + 128, :],
                       ustg[:, oh:oh + 1024]))
    wait("vector", sPE, sPE.n)
    for j in range(24, NW):
        emit_wjob(j)
    wait("tensor", sDV, wcast_t[-1])
    wait("tensor", sAC, kv_ac[-1])
    wait("scalar", sDV, u_dv[-1])
    q_w, g_w = [], []
    qg_ac, qg_dv = [], []
    for i in range(NT):
        pq = (PS[0], PS[1]) if i % 2 == 0 else (PS[4], PS[5])
        pg = (PS[2], PS[3]) if i % 2 == 0 else (PS[6], PS[7])
        if i >= 2:
            wait("tensor", sAC, qg_ac[i - 2])
        for b in range(8):
            lhs = xn_fm[:, T * b + 128 * i: T * b + 128 * i + 128]
            mm(pq[0][:], lhs, wmovA[:, D * b:D * b + 512], b == 0, b == 7)
            q_pe = mm(pq[1][:], lhs, wmovA[:, D * b + 512:D * b + 1024],
                      b == 0, b == 7, sem=sPE if b == 7 else None)
        for b in range(8):
            lhs = xn_fm[:, T * b + 128 * i: T * b + 128 * i + 128]
            mm(pg[0][:], lhs, wmovB[:, D * b:D * b + 512], b == 0, b == 7)
            g_pe = mm(pg[1][:], lhs, wmovB[:, D * b + 512:D * b + 1024],
                      b == 0, b == 7, sem=sPE if b == 7 else None)
        oh = (i % 2) * 1024
        gscr = kvb[i % 2]
        if i >= 2:
            wait("scalar", qS, q_w[i - 2])
            wait("scalar", sDV, qg_dv[i - 2])
        wait("scalar", sPE, q_pe)
        E("scalar", lambda e, o=oh, p0=pq[0]: e.activation(
            qstg[:, o:o + 512], p0[:], AF.Silu))
        E("scalar", lambda e, o=oh, p1=pq[1]: e.activation(
            qstg[:, o + 512:o + 1024], p1[:], AF.Silu))
        wait("scalar", sPE, g_pe)
        E("scalar", lambda e, gg=gscr, p0=pg[0]: e.activation(
            gg[:, 0:512], p0[:], AF.Tanh, scale=0.5))
        gac = op("scalar", lambda e, gg=gscr, p1=pg[1]: e.activation(
            gg[:, 512:1024], p1[:], AF.Tanh, scale=0.5), sAC)
        qg_ac.append(gac)
        wait("vector", sAC, gac)
        if i >= 2:
            wait("vector", qS, g_w[i - 2])
        gdv = TS(gstg[:, oh:oh + 1024], gscr[:], 0.5, 0.5,
                 OP.mult, OP.add, sem=sDV)
        qg_dv.append(gdv)
        wait("sync", sAC, gac)
        q_w.append(dma("sync", qS, d_q.ap()[128 * i:128 * i + 128, :],
                       qstg[:, oh:oh + 1024]))
        wait("sync", sDV, gdv)
        g_w.append(dma("sync", qS, d_g.ap()[128 * i:128 * i + 128, :],
                       gstg[:, oh:oh + 1024]))
    t_passB = sPE.n

    wait("vector", qS, t_fsc)

    def fsl(t2, j):
        return AP(t2, j, [[256, 128], [0, 2], [16, 16]])

    tmpA = AP(t_w1, 0, [[544, 128], [272, 2], [17, 16]])
    tmpB = AP(t_w2, 0, [[544, 128], [272, 2], [17, 16]])
    for j in range(1, 17):
        TTo(tmpA, fsl(f_sc, j - 1), qslot(QTop, j - 1), OP.mult)
        TTo(tmpB, fsl(s_sc, j - 1), qslot(QBot, j - 1), OP.mult)
        STT(qslot(QTop, j), tmpB, EPS, tmpA, OP.mult, OP.add)
        STT(qslot(QBot, j), tmpA, EPS, tmpB, OP.mult, OP.add)
    for jj in range(16):
        i2 = 15 - jj
        STT(tmpA, qslot(SC1, i2 + 1), EPS, qslot(SC0, i2 + 1),
            OP.mult, OP.add)
        STT(tmpB, qslot(SC0, i2 + 1), EPS, qslot(SC1, i2 + 1),
            OP.mult, OP.add)
        TTo(qslot(SC0, i2), fsl(f_sc, i2), tmpA, OP.mult)
        TTo(qslot(SC1, i2), fsl(s_sc, i2), tmpB, OP.mult)

    def slots16(tsr, j0):
        return AP(tsr, j0, [[544, 128], [272, 2], [17, 16], [1, 16]])

    th_ap = AP(t_h, 0, [[512, 128], [256, 2], [16, 16], [1, 16]])
    TTo(th_ap, slots16(SC0, 1), slots16(SC1, 1), OP.add)
    TS(th_ap, th_ap, CPc, None, OP.mult)

    def pp0(tsr, j0):
        return AP(tsr, j0, [[544, 128], [17, 16], [1, 16]])

    def pp1(tsr, j0):
        return AP(tsr, j0 + 272, [[544, 128], [17, 16], [1, 16]])

    tdet_ap = AP(t_det, 0, [[256, 128], [16, 16], [1, 16]])
    trd_ap = AP(t_rdet, 0, [[256, 128], [16, 16], [1, 16]])
    w1s = AP(t_w1, 0, [[544, 128], [16, 16], [1, 16]])
    w2s = AP(t_w2, 0, [[544, 128], [16, 16], [1, 16]])
    TTo(w1s, pp0(QTop, 1), pp1(QBot, 1), OP.mult)
    TTo(w2s, pp1(QTop, 1), pp0(QBot, 1), OP.mult)
    TTo(tdet_ap, w1s, w2s, OP.subtract)
    E("vector", lambda e: e.reciprocal(trd_ap, tdet_ap))
    tg_F = AP(t_g, 0, [[512, 128], [16, 16], [1, 16]])
    tg_S = AP(t_g, 256, [[512, 128], [16, 16], [1, 16]])
    TTo(w1s, pp1(QBot, 1), pp1(QTop, 1), OP.subtract)
    TTo(w1s, w1s, trd_ap, OP.mult)
    TS(tg_F, w1s, CPc, None, OP.mult)
    TTo(w2s, pp0(QTop, 1), pp0(QBot, 1), OP.subtract)
    TTo(w2s, w2s, trd_ap, OP.mult)
    TS(tg_S, w2s, CPc, None, OP.mult)
    f16a = AP(f_sc, 0, [[256, 128], [0, 2], [16, 16], [1, 16]])
    s16a = AP(s_sc, 0, [[256, 128], [0, 2], [16, 16], [1, 16]])
    tr_ap = AP(t_r, 0, [[512, 128], [256, 2], [16, 16], [1, 16]])
    w14 = AP(t_w1, 0, [[544, 128], [272, 2], [17, 16], [1, 16]])
    w24 = AP(t_w2, 0, [[544, 128], [272, 2], [17, 16], [1, 16]])
    TTo(w14, f16a, slots16(QTop, 0), OP.mult)
    TTo(w24, s16a, slots16(QBot, 0), OP.mult)
    TTo(tr_ap, AP(t_w1, 0, [[544, 128], [272, 2], [17, 16], [1, 16]]),
        AP(t_w2, 0, [[544, 128], [272, 2], [17, 16], [1, 16]]), OP.add)
    qb_e, qa_e = {}, {}
    for (nm, tsr, offp) in [("00", QTop, 0), ("01", QTop, 272),
                            ("10", QBot, 0), ("11", QBot, 272)]:
        qb_e[nm] = AP(tsr, offp + 17 + 16, [[544, 128], [34, 8]])
        qa_e[nm] = AP(tsr, offp + 16, [[544, 128], [34, 8]])

    def bc16(a2):
        return AP(a2.tensor, a2.offset, [a2.ap[0], a2.ap[1], [0, 16]])

    hA_F = AP(t_h, 0, [[512, 128], [32, 8], [1, 16]])
    hA_S = AP(t_h, 256, [[512, 128], [32, 8], [1, 16]])
    Hf_F = AP(t_HfA, 0, [[256, 128], [16, 8], [1, 16]])
    Hf_S = AP(t_HfA, 128, [[256, 128], [16, 8], [1, 16]])
    w1h = AP(t_w1, 0, [[544, 128], [16, 8], [1, 16]])
    w2h = AP(t_w2, 0, [[544, 128], [16, 8], [1, 16]])
    TTo(w1h, bc16(qb_e["00"]), hA_F, OP.mult)
    TTo(w2h, bc16(qb_e["01"]), hA_S, OP.mult)
    TTo(Hf_F, w1h, w2h, OP.add)
    TTo(w1h, bc16(qb_e["10"]), hA_F, OP.mult)
    TTo(w2h, bc16(qb_e["11"]), hA_S, OP.mult)
    TTo(Hf_S, w1h, w2h, OP.add)
    rdB = AP(t_rdet, 31, [[256, 128], [32, 8]])
    rB_F = AP(t_r, 16, [[512, 128], [32, 8], [1, 16]])
    rB_S = AP(t_r, 256 + 16, [[512, 128], [32, 8], [1, 16]])
    rt_F = AP(t_rBt, 0, [[256, 128], [16, 8], [1, 16]])
    rt_S = AP(t_rBt, 128, [[256, 128], [16, 8], [1, 16]])
    TTo(w1h, bc16(qb_e["11"]), rB_F, OP.mult)
    TTo(w2h, bc16(qb_e["10"]), rB_S, OP.mult)
    TTo(w1h, w1h, w2h, OP.subtract)
    TTo(rt_F, w1h, bc16(rdB), OP.mult)
    TTo(w1h, bc16(qb_e["00"]), rB_S, OP.mult)
    TTo(w2h, bc16(qb_e["01"]), rB_F, OP.mult)
    TTo(w1h, w1h, w2h, OP.subtract)
    TTo(rt_S, w1h, bc16(rdB), OP.mult)
    rA_F = AP(t_r, 0, [[512, 128], [32, 8], [1, 16]])
    rA_S = AP(t_r, 256, [[512, 128], [32, 8], [1, 16]])
    rf_FA = AP(t_rfl, 0, [[512, 128], [32, 8], [1, 16]])
    rf_SA = AP(t_rfl, 256, [[512, 128], [32, 8], [1, 16]])
    rf_FB = AP(t_rfl, 16, [[512, 128], [32, 8], [1, 16]])
    rf_SB = AP(t_rfl, 256 + 16, [[512, 128], [32, 8], [1, 16]])
    TS(rf_FA, rA_F, 0.5, None, OP.mult)
    TS(rf_SA, rA_S, 0.5, None, OP.mult)
    TTo(w1h, bc16(qa_e["00"]), rB_F, OP.mult)
    TTo(w2h, bc16(qa_e["10"]), rB_S, OP.mult)
    TTo(w1h, w1h, w2h, OP.add)
    TS(rf_FB, w1h, 0.5, None, OP.mult)
    TTo(w1h, bc16(qa_e["01"]), rB_F, OP.mult)
    TTo(w2h, bc16(qa_e["11"]), rB_S, OP.mult)
    TTo(w1h, w1h, w2h, OP.add)
    TS(rf_SB, w1h, 0.5, None, OP.mult)
    for k, (aa, bb2, cc, dd) in enumerate([("00", "00", "01", "10"),
                                           ("00", "01", "01", "11"),
                                           ("10", "00", "11", "10"),
                                           ("10", "01", "11", "11")]):
        TTo(t_w1[:, 0:8], qb_e[aa], qa_e[bb2], OP.mult)
        TTo(t_w2[:, 0:8], qb_e[cc], qa_e[dd], OP.mult)
        TTo(q32[:, 8 * k:8 * k + 8], t_w1[:, 0:8], t_w2[:, 0:8], OP.add)
    dump_jobs = [(d_tg, t_g, 512), (d_th, t_h, 512), (d_tr, t_r, 512),
                 (d_tHfA, t_HfA, 256), (d_trBt, t_rBt, 256)]
    wait("sync", sDV, sDV.n)
    for dstt, srct, wdt in dump_jobs:
        hw = wdt // 2
        for comp in range(2):
            dma("sync", qS,
                AP(dstt, comp * 128 * hw, [[hw, 128], [1, hw]]),
                srct[:, comp * hw:comp * hw + hw], sp=False)
    dvm = CP(t_bf[:, 0:512], t_rfl[:, 0:512], sem=sDV)
    wait("sync", sDV, dvm)
    for comp in range(2):
        t_trfl_dump = dma("sync", qS,
                          AP(d_trfl, comp * 128 * 256,
                             [[256, 128], [1, 256]]),
                          t_bf[:, comp * 256:comp * 256 + 256])
    t_q32w = dma("sync", qS, d_q32.ap()[:, :], q32[:])
    wait("sync", qS, qS.n)
    for ent in range(4):
        tq32 = dma("sync", qS, q32ch[:, 64 * ent:64 * ent + 64],
                   AP(d_q32, 8 * ent, [[256, 16], [32, 8], [1, 8]]),
                   sp=(ent == 3))
    t_dumps = qS.n

    for b4 in range(4):
        dma("sync", qS, AP(d_LWf, b4 * 8192, [[32768, 32], [1, 8192]]),
            c_lwz, sp=False)
    for b16 in range(16):
        dma("sync", qS, AP(d_RWf, b16 * 2176, [[34816, 32], [1, 2176]]),
            c_rws, sp=False)
    for r in range(8):
        dma("sync", qS, AP(d_L2f, r * 32768, [[1, 32768]]),
            AP(c_zb16.tensor, 0, [[1, 32768]]), sp=(r == 7))
    lwk = [("gA", d_tg, 0, 0), ("gA", d_tg, 1, 0),
           ("HfA", d_tHfA, 0, 0), ("HfA", d_tHfA, 1, 0),
           ("gB", d_tg, 0, 16), ("gB", d_tg, 1, 16),
           ("hB", d_th, 0, 16), ("hB", d_th, 1, 16)]
    for m in range(4):
        for k in range(8):
            nm, srcd, comp, off = lwk[k]
            if srcd is d_tHfA:
                soff = comp * 16384 + m * 16
                gst, hst = 64, 1024
            else:
                soff = comp * 32768 + m * 32 + (16 if off == 16 else 0)
                gst, hst = 128, 2048
            dst = AP(d_LWf, (8 * m + k) * 32768 + 32 * m + off,
                     [[2048, 16], [128, 16], [1, 16]])
            src2 = AP(srcd, soff, [[gst, 16], [hst, 16], [1, 16]])
            dma("sync", qS, dst, src2, sp=False)
    rwk = [("rA", d_tr, 0, 0), ("rA", d_tr, 1, 0),
           ("rBt", d_trBt, 0, 16), ("rBt", d_trBt, 1, 16),
           ("rB", d_tr, 0, 16), ("rB", d_tr, 1, 16)]
    for m in range(4):
        for k in range(6):
            nm, srcd, comp, off = rwk[k]
            if srcd is d_trBt:
                soff = comp * 16384 + m * 16
                gst, hst = 64, 1024
            else:
                soff = comp * 32768 + m * 32 + (16 if nm == "rB" else 0)
                gst, hst = 128, 2048
            dst = AP(d_RWf, (8 * m + k) * 34816 + 32 * m + off,
                     [[2176, 16], [136, 16], [1, 16]])
            src2 = AP(srcd, soff, [[gst, 16], [hst, 16], [1, 16]])
            dma("sync", qS, dst, src2, sp=False)
    for m in range(4):
        for s in range(2):
            dst = AP(d_L2f, (2 * m + s) * 32768 + 32 * m,
                     [[2048, 16], [128, 16], [1, 32]])
            src2 = AP(d_trfl, s * 32768 + m * 32,
                      [[128, 16], [2048, 16], [1, 32]])
            dma("sync", qS, dst, src2, sp=(m == 3 and s == 1))
    t_asm = qS.n

    gp.close()

    gw = stack0.enter_context(contextlib.ExitStack())
    LW = sb(gw, "LW", [32, 2048], F32)
    RW = sb(gw, "RW", [32, 2176], F32)
    L2 = sb(gw, "L2", [8, 2048], BF16)
    R2 = sb(gw, "R2", [8, 1024], BF16)
    Hf8 = sb(gw, "Hf8", [128, 64], BF16)
    Wt_t = [sb(gw, f"Wt{i2}", [128, 512], BF16) for i2 in range(2)]
    Bst = sb(gw, "Bst", [8, 1024], F32)
    ot_f = sb(gw, "ot_f", [128, 512], F32)
    ogt = [sb(gw, f"ogt{i2}", [128, D], BF16) for i2 in range(2)]
    urd = [sb(gw, f"urd{i2}", [128, D], BF16) for i2 in range(2)]
    qrd = sb(gw, "qrd", [128, D], BF16)
    grd = sb(gw, "grd", [128, D], BF16)
    zhF = sb(gw, "zhF", [16, 65 * 66], BF16)
    zhS = sb(gw, "zhS", [16, 65 * 66], BF16)

    zgF = sb(gw, "zgF", [16, 64], BF16)
    zgS = sb(gw, "zgS", [16, 64], BF16)
    zi_F = sb(gw, "zi_F", [16, 64], F32)
    zi_S = sb(gw, "zi_S", [16, 64], F32)
    ztmp = sb(gw, "ztmp", [16, 1040], BF16)
    cht = [sb(gw, f"cht{i2}", [16, 66], F32) for i2 in range(2)]
    xq_fm = sb(gw, "xq_fm", [128, 8 * T], BF16)
    wgc = sb(gw, "wgc", [128, 8 * 1024], BF16)
    wuc = sb(gw, "wuc", [128, 8 * 1024], BF16)

    for b in range(8):
        dma("sync", qS, xq_fm[:, T * b:T * b + T],
            AP(d_xq, 128 * b, [[1024, 2048], [1, 128]]), transpose=True,
            sp=True)
    t_xqfm = qS.n

    BcrF = [sb(gw, f"BcrF{i2}", [16, 528], F32) for i2 in range(2)]
    BcrS = [sb(gw, f"BcrS{i2}", [16, 528], F32) for i2 in range(2)]
    bcr_t = [None, None]

    def bcr_load(kb, buf):
        wait("scalar", qS, braw_w[2 * kb + 1])
        dma("scalar", qA, AP(BcrF[buf], 0, [[528, 16], [66, 8], [1, 64]]),
            AP(d_Braw, 8 * kb * 1024, [[64, 16], [1024, 8], [1, 64]]),
            sp=False)
        t2 = dma("scalar", qA,
                 AP(BcrS[buf], 0, [[528, 16], [66, 8], [1, 64]]),
                 AP(d_Braw, 65536 + 8 * kb * 1024,
                    [[64, 16], [1024, 8], [1, 64]]))
        return t2

    E("vector", lambda e: e.memset(zhF[:, 0:66], 0.0))
    E("vector", lambda e: e.memset(zhS[:, 0:66], 0.0))
    E("vector", lambda e: e.memset(zhF[:, 64:65], 1.0))
    E("vector", lambda e: e.memset(zhS[:, 65:66], 1.0))
    wait("vector", qS, tq32)

    chain_state = {"c": 0, "loads": 8}

    def chain_step(sem=None):
        c = chain_state["c"]
        kb, cin = c // 8, c % 8
        buf = kb % 2
        if cin == 0:
            wait("vector", qA, bcr_t[buf])
        zFc = zhF[:, 66 * c:66 * c + 66]
        zSc = zhS[:, 66 * c:66 * c + 66]
        bF = BcrF[buf][:, 66 * cin:66 * cin + 66]
        bS = BcrS[buf][:, 66 * cin:66 * cin + 66]

        def qc(k):
            return q32ch[:, 64 * k + c:64 * k + c + 1]

        TS(cht[0][:], zFc, qc(0), None, OP.mult)
        TS(cht[1][:], zSc, qc(1), None, OP.mult)
        TTo(cht[0][:], cht[0][:], cht[1][:], OP.add)
        TTo(zhF[:, 66 * (c + 1):66 * (c + 1) + 66], cht[0][:], bF, OP.add)
        TS(cht[0][:], zFc, qc(2), None, OP.mult)
        TS(cht[1][:], zSc, qc(3), None, OP.mult)
        TTo(cht[0][:], cht[0][:], cht[1][:], OP.add)
        sem_eff = sem if sem is not None else (sDV if cin == 7 else None)
        lastm = TTo(zhS[:, 66 * (c + 1):66 * (c + 1) + 66], cht[0][:], bS,
                    OP.add, sem=sem_eff)
        chain_state["c"] = c + 1
        if cin == 7 and chain_state["loads"] < 8:
            kb2 = chain_state["loads"]
            wait("scalar", sDV, lastm)
            bcr_t[kb2 % 2] = bcr_load(kb2, kb2 % 2)
            chain_state["loads"] += 1
        return lastm

    wait("vector", qS, t_q32w)
    for buf2 in range(2):
        E("vector", lambda e, b2=buf2: e.memset(BcrF[b2][:], 0.0))
        E("vector", lambda e, b2=buf2: e.memset(BcrS[b2][:], 0.0))
    bm = op("vector", lambda e: e.engine_nop(), sDV)
    wait("scalar", sDV, bm)

    pre_chain_ac = sAC.n
    wait("scalar", qS, t_asm)
    wait("scalar", qS, u_w[-1])
    braw_w = []
    eb_pe = []
    chain_marks = []
    for G in range(16):
        if G > 0:
            wait("scalar", sPE, eb_pe[-1])
        dma("scalar", qA, LW[:, :], AP(d_LWf, G * 2048,
                                       [[32768, 32], [1, 2048]]), sp=False)
        dma("scalar", qA, RW[:, :], AP(d_RWf, G * 2176,
                                       [[34816, 32], [1, 2176]]), sp=False)
        tu = dma("scalar", qA, urd[G % 2][:],
                 AP(d_u, G * 128 * 1024, [[1024, 128], [1, 1024]]))
        wait("tensor", qA, tu)
        for sb8 in range(2):
            pe_m = None
            for gi in range(8):
                hh = sb8 * 8 + gi
                pe_m = mm(PS[5][:, 8 * gi:8 * gi + 8],
                          LW[:, 128 * hh:128 * hh + 128],
                          RW[:, 136 * hh + 128:136 * hh + 136],
                          True, True, sem=sPE if gi == 7 else None, skip=True)
            wait("vector", sPE, pe_m)
            dvm = CP(Hf8[:], PS[5][:, 0:64], sem=sDV)
            wait("tensor", sDV, dvm)
            pb = PS[6] if sb8 % 2 == 0 else PS[7]
            pe_m = None
            for gi in range(8):
                hh = sb8 * 8 + gi
                pe_m = mm(pb[0:8, 64 * gi:64 * gi + 64],
                          Hf8[:, 8 * gi:8 * gi + 8],
                          urd[G % 2][:, 64 * hh:64 * hh + 64],
                          True, True, sem=sPE if gi == 7 else None, skip=True)
            eb_pe.append(pe_m)
            wait("vector", sPE, pe_m)
            if len(braw_w) >= 1 and sb8 == 0:
                wait("vector", qS, braw_w[-1])
            dvm = CP(Bst[:, 512 * sb8:512 * sb8 + 512], pb[0:8, 0:512],
                     sem=sDV)
            if sb8 == 1:
                eb_last_dv = dvm
                wait("sync", sDV, dvm)
                for s in range(2):
                    tb = dma("sync", qS,
                             AP(d_Braw, s * 65536 + 4 * G * 1024,
                                [[1024, 4], [1, 1024]]),
                             AP(Bst, 4096 * s, [[1024, 4], [1, 1024]]))
                braw_w.append(tb)
        if G % 2 == 1:
            m_ = (G - 1) // 2
            if m_ >= 2:
                wait("scalar", sDV, chain_marks[m_ - 2])
            bcr_t[m_ % 2] = bcr_load(m_, m_ % 2)
            if m_ >= 1:
                for _ in range(8):
                    sem2 = sDV if chain_state["c"] == 63 else None
                    cm = chain_step(sem=sem2)
                chain_marks.append(cm)
    for _ in range(8):
        sem2 = sDV if chain_state["c"] == 63 else None
        cm = chain_step(sem=sem2)
    chain_marks.append(cm)
    t_earlyB_pe = eb_pe[-1]

    wait("gpsimd", sDV, cm)
    dma("gpsimd", qG, AP(d_zin, 0, [[64, 16], [1, 64]]),
        AP(zhF, 66 * 64, [[4290, 16], [1, 64]]), sp=False)
    tz = dma("gpsimd", qG, AP(d_zin, 16 * 64, [[64, 16], [1, 64]]),
             AP(zhS, 66 * 64, [[4290, 16], [1, 64]]))
    wait("gpsimd", qG, tz)
    ccz = sCC.inc(1)
    E("gpsimd", lambda e: e.collective_compute(
        "AllGather", OP.bypass,
        replica_groups=[[0, 1], [2, 3], [4, 5], [6, 7]],
        ins=[d_zin.ap().opt()], outs=[d_zout.ap().opt()]).then_inc(sCC.h, 1))
    wait("gpsimd", sCC, ccz)
    dma("gpsimd", qG, zgF[:], d_zout.ap()[0:16, :], sp=False)
    tz2 = dma("gpsimd", qG, zgS[:], d_zout.ap()[16:32, :])

    IC = [(0, 1024), (1024, 1024), (2048, 768)]
    wait("scalar", qG, t_repack)
    wait("scalar", sPE, t_passB)
    wait("tensor", sAC, pre_chain_ac)
    wait("tensor", sDV, eb_last_dv)
    hq_w = []
    hq_i = 0
    gu_last_pe = None
    hdone_l, sg_l = [], []
    for ic, (i0_, cw) in enumerate(IC):
        if ic > 0:
            wait("sync", sPE, gu_last_pe)
        last = None
        for b in range(8):
            dma("sync", qS, wgc[:, 1024 * b:1024 * b + cw],
                AP(d_wgf, i0_ * 1024 + 128 * b, [[1024, cw], [1, 128]]),
                transpose=True, sp=True)
            last = dma("sync", qS, wuc[:, 1024 * b:1024 * b + cw],
                       AP(d_wuf, i0_ * 1024 + 128 * b,
                          [[1024, cw], [1, 128]]),
                       transpose=True, sp=True)
        if ic == 0:
            wait("tensor", qS, t_xqfm)
        wait("tensor", qS, last)
        for i in range(NT):
            pp4 = 4 * (hq_i % 2)
            pga, pgb = PS[pp4], PS[pp4 + 1]
            pua, pub = PS[pp4 + 2], PS[pp4 + 3]
            if len(hdone_l) >= 2:
                wait("tensor", sDV, hdone_l[-2])
                wait("tensor", sAC, sg_l[-2])
            for b in range(8):
                lhs = xq_fm[:, T * b + 128 * i: T * b + 128 * i + 128]
                mm(pga[:, 0:512], lhs, wgc[:, 1024 * b:1024 * b + 512],
                   b == 0, b == 7)
                gate_pe = mm(pgb[:, 0:cw - 512], lhs,
                             wgc[:, 1024 * b + 512:1024 * b + cw],
                             b == 0, b == 7, sem=sPE if b == 7 else None)
            for b in range(8):
                lhs = xq_fm[:, T * b + 128 * i: T * b + 128 * i + 128]
                mm(pua[:, 0:512], lhs, wuc[:, 1024 * b:1024 * b + 512],
                   b == 0, b == 7)
                up_pe = mm(pub[:, 0:cw - 512], lhs,
                           wuc[:, 1024 * b + 512:1024 * b + cw],
                           b == 0, b == 7, sem=sPE if b == 7 else None)
            gu_last_pe = up_pe
            scr = xn_f if hq_i % 2 == 0 else ds_rd
            wait("scalar", sPE, gate_pe)
            E("scalar", lambda e, ss=scr, pa=pga: e.activation(
                ss[:, 0:512], pa[:, 0:512], AF.Silu))
            sg_ac = op("scalar", lambda e, o2=scr[:, 512:cw],
                       p2=pgb[:, 0:cw - 512]: e.activation(
                           o2, p2, AF.Silu), sAC)
            sg_l.append(sg_ac)
            wait("vector", sAC, sg_ac)
            wait("vector", sPE, up_pe)
            TTo(res_f[:, 0:512], scr[:, 0:512], pua[:, 0:512],
                OP.mult)
            hdone = TTo(res_f[:, 512:cw], scr[:, 512:cw],
                        pub[:, 0:cw - 512], OP.mult, sem=sDV)
            hdone_l.append(hdone)
            oq = (hq_i % 2) * 1024
            if hq_i >= 2:
                wait("vector", qS, hq_w[hq_i - 2])
            qm = quant_tile(res_f[:, 0:cw], cw, oq)
            wait("sync", sDV, qm)
            hq_w.append(dma("sync", qS,
                            AP(d_hq, 128 * i * I + i0_,
                               [[2816, 128], [1, cw]]),
                            qout[:, oq:oq + cw]))
            hq_i += 1

    wait("vector", qG, tz2)
    TS(zi_F[:], zgF[:], zmask_s[:, 0:1], None, OP.mult)
    TS(zi_S[:], zgS[:], zmask_s[:, 0:1], None, OP.mult)
    for ch5 in range(5):
        c0 = ch5 * 13
        ncc = 13
        zh3F = AP(zhF, 66 * c0, [[4290, 16], [66, ncc], [1, 64]])
        zh3S = AP(zhS, 66 * c0, [[4290, 16], [66, ncc], [1, 64]])
        zt3 = AP(ztmp, 0, [[1040, 16], [64, ncc], [1, 64]])
        pbF = AP(zhF, 66 * c0 + 64, [[4290, 16], [66, ncc], [0, 64]])
        pbF2 = AP(zhF, 66 * c0 + 65, [[4290, 16], [66, ncc], [0, 64]])
        pbS = AP(zhS, 66 * c0 + 64, [[4290, 16], [66, ncc], [0, 64]])
        pbS2 = AP(zhS, 66 * c0 + 65, [[4290, 16], [66, ncc], [0, 64]])
        zbF = AP(zi_F, 0, [[64, 16], [0, ncc], [1, 64]])
        zbS = AP(zi_S, 0, [[64, 16], [0, ncc], [1, 64]])
        TTo(zt3, pbF, zbF, OP.mult)
        TTo(zh3F, zh3F, zt3, OP.add)
        TTo(zt3, pbF2, zbS, OP.mult)
        TTo(zh3F, zh3F, zt3, OP.add)
        TTo(zt3, pbS, zbF, OP.mult)
        TTo(zh3S, zh3S, zt3, OP.add)
        TTo(zt3, pbS2, zbS, OP.mult)
        zc_dv = TTo(zh3S, zh3S, zt3, OP.add, sem=sDV)
    wait("gpsimd", sDV, zc_dv)
    dma("gpsimd", qG, AP(d_zc, 0, [[4290, 16], [1, 4290]]), zhF[:],
        sp=False)
    tzc = dma("gpsimd", qG,
              AP(d_zc, 16 * 65 * 66, [[4290, 16], [1, 4290]]), zhS[:])
    wait("gpsimd", qG, tzc)
    for m in range(4):
        for s in range(2):
            dst = AP(d_R2f, (2 * m + s) * 16384,
                     [[0, 1], [1024, 16], [64, 16], [1, 64]])
            src3 = AP(d_zc, s * 68640 + m * 66,
                      [[0, 1], [264, 16], [4290, 16], [1, 64]])
            dma("gpsimd", qG, dst, src3, sp=(m == 3 and s == 1))
    wait("gpsimd", qG, qG.n)
    t_r2f = qG.n
    t_hq = qS.n

    lw_prev_pe = t_earlyB_pe
    og_w = []
    wait("scalar", qS, g_w[-1])
    og_dv = None
    for G in range(16):
        wait("scalar", sPE, lw_prev_pe)
        if og_dv is not None:
            wait("scalar", sDV, og_dv)
        dma("scalar", qA, LW[:, :],
            AP(d_LWf, G * 2048, [[32768, 32], [1, 2048]]), sp=False)
        dma("scalar", qA, RW[:, :],
            AP(d_RWf, G * 2176, [[34816, 32], [1, 2176]]), sp=False)
        dma("scalar", qA, L2[:, :],
            AP(d_L2f, G * 2048, [[32768, 8], [1, 2048]]), sp=False)
        if G == 0:
            wait("scalar", qG, t_r2f)
        dma("scalar", qA, R2[:, :],
            AP(d_R2f, G * 1024, [[16384, 8], [1, 1024]]), sp=False)
        dma("scalar", qA, urd[G % 2][:],
            AP(d_u, G * 128 * 1024, [[1024, 128], [1, 1024]]), sp=False)
        dma("scalar", qA, qrd[:],
            AP(d_q, G * 128 * 1024, [[1024, 128], [1, 1024]]), sp=False)
        tf4 = dma("scalar", qA, grd[:],
                  AP(d_g, G * 128 * 1024, [[1024, 128], [1, 1024]]))
        wait("tensor", qA, tf4)
        if len(og_w) >= 2:
            wait("vector", qS, og_w[-2])
        for hhf in range(2):
            pyb = PS[6] if hhf == 0 else PS[7]
            for quad in range(2):
                pe_m = None
                for gq in range(4):
                    hh = hhf * 8 + quad * 4 + gq
                    pe_m = mm(PS[5][:, 128 * gq:128 * gq + 128],
                              LW[:, 128 * hh:128 * hh + 128],
                              RW[:, 136 * hh:136 * hh + 128],
                              True, True,
                              sem=sPE if gq == 3 else None, skip=True)
                wait("vector", sPE, pe_m)
                wtb = Wt_t[quad]
                dvm = TTo(wtb[:], PS[5][:], mask_s[:], OP.mult, sem=sDV)
                wait("tensor", sDV, dvm)
                for gq in range(4):
                    hh = hhf * 8 + quad * 4 + gq
                    gi = quad * 4 + gq
                    mm(pyb[:, 64 * gi:64 * gi + 64],
                       wtb[:, 128 * gq:128 * gq + 128],
                       urd[G % 2][:, 64 * hh:64 * hh + 64],
                       True, False, skip=True)
                    lw_prev_pe = mm(
                        pyb[:, 64 * gi:64 * gi + 64],
                        L2[:, 128 * hh:128 * hh + 128],
                        R2[:, 64 * hh:64 * hh + 64],
                        False, True,
                        sem=sPE if (quad == 1 and gq == 3) else None,
                        skip=True)
            wait("vector", sPE, lw_prev_pe)
            qsl = qrd[:, 512 * hhf:512 * hhf + 512]
            gsl = grd[:, 512 * hhf:512 * hhf + 512]
            TTo(ot_f[:], qsl, pyb[:], OP.mult)
            og_dv = TTo(ogt[G % 2][:, 512 * hhf:512 * hhf + 512],
                        ot_f[:], gsl, OP.mult, sem=sDV)
            wait("tensor", sDV, og_dv)
        wait("sync", sDV, og_dv)
        og_w.append(dma("sync", qS, d_og.ap()[128 * G:128 * G + 128, :],
                        ogt[G % 2][:]))
    t_ogw = qS.n
    t_lateW_pe = lw_prev_pe

    gw.close()

    ga = stack0.enter_context(contextlib.ExitStack())
    attn_tm = sb(ga, "attn_tm", [128, NT * D], BF16)
    gp8 = stack0.enter_context(contextlib.ExitStack())
    og_fm = sb(gp8, "og_fm", [128, 8 * T], BF16)
    woT_s = sb(gp8, "woT_s", [128, 8 * D], BF16)

    wait("sync", qS, t_ogw)
    for b in range(8):
        dma("sync", qS, og_fm[:, T * b:T * b + T],
            AP(d_og, 128 * b, [[1024, 2048], [1, 128]]), transpose=True,
            sp=True)
    t_ogfm = qS.n
    wo_cast = []
    wo_ld = []
    for b in range(8):
        if b >= 2:
            wait("gpsimd", sDV, wo_cast[b - 2])
        wo_ld.append(dma("gpsimd", qG, xt_f[b % 2][:],
                         woT_i[128 * b:128 * b + 128, :]))
        wait("vector", qG, max(wo_ld[min(1, b)], wo_ld[b]))
        wo_cast.append(CP(woT_s[:, D * b:D * b + D], xt_f[b % 2][:], sem=sDV))
    wait("tensor", qS, t_ogfm)
    wait("tensor", sDV, wo_cast[-1])
    for i in range(NT):
        for b in range(8):
            lhs = og_fm[:, T * b + 128 * i: T * b + 128 * i + 128]
            mm(PS[0][:], lhs, woT_s[:, D * b:D * b + 512], b == 0, b == 7)
            wo_pe = mm(PS[1][:], lhs, woT_s[:, D * b + 512:D * b + 1024],
                       b == 0, b == 7, sem=sPE if b == 7 else None)
        wait("scalar", sPE, wo_pe)
        E("scalar", lambda e, ii=i: e.activation(
            attn_tm[:, D * ii:D * ii + 512], PS[0][:], AF.Copy))
        aac = op("scalar", lambda e, ii=i: e.activation(
            attn_tm[:, D * ii + 512:D * ii + 1024], PS[1][:], AF.Copy), sAC)
        wait("tensor", sAC, aac)
    t_wo_pe = sPE.n

    gp8.close()
    gp9 = stack0.enter_context(contextlib.ExitStack())
    wdT_s = sb(gp9, "wdT_s", [128, 11 * 1024], BF16)
    hq_fm = sb(gp9, "hq_fm", [128, 11 * 1024], BF16)

    out_w, res_dv, dn_pe_l, ds_w = [], [], [], []
    wait("vector", qS, t_hq)
    wait("scalar", qG, t_repack)
    wait("tensor", sDV, sDV.n)
    wait("tensor", sAC, sAC.n)
    for ihalf in range(2):
        wait("sync", sPE, t_wo_pe if ihalf == 0 else dn_pe_l[-1])
        if ihalf == 1:
            wait("sync", qS, ds_w[-1])
        wdl = []
        for ib in range(11):
            ib2 = ihalf * 11 + ib
            wdl.append(dma("sync", qS,
                           wdT_s[:, 1024 * ib:1024 * ib + 1024],
                           AP(d_wdf, 128 * ib2, [[2816, 1024], [1, 128]]),
                           transpose=True, sp=True))
        for thalf in range(2):
            if ihalf == 0 and thalf == 0:
                wait("sync", qS, t_hq)
                wait("sync", sPE, t_wo_pe)
            if not (ihalf == 0 and thalf == 0):
                wait("sync", sPE, dn_pe_l[-1])
            hql = []
            for ib in range(11):
                ib2 = ihalf * 11 + ib
                hql.append(dma("sync", qS,
                               hq_fm[:, 1024 * ib:1024 * ib + 1024],
                               AP(d_hq, thalf * 1024 * I + 128 * ib2,
                                  [[2816, 1024], [1, 128]]), transpose=True,
                               sp=True))
            wait("tensor", qS, hql[-1])
            if thalf == 0:
                wait("tensor", qS, wdl[-1])
            for it in range(8):
                i = thalf * 8 + it
                nbuf = len(out_w) + len(ds_w)
                if nbuf >= 2:
                    if len(res_dv) >= 2:
                        wait("sync", sDV, res_dv[-2])
                    if ihalf == 0 and len(ds_w) >= 2:
                        wait("vector", qS, ds_w[-2])
                    if ihalf == 1 and len(out_w) >= 2:
                        wait("vector", qS, out_w[-2])
                if len(res_dv) >= 2:
                    wait("tensor", sDV, res_dv[-2])
                pba, pbb = PS[2 * (it % 2)], PS[2 * (it % 2) + 1]
                for ib in range(11):
                    lhs = hq_fm[:, 1024 * ib + 128 * it:
                                1024 * ib + 128 * it + 128]
                    mm(pba[:], lhs, wdT_s[:, 1024 * ib:1024 * ib + 512],
                       ib == 0, ib == 10)
                    dn_pe = mm(pbb[:], lhs,
                               wdT_s[:, 1024 * ib + 512:1024 * ib + 1024],
                               ib == 0, ib == 10,
                               sem=sPE if ib == 10 else None)
                dn_pe_l.append(dn_pe)
                if ihalf == 0:
                    wait("vector", sPE, dn_pe)
                    oh9 = (it % 2) * 1024
                    CP(qout[:, oh9:oh9 + 512], pba[:])
                    dvm = CP(qout[:, oh9 + 512:oh9 + 1024], pbb[:], sem=sDV)
                    res_dv.append(dvm)
                    wait("sync", sDV, dvm)
                    ds_w.append(dma("sync", qS,
                                    AP(d_ds, 128 * i * 1024,
                                       [[1024, 128], [1, 1024]]),
                                    qout[:, oh9:oh9 + 1024]))
                else:
                    xr = dma("sync", qS, xt_f[it % 2][:],
                             x_in[128 * i:128 * i + 128, :])
                    oh9 = (it % 2) * 1024
                    dsr = dma("sync", qS, qout[:, oh9:oh9 + 1024],
                              AP(d_ds, 128 * i * 1024,
                                 [[1024, 128], [1, 1024]]))
                    wait("vector", qS, dsr)
                    wait("vector", sPE, dn_pe)
                    E("vector", lambda e, x=xt_f[it % 2], ii=i:
                      e.tensor_tensor(res_f[:], x[:],
                                      attn_tm[:, D * ii:D * ii + D], OP.add))
                    TTo(res_f[:], res_f[:], qout[:, oh9:oh9 + 1024], OP.add)
                    TTo(out_f[it % 2][:, 0:512], res_f[:, 0:512], pba[:],
                        OP.add)
                    dvm = TTo(out_f[it % 2][:, 512:1024], res_f[:, 512:1024],
                              pbb[:], OP.add, sem=sDV)
                    res_dv.append(dvm)
                    wait("sync", sDV, dvm)
                    out_w.append(dma("sync", qS,
                                     out_t[128 * i:128 * i + 128, :],
                                     out_f[it % 2][:]))
    wait("sync", qS, qS.n)
    wait("gpsimd", qS, qS.n)
    gp9.close()
    ga.close()

    with nc.Block() as blk:
        @blk.sync
        def _(e):
            for f in plan["sync"]:
                f(e)

        @blk.tensor
        def _(e):
            for f in plan["tensor"]:
                f(e)

        @blk.vector
        def _(e):
            for f in plan["vector"]:
                f(e)

        @blk.scalar
        def _(e):
            for f in plan["scalar"]:
                f(e)

        @blk.gpsimd
        def _(e):
            for f in plan["gpsimd"]:
                f(e)

    stack0.close()
    return nc


def _consts():
    c = {}
    m32 = 0.5 * np.triu(np.ones((32, 32), np.float32), 1)
    m128 = np.zeros((128, 128), np.float32)
    for m in range(4):
        m128[32 * m:32 * m + 32, 32 * m:32 * m + 32] = m32
    c["c_mask"] = np.tile(m128, (1, 4)).astype(bf16)
    c["c_eye"] = np.eye(128, dtype=np.float32)
    c["c_ones"] = np.ones((1, 128), np.float32).astype(bf16)
    qt = np.zeros((128, 32), np.float32)
    qt[:, 0:16] = 1.0
    qb = np.zeros((128, 32), np.float32)
    qb[:, 16:32] = 1.0
    c["c_qtop0"] = qt
    c["c_qbot0"] = qb
    c["c_lwz"] = np.zeros((32, 8192), np.float32)
    c["c_zb16"] = np.zeros((1, 32768), np.float32).astype(bf16)
    rws = np.zeros((32, 2176), np.float32)
    for gb in range(16):
        for m in range(4):
            for s in range(2):
                col = gb * 136 + 128 + 4 * s + m
                rws[8 * m + 2 + s, col] = 1.0
                rws[8 * m + 6 + s, col] = 1.0
    c["c_rws"] = rws
    return c


def kernel(L0_attn_x, L0_attn_Wq, L0_attn_Wk, L0_attn_Wv, L0_attn_Wo,
           L0_attn_Wb, L0_attn_Wfd, L0_attn_bfd, L0_attn_Wsd, L0_attn_bsd,
           L0_attn_Wg, L0_ffn_w_gate, L0_ffn_w_up, L0_ffn_w_down):
    if "nc" not in _CACHE:
        _CACHE["nc"] = build_program()
        _CACHE["consts"] = _consts()
    nc = _CACHE["nc"]
    consts = _CACHE["consts"]

    def f32(a):
        return np.ascontiguousarray(np.asarray(a, np.float32))

    x = f32(L0_attn_x)
    wqT = f32(np.asarray(L0_attn_Wq).T)
    wkT = f32(np.asarray(L0_attn_Wk).T)
    wvT = f32(np.asarray(L0_attn_Wv).T)
    wgT = f32(np.asarray(L0_attn_Wg).T)
    woT = f32(np.asarray(L0_attn_Wo).T)
    wbfs = np.concatenate([f32(L0_attn_Wb), f32(L0_attn_Wfd),
                           f32(L0_attn_Wsd)], 0)
    wbfsT = np.ascontiguousarray(wbfs.T)
    bias = np.concatenate([np.zeros(16, np.float32),
                           f32(L0_attn_bfd).ravel(),
                           f32(L0_attn_bsd).ravel()]).reshape(1, 48)
    wg_f = f32(L0_ffn_w_gate)
    wu_f = f32(L0_ffn_w_up)
    wd_f = f32(L0_ffn_w_down)

    in_maps = []
    for c in range(8):
        b, half = c // 2, c % 2
        gsh = np.zeros((384, D), np.float32)
        gsh[0:352] = wg_f[352 * c:352 * c + 352]
        ush = np.zeros((384, D), np.float32)
        ush[0:352] = wu_f[352 * c:352 * c + 352]
        dsh = np.ascontiguousarray(wd_f[128 * c:128 * c + 128])
        zm = np.full((16, 1), float(half), np.float32)
        m = {"x_s": np.ascontiguousarray(x[b, 2048 * half:2048 * half + 2048]),
             "wqT": wqT, "wkT": wkT, "wvT": wvT, "wgT": wgT, "woT": woT,
             "wbfsT": wbfsT, "bias_bfs": bias,
             "wg_sh": gsh, "wu_sh": ush, "wd_sh": dsh,
             "c_zmask": zm}
        m.update(consts)
        in_maps.append(m)

    res = run_bass_kernel_spmd(nc, in_maps, core_ids=list(range(8)))
    out = np.zeros((4, 4096, 1024), np.float32)
    for c in range(8):
        b, half = c // 2, c % 2
        out[b, 2048 * half:2048 * half + 2048] = res.results[c]["out"]
    return out


# version 28
# speedup vs baseline: 1.0230x; 1.0230x over previous
import numpy as np
import ml_dtypes
import contextlib
import concourse.bass as bass
import concourse.mybir as mybir
from concourse.bass_utils import run_bass_kernel_spmd

bf16 = ml_dtypes.bfloat16
F32 = mybir.dt.float32
BF16 = mybir.dt.bfloat16
OP = mybir.AluOpType
AF = mybir.ActivationFunctionType
AX = mybir.AxisListType
AP = bass.AP

T, D, H, HD, I = 2048, 1024, 16, 64, 2816
NT = 16
EPS, CPc = 0.05, 1.05

_CACHE = {}


class Sem:
    def __init__(self, h):
        self.h = h
        self.n = 0
        self.max_wait = 0
        self.drained = 0

    def inc(self, k=1):
        self.n += k
        return self.n


def build_program():
    nc = bass.Bass("TRN2", target_bir_lowering=False)

    def ein(name, shape, dt=F32):
        return nc.dram_tensor(name, shape, dt, kind="ExternalInput").ap()

    x_in = ein("x_s", [T, D])
    wqT_i = ein("wqT", [D, D]); wkT_i = ein("wkT", [D, D])
    wvT_i = ein("wvT", [D, D]); wgT_i = ein("wgT", [D, D])
    woT_i = ein("woT", [D, D])
    wbfsT_i = ein("wbfsT", [D, 48])
    bias_in = ein("bias_bfs", [1, 48])
    wg_sh = ein("wg_sh", [384, D])
    wu_sh = ein("wu_sh", [384, D])
    wd_sh = ein("wd_sh", [128, I])
    c_mask = ein("c_mask", [128, 512], BF16)
    c_eye = ein("c_eye", [128, 128])
    c_ones = ein("c_ones", [1, 128], BF16)
    c_qtop0 = ein("c_qtop0", [128, 32])
    c_qbot0 = ein("c_qbot0", [128, 32])
    c_lwz = ein("c_lwz", [32, 8192])
    c_rws = ein("c_rws", [32, 2176])
    c_zb16 = ein("c_zb16", [1, 32768], BF16)
    c_zmask = ein("c_zmask", [16, 1])
    out_t = nc.dram_tensor("out", [T, D], F32, kind="ExternalOutput").ap()

    def scr(name, shape, dt):
        return nc.dram_tensor(name, shape, dt)

    d_xnb = scr("d_xnb", [T, D], BF16)
    d_xq = scr("d_xq", [T, D], BF16)
    d_u = scr("d_u", [T, D], BF16)
    d_q = scr("d_q", [T, D], BF16)
    d_g = scr("d_g", [T, D], BF16)
    d_fs = scr("d_fs", [48, T], F32)
    d_og = scr("d_og", [T, D], BF16)
    d_hq = scr("d_hq", [T, I], BF16)
    d_ds = scr("d_ds", [T, D], BF16)
    d_tg = scr("d_tg", [2, 32768], F32)
    d_th = scr("d_th", [2, 32768], F32)
    d_tr = scr("d_tr", [2, 32768], F32)
    d_tHfA = scr("d_tHfA", [2, 16384], F32)
    d_trBt = scr("d_trBt", [2, 16384], F32)
    d_trfl = scr("d_trfl", [2, 32768], BF16)
    d_q32 = scr("d_q32", [128, 32], F32)
    d_Braw = scr("d_Braw", [2, 65536], F32)
    d_zc = scr("d_zc", [2, 16 * 65 * 66], BF16)
    d_wqin = scr("d_wqin", [1056, 1024], BF16)
    d_wqout = scr("d_wqout", [8448, 1024], BF16)
    d_zin = scr("d_zin", [32, 64], BF16)
    d_zout = scr("d_zout", [64, 64], BF16)
    d_wgf = scr("d_wgf", [I, D], BF16)
    d_wuf = scr("d_wuf", [I, D], BF16)
    d_wdf = scr("d_wdf", [D, I], BF16)
    d_LWf = scr("d_LWf", [32, 32768], F32)
    d_RWf = scr("d_RWf", [32, 34816], F32)
    d_L2f = scr("d_L2f", [8, 32768], BF16)
    d_R2f = scr("d_R2f", [8, 16384], BF16)

    plan = {k: [] for k in ["sync", "tensor", "vector", "scalar", "gpsimd"]}
    stack0 = contextlib.ExitStack()

    sems = {s: Sem(stack0.enter_context(nc.semaphore(s)))
            for s in ["qS", "qG", "qA", "qV", "sPE", "sDV", "sAC", "sCC"]}
    qS, qG, qA, qV = sems["qS"], sems["qG"], sems["qA"], sems["qV"]
    sPE, sDV, sAC, sCC = sems["sPE"], sems["sDV"], sems["sAC"], sems["sCC"]

    def E(eng, fn):
        if eng == "vector" or eng == "scalar":
            s = sDV if eng == "vector" else sAC
            v = s.n
            plan[eng].append(lambda e, ss=s.h, vv=v: e.wait_ge(ss, vv))
            tgt = s.inc(1)
            plan[eng].append(lambda e, f=fn, ss=s.h: f(e).then_inc(ss, 1))
            return tgt
        plan[eng].append(fn)
        return None

    def dma(eng, q, dst, src, transpose=False, sp=True):
        tgt = q.inc(16)
        if transpose:
            plan[eng].append(
                lambda e, d=dst, s=src: e.dma_start(d, s, transpose=True)
                .then_inc(q.h, 16))
        else:
            plan[eng].append(lambda e, d=dst, s=src: e.dma_start(d, s)
                             .then_inc(q.h, 16))
        if sp:
            plan[eng].append(lambda e, s=q.h, vv=tgt: e.wait_ge(s, vv))
            q.drained = tgt
        return tgt

    def wait(eng, sem, val):
        sem.max_wait = max(sem.max_wait, val)
        plan[eng].append(lambda e, s=sem.h, v=val: e.wait_ge(s, v))

    def op(eng, fn, sem=None):
        if eng == "vector" or eng == "scalar":
            return E(eng, fn)
        if sem is None:
            E(eng, fn)
            return None
        tgt = sem.inc(1)
        plan[eng].append(lambda e, f=fn, s=sem.h: f(e).then_inc(s, 1))
        return tgt

    def mm(psum_ap, lhsT_ap, rhs_ap, start, stop, sem=None, skip=False):
        if sem is None:
            E("tensor", lambda e, o=psum_ap, l=lhsT_ap, r=rhs_ap, st=start,
              sp=stop, sk=skip: e.matmul(o, l, r, start=st, stop=sp,
                                         skip_group_check=sk))
            return None
        tgt = sem.inc(1)
        E("tensor", lambda e, o=psum_ap, l=lhsT_ap, r=rhs_ap, st=start,
          sp=stop, s=sem.h, sk=skip: e.matmul(
              o, l, r, start=st, stop=sp, skip_group_check=sk).then_inc(s, 1))
        return tgt

    def TTo(o, a, b, alu, sem=None):
        return op("vector", lambda e, oo=o, aa=a, bb=b, al=alu:
                  e.tensor_tensor(oo, aa, bb, al), sem)

    def STT(o, a, s, b, op0, op1, sem=None):
        return op("vector", lambda e, oo=o, aa=a, ss=s, bb=b, o0=op0, o1=op1:
                  e.scalar_tensor_tensor(oo, aa, ss, bb, o0, o1), sem)

    def TS(o, a, s1, s2, op0, op1=None, sem=None):
        if op1 is None:
            return op("vector", lambda e, oo=o, aa=a, a1=s1, o0=op0:
                      e.tensor_scalar(oo, aa, a1, None, o0), sem)
        return op("vector", lambda e, oo=o, aa=a, a1=s1, a2=s2, o0=op0,
                  o1=op1: e.tensor_scalar(oo, aa, a1, a2, o0, o1), sem)

    def CP(o, a, sem=None):
        return op("vector", lambda e, oo=o, aa=a: e.tensor_copy(oo, aa), sem)

    def sb(st, name, shape, dt):
        return st.enter_context(nc.sbuf_tensor(name, shape, dt))

    g0 = stack0
    xt_f = [sb(g0, f"xt{i}", [128, D], F32) for i in range(2)]
    xn_f = sb(g0, "xn_f", [128, D], F32)
    kvb = [sb(g0, f"kvb{i}", [128, D], BF16) for i in range(4)]
    res_f = sb(g0, "res_f", [128, D], F32)
    out_f = [sb(g0, f"outf{i}", [128, D], F32) for i in range(2)]
    ds_rd = sb(g0, "ds_rd", [128, D], F32)
    wsh_f = sb(g0, "wsh_f", [128, 1024], F32)
    col_a = sb(g0, "col_a", [128, 1], F32)
    col_b = sb(g0, "col_b", [128, 1], F32)
    col_c = sb(g0, "col_c", [128, 1], F32)
    col_d = sb(g0, "col_d", [128, 1], F32)
    qy = sb(g0, "qy", [128, 1024], F32)
    qv = sb(g0, "qv", [128, 1024], F32)
    qsg = sb(g0, "qsg", [128, 1024], BF16)
    qa_ = sb(g0, "qa_", [128, 1024], BF16)
    qb_ = sb(g0, "qb_", [128, 1024], BF16)
    qrB = sb(g0, "qrB", [128, 1024], BF16)
    qu2 = sb(g0, "qu2", [128, 1024], BF16)
    qz_ = sb(g0, "qz_", [128, 1024], BF16)
    qout = sb(g0, "qout", [128, 2048], BF16)
    amax = sb(g0, "amax", [128, 64], F32)
    scl = sb(g0, "scl", [128, 64], F32)
    rscl = sb(g0, "rscl", [128, 64], F32)
    scl16 = sb(g0, "scl16", [128, 64], BF16)
    mask_s = sb(g0, "mask_s", [128, 512], BF16)
    eye_s = sb(g0, "eye_s", [128, 128], F32)
    ones_s = sb(g0, "ones_s", [1, 128], BF16)
    bias_f = sb(g0, "bias_f", [1, 48], F32)
    bias_s = sb(g0, "bias_s", [1, 48], BF16)
    zmask_s = sb(g0, "zmask_s", [16, 1], F32)
    q32ch = sb(g0, "q32ch", [16, 256], F32)

    PS = [stack0.enter_context(nc.psum_tensor(f"PS{i}", [128, 512], F32))
          for i in range(8)]

    dma("sync", qS, eye_s[:], c_eye, sp=False)
    dma("sync", qS, mask_s[:], c_mask, sp=False)
    dma("sync", qS, ones_s[:], c_ones, sp=False)
    dma("sync", qS, bias_f[:], bias_in, sp=False)
    dma("sync", qS, zmask_s[:], c_zmask)
    t_consts0 = qS.n
    wait("sync", qS, t_consts0)
    xload = [dma("sync", qS, xt_f[0][:], x_in[0:128, :])]

    quant_state = {}

    def quant_tile(xin_ap, w, oq):
        nblk = w // 16
        x3 = xin_ap.rearrange("p (n k) -> p n k", k=16)
        am, sc_ = amax[:, 0:nblk], scl[:, 0:nblk]
        rs_, s16 = rscl[:, 0:nblk], scl16[:, 0:nblk]
        E("vector", lambda e, a=x3, b=am: e.tensor_reduce(
            b, a, AX.X, OP.max, apply_absolute_value=True))
        TS(sc_, am, 1.0 / 6.0, 1e-12, OP.mult, OP.max)
        E("vector", lambda e, a=sc_, b=rs_: e.reciprocal(b, a))
        rs_b = AP(rscl, 0, [[64, 128], [1, nblk], [0, 16]])
        sc_b = AP(scl, 0, [[64, 128], [1, nblk], [0, 16]])
        y3 = qy[:, 0:w].rearrange("p (n k) -> p n k", k=16)
        dvm = TTo(y3, x3, rs_b, OP.mult, sem=sDV)
        quant_state["y_mark"] = dvm
        yw, vw, sg = qy[:, 0:w], qv[:, 0:w], qsg[:, 0:w]
        wait("scalar", sDV, dvm)
        absm = op("scalar", lambda e, a=yw, o=vw: e.activation(o, a, AF.Abs),
                  sAC)
        acm = op("scalar", lambda e, a=yw, o=sg: e.activation(o, a, AF.Sign),
                 sAC)
        wait("vector", sAC, absm)
        aw, bw, zw = qa_[:, 0:w], qb_[:, 0:w], qz_[:, 0:w]
        TS(aw, vw, 96.0, None, OP.add)
        TS(bw, aw, -96.0, 2.0, OP.add, OP.min)
        STT(aw, vw, 2.5, bw, OP.is_gt, OP.add)
        STT(bw, vw, 3.5, aw, OP.is_gt, OP.add)
        STT(aw, vw, 5.0, bw, OP.is_gt, OP.add)
        STT(bw, vw, 5.0, aw, OP.is_gt, OP.add)
        wait("vector", sAC, acm)
        TTo(zw, bw, sg, OP.mult)
        z3 = qz_[:, 0:w].rearrange("p (n k) -> p n k", k=16)
        qo3 = qout[:, oq:oq + w].rearrange("p (n k) -> p n k", k=16)
        return TTo(qo3, z3, sc_b, OP.mult, sem=sDV)

    prev_qm = None
    wq_w = []
    shi = 0
    jobs = []
    for ti in range(3):
        jobs.append((wg_sh[128 * ti:128 * ti + 128, :], 128 * ti, 1024,
                     96 if ti == 2 else 128))
    for ti in range(3):
        jobs.append((wu_sh[128 * ti:128 * ti + 128, :], 352 + 128 * ti, 1024,
                     96 if ti == 2 else 128))
    for cc3 in range(3):
        cw3 = 768 if cc3 == 2 else 1024
        jobs.append((wd_sh[:, 1024 * cc3:1024 * cc3 + cw3],
                     ("wd", cc3, cw3), cw3, 128))
    for (srcap, dsti, wc, nrow) in jobs:
        if prev_qm is not None:
            wait("scalar", sDV, quant_state["y_mark"])
        t = dma("scalar", qA, wsh_f[:, 0:wc], srcap)
        wait("vector", qA, t)
        oq = (shi % 2) * 1024
        if shi >= 2:
            wait("vector", qS, wq_w[shi - 2])
        qm = quant_tile(wsh_f[:, 0:wc], wc, oq)
        prev_qm = qm
        wait("sync", sDV, qm)
        if isinstance(dsti, tuple):
            _, cc3, cw3 = dsti
            wq_w.append(dma("sync", qS,
                            AP(d_wqin, 704 * 1024 + 1024 * cc3,
                               [[2816, 128], [1, cw3]]),
                            qout[:, oq:oq + cw3]))
        else:
            wq_w.append(dma("sync", qS,
                            AP(d_wqin, dsti * 1024, [[1024, nrow], [1, 1024]]),
                            qout[0:nrow, oq:oq + 1024]))
        shi += 1
    t_wqin = qS.n

    gp = stack0.enter_context(contextlib.ExitStack())
    xn_fm = sb(gp, "xn_fm", [128, 8 * T], BF16)
    wmovA = sb(gp, "wmovA", [128, 8 * D], BF16)
    wmovB = sb(gp, "wmovB", [128, 8 * D], BF16)
    wbfs_s = sb(gp, "wbfs_s", [128, 8 * 48], BF16)
    bfs_tm = sb(gp, "bfs_tm", [128, NT * 48], F32)
    fs_st = sb(gp, "fs_st", [48, T], F32)
    wtmp = [sb(gp, f"wtmp{i}", [128, D], F32) for i in range(4)]
    ustg = sb(gp, "ustg", [128, 2048], BF16)
    qstg = sb(gp, "qstg", [128, 2048], BF16)
    gstg = sb(gp, "gstg", [128, 2048], BF16)
    f_sc = sb(gp, "f_sc", [128, 256], F32)
    s_sc = sb(gp, "s_sc", [128, 256], F32)
    QTop = sb(gp, "QTop", [128, 544], F32)
    QBot = sb(gp, "QBot", [128, 544], F32)
    SC0 = sb(gp, "SC0", [128, 544], F32)
    SC1 = sb(gp, "SC1", [128, 544], F32)
    t_g = sb(gp, "t_g", [128, 512], F32)
    t_h = sb(gp, "t_h", [128, 512], F32)
    t_r = sb(gp, "t_r", [128, 512], F32)
    t_HfA = sb(gp, "t_HfA", [128, 256], F32)
    t_rBt = sb(gp, "t_rBt", [128, 256], F32)
    t_rfl = sb(gp, "t_rfl", [128, 512], F32)
    t_det = sb(gp, "t_det", [128, 256], F32)
    t_rdet = sb(gp, "t_rdet", [128, 256], F32)
    t_w1 = sb(gp, "t_w1", [128, 544], F32)
    t_w2 = sb(gp, "t_w2", [128, 544], F32)
    t_bf = sb(gp, "t_bf", [128, 512], BF16)
    q32 = sb(gp, "q32", [128, 32], F32)

    def qslot(tsr, j):
        return AP(tsr, j, [[544, 128], [272, 2], [17, 16]])

    def qhalf(tsr, j, pair):
        return AP(tsr, j + 272 * pair, [[544, 128], [17, 16]])

    E("vector", lambda e: e.memset(qhalf(QTop, 0, 0), 1.0))
    E("vector", lambda e: e.memset(qhalf(QTop, 0, 1), 0.0))
    E("vector", lambda e: e.memset(qhalf(QBot, 0, 0), 0.0))
    E("vector", lambda e: e.memset(qhalf(QBot, 0, 1), 1.0))
    E("vector", lambda e: e.memset(qhalf(SC0, 16, 0), 1.0))
    E("vector", lambda e: e.memset(qhalf(SC0, 16, 1), 0.0))
    E("vector", lambda e: e.memset(qhalf(SC1, 16, 0), 0.0))
    E("vector", lambda e: e.memset(qhalf(SC1, 16, 1), 1.0))

    wjobs = []
    for src_, dst, ncol in [(wkT_i, wmovA, D), (wvT_i, wmovB, D),
                            (wbfsT_i, wbfs_s, 48)]:
        for b in range(8):
            wjobs.append((src_, dst, ncol, b))
    for src_, dst, ncol in [(wqT_i, wmovA, D), (wgT_i, wmovB, D)]:
        for b in range(8):
            wjobs.append((src_, dst, ncol, b))
    NW = len(wjobs)
    wload_t, wcast_t = [], []

    def emit_wjob(j):
        src_, dst, ncol, b = wjobs[j]
        if j >= 4:
            wait("gpsimd", sDV, wcast_t[j - 4])
        if ncol == D:
            t = dma("gpsimd", qG, wtmp[j % 4][:],
                    src_[128 * b:128 * b + 128, :])
        else:
            t = dma("gpsimd", qG, wtmp[j % 4][:, 0:48],
                    src_[128 * b:128 * b + 128, :])
        wload_t.append(t)
        wait("vector", qG, max(wload_t[min(3, j)], t))
        if ncol == D:
            wcast_t.append(CP(dst[:, D * b:D * b + D], wtmp[j % 4][:],
                              sem=sDV))
        else:
            wcast_t.append(CP(dst[:, 48 * b:48 * b + 48],
                              wtmp[j % 4][:, 0:48], sem=sDV))

    for j in range(24):
        emit_wjob(j)
    wait("vector", qS, t_consts0)
    bias_cast = CP(bias_s[:], bias_f[:])
    wait("gpsimd", qS, t_wqin)
    cc1 = sCC.inc(1)
    E("gpsimd", lambda e: e.collective_compute(
        "AllGather", OP.bypass, replica_groups=[[0, 1, 2, 3, 4, 5, 6, 7]],
        ins=[d_wqin.ap().opt()], outs=[d_wqout.ap().opt()]).then_inc(sCC.h, 1))
    wait("gpsimd", sCC, cc1)
    for r in range(8):
        dma("gpsimd", qG, AP(d_wgf, 352 * 1024 * r, [[1024, 352], [1, 1024]]),
            AP(d_wqout, 1056 * 1024 * r, [[1024, 352], [1, 1024]]), sp=False)
        dma("gpsimd", qG, AP(d_wuf, 352 * 1024 * r, [[1024, 352], [1, 1024]]),
            AP(d_wqout, (1056 * r + 352) * 1024, [[1024, 352], [1, 1024]]),
            sp=False)
        dma("gpsimd", qG, AP(d_wdf, 128 * 2816 * r, [[2816, 128], [1, 2816]]),
            AP(d_wqout, (1056 * r + 704) * 1024, [[2816, 128], [1, 2816]]),
            sp=(r == 7))
    t_repack = qG.n

    ph1_marks = []
    xnb_w, xq_w = [], []
    for i in range(NT + 1):
        if i < NT:
            xt = xt_f[i % 2]
            xbuf = xn_f if i % 2 == 0 else ds_rd
            cb = col_b if i % 2 == 0 else col_d
            wait("vector", qS, xload[i])
            E("vector", lambda e, a=xt: e.tensor_reduce(
                col_a[:], a[:], AX.X, OP.add))
            TS(col_a[:], col_a[:], -1.0 / D, None, OP.mult)
            dv_mean = TS(xbuf[:], xt[:], col_a[:], None, OP.add, sem=sDV)
            wait("scalar", sDV, dv_mean)
            ac_sq = op("scalar", lambda e, xb=xbuf, c=cb: e.activation(
                qrB[:], xb[:], AF.Square, accum_out=c[:]), sAC)
            ph1_marks.append((dv_mean, ac_sq))
            if i + 1 < NT:
                if i >= 1:
                    wait("sync", sDV, ph1_marks[i - 1][0])
                xload.append(dma("sync", qS, xt_f[(i + 1) % 2][:],
                                 x_in[128 * (i + 1):128 * (i + 2), :]))
        if i >= 1:
            j = i - 1
            xbufj = xn_f if j % 2 == 0 else ds_rd
            cbj = col_b if j % 2 == 0 else col_d
            wait("vector", sAC, ph1_marks[j][1])
            dvm = TS(cbj[:], cbj[:], 1.0 / D, 1e-5, OP.mult, OP.add,
                     sem=sDV)
            wait("scalar", sDV, dvm)
            E("scalar", lambda e, c=cbj: e.activation(col_c[:], c[:], AF.Ln))
            acm = op("scalar", lambda e: e.activation(
                col_c[:], col_c[:], AF.Exp, scale=-0.5), sAC)
            wait("vector", sAC, acm)
            TS(xbufj[:], xbufj[:], col_c[:], None, OP.mult)
            oh = (j % 2) * 1024
            if j >= 1:
                wait("vector", qS, xnb_w[j - 1])
            dvm = CP(qu2[:], xbufj[:], sem=sDV)
            wait("sync", sDV, dvm)
            xnb_w.append(dma("sync", qS,
                             d_xnb.ap()[128 * j:128 * j + 128, :], qu2[:]))
            if j >= 2:
                wait("vector", qS, xq_w[j - 2])
            else:
                wait("vector", qS, wq_w[7 + j])
            qm = quant_tile(xbufj[:], D, oh)
            wait("sync", sDV, qm)
            xq_w.append(dma("sync", qS,
                            d_xq.ap()[128 * j:128 * j + 128, :],
                            qout[:, oh:oh + 1024]))
    ph1_qs = qS.n



    wait("sync", qS, ph1_qs)
    for b in range(8):
        dma("sync", qS, xn_fm[:, T * b:T * b + T],
            AP(d_xnb, 128 * b, [[1024, 2048], [1, 128]]), transpose=True,
            sp=True)
    t_xnfm = qS.n

    wait("tensor", qS, t_xnfm)
    wait("tensor", sDV, bias_cast)
    bfs_ac = []
    for i in range(NT):
        pb_ = PS[4] if i % 2 == 0 else PS[5]
        if i >= 2:
            wait("tensor", sAC, bfs_ac[i - 2])
        for b in range(8):
            lhs = xn_fm[:, T * b + 128 * i: T * b + 128 * i + 128]
            mm(pb_[:, 0:48], lhs, wbfs_s[:, 48 * b:48 * b + 48],
               b == 0, False, skip=True)
        bfs_pe = mm(pb_[:, 0:48], ones_s[:], bias_s[:], False,
                    True, sem=sPE, skip=True)
        wait("scalar", sPE, bfs_pe)
        bac = op("scalar", lambda e, ii=i, pp=pb_: e.activation(
            bfs_tm[:, 48 * ii:48 * ii + 48], pp[:, 0:48], AF.Tanh,
            scale=0.5), sAC)
        bfs_ac.append(bac)
    wait("vector", sAC, bfs_ac[-1])
    bdv = TS(bfs_tm[:], bfs_tm[:], 0.5, 0.5, OP.mult, OP.add, sem=sDV)
    fs_dv = []
    wait("tensor", sDV, bdv)
    for i in range(NT):
        pt_ = PS[6] if i % 2 == 0 else PS[7]
        if i >= 2:
            wait("tensor", sDV, fs_dv[i - 2])
        trm = op("tensor", lambda e, ii=i, pp=pt_: e.transpose(
            pp[0:48, 0:128], bfs_tm[:, 48 * ii:48 * ii + 48],
            eye_s[:]), sPE)
        wait("vector", sPE, trm)
        dvm = CP(fs_st[:, 128 * i:128 * i + 128], pt_[0:48, 0:128],
                 sem=sDV)
        fs_dv.append(dvm)
    wait("sync", sDV, fs_dv[-1])
    dma("sync", qS, d_fs.ap()[:, :], fs_st[:])
    wait("sync", qS, qS.n)
    for hh in range(16):
        dma("sync", qS, f_sc[8 * hh:8 * hh + 8, :],
            AP(d_fs, (16 + hh) * 2048, [[256, 8], [1, 256]]), sp=False)
        dma("sync", qS, s_sc[8 * hh:8 * hh + 8, :],
            AP(d_fs, (32 + hh) * 2048, [[256, 8], [1, 256]]), sp=(hh == 15))
    t_fsc = qS.n
    wait("tensor", sAC, bfs_ac[-1])
    wait("tensor", sDV, fs_dv[-1])
    u_w, u_dv, kv_ac = [], [], []
    for i in range(NT):
        pk = (PS[0], PS[1]) if i % 2 == 0 else (PS[4], PS[5])
        pv = (PS[2], PS[3]) if i % 2 == 0 else (PS[6], PS[7])
        if i >= 2:
            wait("tensor", sAC, kv_ac[i - 2])
        for b in range(8):
            lhs = xn_fm[:, T * b + 128 * i: T * b + 128 * i + 128]
            mm(pk[0][:], lhs, wmovA[:, D * b:D * b + 512], b == 0, b == 7)
            k_pe = mm(pk[1][:], lhs, wmovA[:, D * b + 512:D * b + 1024],
                      b == 0, b == 7, sem=sPE if b == 7 else None)
        for b in range(8):
            lhs = xn_fm[:, T * b + 128 * i: T * b + 128 * i + 128]
            mm(pv[0][:], lhs, wmovB[:, D * b:D * b + 512], b == 0, b == 7)
            v_pe = mm(pv[1][:], lhs, wmovB[:, D * b + 512:D * b + 1024],
                      b == 0, b == 7, sem=sPE if b == 7 else None)
        kb_, vb_ = kvb[i % 2], kvb[2 + i % 2]
        if i >= 2:
            wait("scalar", sDV, u_dv[i - 2])
        wait("scalar", sPE, k_pe)
        E("scalar", lambda e, kk=kb_, p0=pk[0]: e.activation(
            kk[:, 0:512], p0[:], AF.Silu))
        E("scalar", lambda e, kk=kb_, p1=pk[1]: e.activation(
            kk[:, 512:1024], p1[:], AF.Silu))
        wait("scalar", sPE, v_pe)
        E("scalar", lambda e, vv=vb_, p0=pv[0]: e.activation(
            vv[:, 0:512], p0[:], AF.Silu))
        vac = op("scalar", lambda e, vv=vb_, p1=pv[1]: e.activation(
            vv[:, 512:1024], p1[:], AF.Silu), sAC)
        kv_ac.append(vac)
        wait("vector", sAC, vac)
        if i >= 2:
            wait("vector", qS, u_w[i - 2])
        bbc = AP(bfs_tm, 48 * i, [[NT * 48, 128], [1, 16], [0, 64]])
        k3 = kb_[:].rearrange("p (h d) -> p h d", d=64)
        v3 = vb_[:].rearrange("p (h d) -> p h d", d=64)
        oh = (i % 2) * 1024
        u3 = ustg[:, oh:oh + 1024].rearrange("p (h d) -> p h d", d=64)
        y3d = qy[:].rearrange("p (h d) -> p h d", d=64)
        TTo(y3d, k3, bbc, OP.mult)
        udv = TTo(u3, y3d, v3, OP.mult, sem=sDV)
        u_dv.append(udv)
        wait("sync", sDV, udv)
        u_w.append(dma("sync", qS, d_u.ap()[128 * i:128 * i + 128, :],
                       ustg[:, oh:oh + 1024]))
    wait("vector", sPE, sPE.n)
    for j in range(24, NW):
        emit_wjob(j)
    wait("tensor", sDV, wcast_t[-1])
    wait("tensor", sAC, kv_ac[-1])
    wait("scalar", sDV, u_dv[-1])
    q_w, g_w = [], []
    qg_ac, qg_dv = [], []
    for i in range(NT):
        pq = (PS[0], PS[1]) if i % 2 == 0 else (PS[4], PS[5])
        pg = (PS[2], PS[3]) if i % 2 == 0 else (PS[6], PS[7])
        if i >= 2:
            wait("tensor", sAC, qg_ac[i - 2])
        for b in range(8):
            lhs = xn_fm[:, T * b + 128 * i: T * b + 128 * i + 128]
            mm(pq[0][:], lhs, wmovA[:, D * b:D * b + 512], b == 0, b == 7)
            q_pe = mm(pq[1][:], lhs, wmovA[:, D * b + 512:D * b + 1024],
                      b == 0, b == 7, sem=sPE if b == 7 else None)
        for b in range(8):
            lhs = xn_fm[:, T * b + 128 * i: T * b + 128 * i + 128]
            mm(pg[0][:], lhs, wmovB[:, D * b:D * b + 512], b == 0, b == 7)
            g_pe = mm(pg[1][:], lhs, wmovB[:, D * b + 512:D * b + 1024],
                      b == 0, b == 7, sem=sPE if b == 7 else None)
        oh = (i % 2) * 1024
        gscr = kvb[i % 2]
        if i >= 2:
            wait("scalar", qS, q_w[i - 2])
            wait("scalar", sDV, qg_dv[i - 2])
        wait("scalar", sPE, q_pe)
        E("scalar", lambda e, o=oh, p0=pq[0]: e.activation(
            qstg[:, o:o + 512], p0[:], AF.Silu))
        E("scalar", lambda e, o=oh, p1=pq[1]: e.activation(
            qstg[:, o + 512:o + 1024], p1[:], AF.Silu))
        wait("scalar", sPE, g_pe)
        E("scalar", lambda e, gg=gscr, p0=pg[0]: e.activation(
            gg[:, 0:512], p0[:], AF.Tanh, scale=0.5))
        gac = op("scalar", lambda e, gg=gscr, p1=pg[1]: e.activation(
            gg[:, 512:1024], p1[:], AF.Tanh, scale=0.5), sAC)
        qg_ac.append(gac)
        wait("vector", sAC, gac)
        if i >= 2:
            wait("vector", qS, g_w[i - 2])
        gdv = TS(gstg[:, oh:oh + 1024], gscr[:], 0.5, 0.5,
                 OP.mult, OP.add, sem=sDV)
        qg_dv.append(gdv)
        wait("sync", sAC, gac)
        q_w.append(dma("sync", qS, d_q.ap()[128 * i:128 * i + 128, :],
                       qstg[:, oh:oh + 1024]))
        wait("sync", sDV, gdv)
        g_w.append(dma("sync", qS, d_g.ap()[128 * i:128 * i + 128, :],
                       gstg[:, oh:oh + 1024]))
    t_passB = sPE.n

    wait("vector", qS, t_fsc)

    def fsl(t2, j):
        return AP(t2, j, [[256, 128], [0, 2], [16, 16]])

    tmpA = AP(t_w1, 0, [[544, 128], [272, 2], [17, 16]])
    tmpB = AP(t_w2, 0, [[544, 128], [272, 2], [17, 16]])
    for j in range(1, 17):
        TTo(tmpA, fsl(f_sc, j - 1), qslot(QTop, j - 1), OP.mult)
        TTo(tmpB, fsl(s_sc, j - 1), qslot(QBot, j - 1), OP.mult)
        STT(qslot(QTop, j), tmpB, EPS, tmpA, OP.mult, OP.add)
        STT(qslot(QBot, j), tmpA, EPS, tmpB, OP.mult, OP.add)
    for jj in range(16):
        i2 = 15 - jj
        STT(tmpA, qslot(SC1, i2 + 1), EPS, qslot(SC0, i2 + 1),
            OP.mult, OP.add)
        STT(tmpB, qslot(SC0, i2 + 1), EPS, qslot(SC1, i2 + 1),
            OP.mult, OP.add)
        TTo(qslot(SC0, i2), fsl(f_sc, i2), tmpA, OP.mult)
        TTo(qslot(SC1, i2), fsl(s_sc, i2), tmpB, OP.mult)

    def slots16(tsr, j0):
        return AP(tsr, j0, [[544, 128], [272, 2], [17, 16], [1, 16]])

    th_ap = AP(t_h, 0, [[512, 128], [256, 2], [16, 16], [1, 16]])
    TTo(th_ap, slots16(SC0, 1), slots16(SC1, 1), OP.add)
    TS(th_ap, th_ap, CPc, None, OP.mult)

    def pp0(tsr, j0):
        return AP(tsr, j0, [[544, 128], [17, 16], [1, 16]])

    def pp1(tsr, j0):
        return AP(tsr, j0 + 272, [[544, 128], [17, 16], [1, 16]])

    tdet_ap = AP(t_det, 0, [[256, 128], [16, 16], [1, 16]])
    trd_ap = AP(t_rdet, 0, [[256, 128], [16, 16], [1, 16]])
    w1s = AP(t_w1, 0, [[544, 128], [16, 16], [1, 16]])
    w2s = AP(t_w2, 0, [[544, 128], [16, 16], [1, 16]])
    TTo(w1s, pp0(QTop, 1), pp1(QBot, 1), OP.mult)
    TTo(w2s, pp1(QTop, 1), pp0(QBot, 1), OP.mult)
    TTo(tdet_ap, w1s, w2s, OP.subtract)
    E("vector", lambda e: e.reciprocal(trd_ap, tdet_ap))
    tg_F = AP(t_g, 0, [[512, 128], [16, 16], [1, 16]])
    tg_S = AP(t_g, 256, [[512, 128], [16, 16], [1, 16]])
    TTo(w1s, pp1(QBot, 1), pp1(QTop, 1), OP.subtract)
    TTo(w1s, w1s, trd_ap, OP.mult)
    TS(tg_F, w1s, CPc, None, OP.mult)
    TTo(w2s, pp0(QTop, 1), pp0(QBot, 1), OP.subtract)
    TTo(w2s, w2s, trd_ap, OP.mult)
    TS(tg_S, w2s, CPc, None, OP.mult)
    f16a = AP(f_sc, 0, [[256, 128], [0, 2], [16, 16], [1, 16]])
    s16a = AP(s_sc, 0, [[256, 128], [0, 2], [16, 16], [1, 16]])
    tr_ap = AP(t_r, 0, [[512, 128], [256, 2], [16, 16], [1, 16]])
    w14 = AP(t_w1, 0, [[544, 128], [272, 2], [17, 16], [1, 16]])
    w24 = AP(t_w2, 0, [[544, 128], [272, 2], [17, 16], [1, 16]])
    TTo(w14, f16a, slots16(QTop, 0), OP.mult)
    TTo(w24, s16a, slots16(QBot, 0), OP.mult)
    TTo(tr_ap, AP(t_w1, 0, [[544, 128], [272, 2], [17, 16], [1, 16]]),
        AP(t_w2, 0, [[544, 128], [272, 2], [17, 16], [1, 16]]), OP.add)
    qb_e, qa_e = {}, {}
    for (nm, tsr, offp) in [("00", QTop, 0), ("01", QTop, 272),
                            ("10", QBot, 0), ("11", QBot, 272)]:
        qb_e[nm] = AP(tsr, offp + 17 + 16, [[544, 128], [34, 8]])
        qa_e[nm] = AP(tsr, offp + 16, [[544, 128], [34, 8]])

    def bc16(a2):
        return AP(a2.tensor, a2.offset, [a2.ap[0], a2.ap[1], [0, 16]])

    hA_F = AP(t_h, 0, [[512, 128], [32, 8], [1, 16]])
    hA_S = AP(t_h, 256, [[512, 128], [32, 8], [1, 16]])
    Hf_F = AP(t_HfA, 0, [[256, 128], [16, 8], [1, 16]])
    Hf_S = AP(t_HfA, 128, [[256, 128], [16, 8], [1, 16]])
    w1h = AP(t_w1, 0, [[544, 128], [16, 8], [1, 16]])
    w2h = AP(t_w2, 0, [[544, 128], [16, 8], [1, 16]])
    TTo(w1h, bc16(qb_e["00"]), hA_F, OP.mult)
    TTo(w2h, bc16(qb_e["01"]), hA_S, OP.mult)
    TTo(Hf_F, w1h, w2h, OP.add)
    TTo(w1h, bc16(qb_e["10"]), hA_F, OP.mult)
    TTo(w2h, bc16(qb_e["11"]), hA_S, OP.mult)
    TTo(Hf_S, w1h, w2h, OP.add)
    rdB = AP(t_rdet, 31, [[256, 128], [32, 8]])
    rB_F = AP(t_r, 16, [[512, 128], [32, 8], [1, 16]])
    rB_S = AP(t_r, 256 + 16, [[512, 128], [32, 8], [1, 16]])
    rt_F = AP(t_rBt, 0, [[256, 128], [16, 8], [1, 16]])
    rt_S = AP(t_rBt, 128, [[256, 128], [16, 8], [1, 16]])
    TTo(w1h, bc16(qb_e["11"]), rB_F, OP.mult)
    TTo(w2h, bc16(qb_e["10"]), rB_S, OP.mult)
    TTo(w1h, w1h, w2h, OP.subtract)
    TTo(rt_F, w1h, bc16(rdB), OP.mult)
    TTo(w1h, bc16(qb_e["00"]), rB_S, OP.mult)
    TTo(w2h, bc16(qb_e["01"]), rB_F, OP.mult)
    TTo(w1h, w1h, w2h, OP.subtract)
    TTo(rt_S, w1h, bc16(rdB), OP.mult)
    rA_F = AP(t_r, 0, [[512, 128], [32, 8], [1, 16]])
    rA_S = AP(t_r, 256, [[512, 128], [32, 8], [1, 16]])
    rf_FA = AP(t_rfl, 0, [[512, 128], [32, 8], [1, 16]])
    rf_SA = AP(t_rfl, 256, [[512, 128], [32, 8], [1, 16]])
    rf_FB = AP(t_rfl, 16, [[512, 128], [32, 8], [1, 16]])
    rf_SB = AP(t_rfl, 256 + 16, [[512, 128], [32, 8], [1, 16]])
    TS(rf_FA, rA_F, 0.5, None, OP.mult)
    TS(rf_SA, rA_S, 0.5, None, OP.mult)
    TTo(w1h, bc16(qa_e["00"]), rB_F, OP.mult)
    TTo(w2h, bc16(qa_e["10"]), rB_S, OP.mult)
    TTo(w1h, w1h, w2h, OP.add)
    TS(rf_FB, w1h, 0.5, None, OP.mult)
    TTo(w1h, bc16(qa_e["01"]), rB_F, OP.mult)
    TTo(w2h, bc16(qa_e["11"]), rB_S, OP.mult)
    TTo(w1h, w1h, w2h, OP.add)
    TS(rf_SB, w1h, 0.5, None, OP.mult)
    for k, (aa, bb2, cc, dd) in enumerate([("00", "00", "01", "10"),
                                           ("00", "01", "01", "11"),
                                           ("10", "00", "11", "10"),
                                           ("10", "01", "11", "11")]):
        TTo(t_w1[:, 0:8], qb_e[aa], qa_e[bb2], OP.mult)
        TTo(t_w2[:, 0:8], qb_e[cc], qa_e[dd], OP.mult)
        TTo(q32[:, 8 * k:8 * k + 8], t_w1[:, 0:8], t_w2[:, 0:8], OP.add)
    dump_jobs = [(d_tg, t_g, 512), (d_th, t_h, 512), (d_tr, t_r, 512),
                 (d_tHfA, t_HfA, 256), (d_trBt, t_rBt, 256)]
    wait("sync", sDV, sDV.n)
    for dstt, srct, wdt in dump_jobs:
        hw = wdt // 2
        for comp in range(2):
            dma("sync", qS,
                AP(dstt, comp * 128 * hw, [[hw, 128], [1, hw]]),
                srct[:, comp * hw:comp * hw + hw], sp=False)
    dvm = CP(t_bf[:, 0:512], t_rfl[:, 0:512], sem=sDV)
    wait("sync", sDV, dvm)
    for comp in range(2):
        t_trfl_dump = dma("sync", qS,
                          AP(d_trfl, comp * 128 * 256,
                             [[256, 128], [1, 256]]),
                          t_bf[:, comp * 256:comp * 256 + 256])
    t_q32w = dma("sync", qS, d_q32.ap()[:, :], q32[:])
    wait("sync", qS, qS.n)
    for ent in range(4):
        tq32 = dma("sync", qS, q32ch[:, 64 * ent:64 * ent + 64],
                   AP(d_q32, 8 * ent, [[256, 16], [32, 8], [1, 8]]),
                   sp=(ent == 3))
    t_dumps = qS.n

    for b4 in range(4):
        dma("sync", qS, AP(d_LWf, b4 * 8192, [[32768, 32], [1, 8192]]),
            c_lwz, sp=False)
    for b16 in range(16):
        dma("sync", qS, AP(d_RWf, b16 * 2176, [[34816, 32], [1, 2176]]),
            c_rws, sp=False)
    for r in range(8):
        dma("sync", qS, AP(d_L2f, r * 32768, [[1, 32768]]),
            AP(c_zb16.tensor, 0, [[1, 32768]]), sp=(r == 7))
    lwk = [("gA", d_tg, 0, 0), ("gA", d_tg, 1, 0),
           ("HfA", d_tHfA, 0, 0), ("HfA", d_tHfA, 1, 0),
           ("gB", d_tg, 0, 16), ("gB", d_tg, 1, 16),
           ("hB", d_th, 0, 16), ("hB", d_th, 1, 16)]
    for m in range(4):
        for k in range(8):
            nm, srcd, comp, off = lwk[k]
            if srcd is d_tHfA:
                soff = comp * 16384 + m * 16
                gst, hst = 64, 1024
            else:
                soff = comp * 32768 + m * 32 + (16 if off == 16 else 0)
                gst, hst = 128, 2048
            dst = AP(d_LWf, (8 * m + k) * 32768 + 32 * m + off,
                     [[2048, 16], [128, 16], [1, 16]])
            src2 = AP(srcd, soff, [[gst, 16], [hst, 16], [1, 16]])
            dma("sync", qS, dst, src2, sp=False)
    rwk = [("rA", d_tr, 0, 0), ("rA", d_tr, 1, 0),
           ("rBt", d_trBt, 0, 16), ("rBt", d_trBt, 1, 16),
           ("rB", d_tr, 0, 16), ("rB", d_tr, 1, 16)]
    for m in range(4):
        for k in range(6):
            nm, srcd, comp, off = rwk[k]
            if srcd is d_trBt:
                soff = comp * 16384 + m * 16
                gst, hst = 64, 1024
            else:
                soff = comp * 32768 + m * 32 + (16 if nm == "rB" else 0)
                gst, hst = 128, 2048
            dst = AP(d_RWf, (8 * m + k) * 34816 + 32 * m + off,
                     [[2176, 16], [136, 16], [1, 16]])
            src2 = AP(srcd, soff, [[gst, 16], [hst, 16], [1, 16]])
            dma("sync", qS, dst, src2, sp=False)
    for m in range(4):
        for s in range(2):
            dst = AP(d_L2f, (2 * m + s) * 32768 + 32 * m,
                     [[2048, 16], [128, 16], [1, 32]])
            src2 = AP(d_trfl, s * 32768 + m * 32,
                      [[128, 16], [2048, 16], [1, 32]])
            dma("sync", qS, dst, src2, sp=(m == 3 and s == 1))
    t_asm = qS.n

    gp.close()

    gw = stack0.enter_context(contextlib.ExitStack())
    LW = sb(gw, "LW", [32, 2048], F32)
    RW = sb(gw, "RW", [32, 2176], F32)
    L2 = sb(gw, "L2", [8, 2048], BF16)
    R2 = sb(gw, "R2", [8, 1024], BF16)
    Hf8 = sb(gw, "Hf8", [128, 64], BF16)
    Wt_t = [sb(gw, f"Wt{i2}", [128, 512], BF16) for i2 in range(2)]
    Bst = sb(gw, "Bst", [8, 1024], F32)
    ot_f = sb(gw, "ot_f", [128, 512], F32)
    ogt = [sb(gw, f"ogt{i2}", [128, D], BF16) for i2 in range(2)]
    urd = [sb(gw, f"urd{i2}", [128, D], BF16) for i2 in range(2)]
    qrd = sb(gw, "qrd", [128, D], BF16)
    grd = sb(gw, "grd", [128, D], BF16)
    zhF = sb(gw, "zhF", [16, 65 * 66], BF16)
    zhS = sb(gw, "zhS", [16, 65 * 66], BF16)

    zgF = sb(gw, "zgF", [16, 64], BF16)
    zgS = sb(gw, "zgS", [16, 64], BF16)
    zi_F = sb(gw, "zi_F", [16, 64], F32)
    zi_S = sb(gw, "zi_S", [16, 64], F32)
    ztmp = sb(gw, "ztmp", [16, 1040], BF16)
    cht = [sb(gw, f"cht{i2}", [16, 66], F32) for i2 in range(2)]
    xq_fm = sb(gw, "xq_fm", [128, 8 * T], BF16)
    wgc = sb(gw, "wgc", [128, 8 * 1024], BF16)
    wuc = sb(gw, "wuc", [128, 8 * 1024], BF16)

    for b in range(8):
        dma("sync", qS, xq_fm[:, T * b:T * b + T],
            AP(d_xq, 128 * b, [[1024, 2048], [1, 128]]), transpose=True,
            sp=True)
    t_xqfm = qS.n

    BcrF = [sb(gw, f"BcrF{i2}", [16, 528], F32) for i2 in range(2)]
    BcrS = [sb(gw, f"BcrS{i2}", [16, 528], F32) for i2 in range(2)]
    bcr_t = [None, None]

    def bcr_load(kb, buf):
        wait("scalar", qS, braw_w[2 * kb + 1])
        dma("scalar", qA, AP(BcrF[buf], 0, [[528, 16], [66, 8], [1, 64]]),
            AP(d_Braw, 8 * kb * 1024, [[64, 16], [1024, 8], [1, 64]]),
            sp=False)
        t2 = dma("scalar", qA,
                 AP(BcrS[buf], 0, [[528, 16], [66, 8], [1, 64]]),
                 AP(d_Braw, 65536 + 8 * kb * 1024,
                    [[64, 16], [1024, 8], [1, 64]]))
        return t2

    E("vector", lambda e: e.memset(zhF[:, 0:66], 0.0))
    E("vector", lambda e: e.memset(zhS[:, 0:66], 0.0))
    E("vector", lambda e: e.memset(zhF[:, 64:65], 1.0))
    E("vector", lambda e: e.memset(zhS[:, 65:66], 1.0))
    wait("vector", qS, tq32)

    chain_state = {"c": 0, "loads": 8}

    def chain_step(sem=None):
        c = chain_state["c"]
        kb, cin = c // 8, c % 8
        buf = kb % 2
        if cin == 0:
            wait("vector", qA, bcr_t[buf])
        zFc = zhF[:, 66 * c:66 * c + 66]
        zSc = zhS[:, 66 * c:66 * c + 66]
        bF = BcrF[buf][:, 66 * cin:66 * cin + 66]
        bS = BcrS[buf][:, 66 * cin:66 * cin + 66]

        def qc(k):
            return q32ch[:, 64 * k + c:64 * k + c + 1]

        TS(cht[0][:], zFc, qc(0), None, OP.mult)
        TS(cht[1][:], zSc, qc(1), None, OP.mult)
        TTo(cht[0][:], cht[0][:], cht[1][:], OP.add)
        TTo(zhF[:, 66 * (c + 1):66 * (c + 1) + 66], cht[0][:], bF, OP.add)
        TS(cht[0][:], zFc, qc(2), None, OP.mult)
        TS(cht[1][:], zSc, qc(3), None, OP.mult)
        TTo(cht[0][:], cht[0][:], cht[1][:], OP.add)
        sem_eff = sem if sem is not None else (sDV if cin == 7 else None)
        lastm = TTo(zhS[:, 66 * (c + 1):66 * (c + 1) + 66], cht[0][:], bS,
                    OP.add, sem=sem_eff)
        chain_state["c"] = c + 1
        if cin == 7 and chain_state["loads"] < 8:
            kb2 = chain_state["loads"]
            wait("scalar", sDV, lastm)
            bcr_t[kb2 % 2] = bcr_load(kb2, kb2 % 2)
            chain_state["loads"] += 1
        return lastm

    wait("vector", qS, t_q32w)
    for buf2 in range(2):
        E("vector", lambda e, b2=buf2: e.memset(BcrF[b2][:], 0.0))
        E("vector", lambda e, b2=buf2: e.memset(BcrS[b2][:], 0.0))
    bm = op("vector", lambda e: e.engine_nop(), sDV)
    wait("scalar", sDV, bm)

    pre_chain_ac = sAC.n
    wait("scalar", qS, t_asm)
    wait("scalar", qS, u_w[-1])
    braw_w = []
    eb_pe = []
    chain_marks = []
    for G in range(16):
        if G > 0:
            wait("scalar", sPE, eb_pe[-1])
        dma("scalar", qA, LW[:, :], AP(d_LWf, G * 2048,
                                       [[32768, 32], [1, 2048]]), sp=False)
        dma("scalar", qA, RW[:, :], AP(d_RWf, G * 2176,
                                       [[34816, 32], [1, 2176]]), sp=False)
        tu = dma("scalar", qA, urd[G % 2][:],
                 AP(d_u, G * 128 * 1024, [[1024, 128], [1, 1024]]))
        wait("tensor", qA, tu)
        for sb8 in range(2):
            pe_m = None
            for gi in range(8):
                hh = sb8 * 8 + gi
                pe_m = mm(PS[5][:, 8 * gi:8 * gi + 8],
                          LW[:, 128 * hh:128 * hh + 128],
                          RW[:, 136 * hh + 128:136 * hh + 136],
                          True, True, sem=sPE if gi == 7 else None, skip=True)
            wait("vector", sPE, pe_m)
            dvm = CP(Hf8[:], PS[5][:, 0:64], sem=sDV)
            wait("tensor", sDV, dvm)
            pb = PS[6] if sb8 % 2 == 0 else PS[7]
            pe_m = None
            for gi in range(8):
                hh = sb8 * 8 + gi
                pe_m = mm(pb[0:8, 64 * gi:64 * gi + 64],
                          Hf8[:, 8 * gi:8 * gi + 8],
                          urd[G % 2][:, 64 * hh:64 * hh + 64],
                          True, True, sem=sPE if gi == 7 else None, skip=True)
            eb_pe.append(pe_m)
            wait("vector", sPE, pe_m)
            if len(braw_w) >= 1 and sb8 == 0:
                wait("vector", qS, braw_w[-1])
            dvm = CP(Bst[:, 512 * sb8:512 * sb8 + 512], pb[0:8, 0:512],
                     sem=sDV)
            if sb8 == 1:
                eb_last_dv = dvm
                wait("sync", sDV, dvm)
                for s in range(2):
                    tb = dma("sync", qS,
                             AP(d_Braw, s * 65536 + 4 * G * 1024,
                                [[1024, 4], [1, 1024]]),
                             AP(Bst, 4096 * s, [[1024, 4], [1, 1024]]))
                braw_w.append(tb)
        if G % 2 == 1:
            m_ = (G - 1) // 2
            if m_ >= 2:
                wait("scalar", sDV, chain_marks[m_ - 2])
            bcr_t[m_ % 2] = bcr_load(m_, m_ % 2)
            if m_ >= 1:
                for _ in range(8):
                    sem2 = sDV if chain_state["c"] == 63 else None
                    cm = chain_step(sem=sem2)
                chain_marks.append(cm)
    for _ in range(8):
        sem2 = sDV if chain_state["c"] == 63 else None
        cm = chain_step(sem=sem2)
    chain_marks.append(cm)
    t_earlyB_pe = eb_pe[-1]

    wait("gpsimd", sDV, cm)
    dma("gpsimd", qG, AP(d_zin, 0, [[64, 16], [1, 64]]),
        AP(zhF, 66 * 64, [[4290, 16], [1, 64]]), sp=False)
    tz = dma("gpsimd", qG, AP(d_zin, 16 * 64, [[64, 16], [1, 64]]),
             AP(zhS, 66 * 64, [[4290, 16], [1, 64]]))
    wait("gpsimd", qG, tz)
    ccz = sCC.inc(1)
    E("gpsimd", lambda e: e.collective_compute(
        "AllGather", OP.bypass,
        replica_groups=[[0, 1], [2, 3], [4, 5], [6, 7]],
        ins=[d_zin.ap().opt()], outs=[d_zout.ap().opt()]).then_inc(sCC.h, 1))
    wait("gpsimd", sCC, ccz)
    dma("gpsimd", qG, zgF[:], d_zout.ap()[0:16, :], sp=False)
    tz2 = dma("gpsimd", qG, zgS[:], d_zout.ap()[16:32, :])

    IC = [(0, 1024), (1024, 1024), (2048, 768)]
    wait("scalar", qG, t_repack)
    wait("scalar", sPE, t_passB)
    wait("tensor", sAC, pre_chain_ac)
    wait("tensor", sDV, eb_last_dv)
    hq_w = []
    hq_i = 0
    gu_last_pe = None
    hdone_l, sg_l = [], []
    for ic, (i0_, cw) in enumerate(IC):
        if ic > 0:
            wait("sync", sPE, gu_last_pe)
        last = None
        for b in range(8):
            dma("sync", qS, wgc[:, 1024 * b:1024 * b + cw],
                AP(d_wgf, i0_ * 1024 + 128 * b, [[1024, cw], [1, 128]]),
                transpose=True, sp=True)
            last = dma("sync", qS, wuc[:, 1024 * b:1024 * b + cw],
                       AP(d_wuf, i0_ * 1024 + 128 * b,
                          [[1024, cw], [1, 128]]),
                       transpose=True, sp=True)
        if ic == 0:
            wait("tensor", qS, t_xqfm)
        wait("tensor", qS, last)
        for i in range(NT):
            pp4 = 4 * (hq_i % 2)
            pga, pgb = PS[pp4], PS[pp4 + 1]
            pua, pub = PS[pp4 + 2], PS[pp4 + 3]
            if len(hdone_l) >= 2:
                wait("tensor", sDV, hdone_l[-2])
                wait("tensor", sAC, sg_l[-2])
            for b in range(8):
                lhs = xq_fm[:, T * b + 128 * i: T * b + 128 * i + 128]
                mm(pga[:, 0:512], lhs, wgc[:, 1024 * b:1024 * b + 512],
                   b == 0, b == 7)
                gate_pe = mm(pgb[:, 0:cw - 512], lhs,
                             wgc[:, 1024 * b + 512:1024 * b + cw],
                             b == 0, b == 7, sem=sPE if b == 7 else None)
            for b in range(8):
                lhs = xq_fm[:, T * b + 128 * i: T * b + 128 * i + 128]
                mm(pua[:, 0:512], lhs, wuc[:, 1024 * b:1024 * b + 512],
                   b == 0, b == 7)
                up_pe = mm(pub[:, 0:cw - 512], lhs,
                           wuc[:, 1024 * b + 512:1024 * b + cw],
                           b == 0, b == 7, sem=sPE if b == 7 else None)
            gu_last_pe = up_pe
            scr = xn_f if hq_i % 2 == 0 else ds_rd
            wait("scalar", sPE, gate_pe)
            E("scalar", lambda e, ss=scr, pa=pga: e.activation(
                ss[:, 0:512], pa[:, 0:512], AF.Silu))
            sg_ac = op("scalar", lambda e, o2=scr[:, 512:cw],
                       p2=pgb[:, 0:cw - 512]: e.activation(
                           o2, p2, AF.Silu), sAC)
            sg_l.append(sg_ac)
            wait("vector", sAC, sg_ac)
            wait("vector", sPE, up_pe)
            TTo(res_f[:, 0:512], scr[:, 0:512], pua[:, 0:512],
                OP.mult)
            hdone = TTo(res_f[:, 512:cw], scr[:, 512:cw],
                        pub[:, 0:cw - 512], OP.mult, sem=sDV)
            hdone_l.append(hdone)
            oq = (hq_i % 2) * 1024
            if hq_i >= 2:
                wait("vector", qS, hq_w[hq_i - 2])
            qm = quant_tile(res_f[:, 0:cw], cw, oq)
            wait("sync", sDV, qm)
            hq_w.append(dma("sync", qS,
                            AP(d_hq, 128 * i * I + i0_,
                               [[2816, 128], [1, cw]]),
                            qout[:, oq:oq + cw]))
            hq_i += 1

    wait("vector", qG, tz2)
    TS(zi_F[:], zgF[:], zmask_s[:, 0:1], None, OP.mult)
    TS(zi_S[:], zgS[:], zmask_s[:, 0:1], None, OP.mult)
    for ch5 in range(5):
        c0 = ch5 * 13
        ncc = 13
        zh3F = AP(zhF, 66 * c0, [[4290, 16], [66, ncc], [1, 64]])
        zh3S = AP(zhS, 66 * c0, [[4290, 16], [66, ncc], [1, 64]])
        zt3 = AP(ztmp, 0, [[1040, 16], [64, ncc], [1, 64]])
        pbF = AP(zhF, 66 * c0 + 64, [[4290, 16], [66, ncc], [0, 64]])
        pbF2 = AP(zhF, 66 * c0 + 65, [[4290, 16], [66, ncc], [0, 64]])
        pbS = AP(zhS, 66 * c0 + 64, [[4290, 16], [66, ncc], [0, 64]])
        pbS2 = AP(zhS, 66 * c0 + 65, [[4290, 16], [66, ncc], [0, 64]])
        zbF = AP(zi_F, 0, [[64, 16], [0, ncc], [1, 64]])
        zbS = AP(zi_S, 0, [[64, 16], [0, ncc], [1, 64]])
        TTo(zt3, pbF, zbF, OP.mult)
        TTo(zh3F, zh3F, zt3, OP.add)
        TTo(zt3, pbF2, zbS, OP.mult)
        TTo(zh3F, zh3F, zt3, OP.add)
        TTo(zt3, pbS, zbF, OP.mult)
        TTo(zh3S, zh3S, zt3, OP.add)
        TTo(zt3, pbS2, zbS, OP.mult)
        zc_dv = TTo(zh3S, zh3S, zt3, OP.add, sem=sDV)
    wait("gpsimd", sDV, zc_dv)
    dma("gpsimd", qG, AP(d_zc, 0, [[4290, 16], [1, 4290]]), zhF[:],
        sp=False)
    tzc = dma("gpsimd", qG,
              AP(d_zc, 16 * 65 * 66, [[4290, 16], [1, 4290]]), zhS[:])
    wait("gpsimd", qG, tzc)
    for m in range(4):
        for s in range(2):
            dst = AP(d_R2f, (2 * m + s) * 16384,
                     [[0, 1], [1024, 16], [64, 16], [1, 64]])
            src3 = AP(d_zc, s * 68640 + m * 66,
                      [[0, 1], [264, 16], [4290, 16], [1, 64]])
            dma("gpsimd", qG, dst, src3, sp=(m == 3 and s == 1))
    wait("gpsimd", qG, qG.n)
    t_r2f = qG.n
    t_hq = qS.n

    lw_prev_pe = t_earlyB_pe
    og_w = []
    wait("scalar", qS, g_w[-1])
    og_dv = None
    for G in range(16):
        wait("scalar", sPE, lw_prev_pe)
        if og_dv is not None:
            wait("scalar", sDV, og_dv)
        dma("scalar", qA, LW[:, :],
            AP(d_LWf, G * 2048, [[32768, 32], [1, 2048]]), sp=False)
        dma("scalar", qA, RW[:, :],
            AP(d_RWf, G * 2176, [[34816, 32], [1, 2176]]), sp=False)
        dma("scalar", qA, L2[:, :],
            AP(d_L2f, G * 2048, [[32768, 8], [1, 2048]]), sp=False)
        if G == 0:
            wait("scalar", qG, t_r2f)
        dma("scalar", qA, R2[:, :],
            AP(d_R2f, G * 1024, [[16384, 8], [1, 1024]]), sp=False)
        dma("scalar", qA, urd[G % 2][:],
            AP(d_u, G * 128 * 1024, [[1024, 128], [1, 1024]]), sp=False)
        dma("scalar", qA, qrd[:],
            AP(d_q, G * 128 * 1024, [[1024, 128], [1, 1024]]), sp=False)
        tf4 = dma("scalar", qA, grd[:],
                  AP(d_g, G * 128 * 1024, [[1024, 128], [1, 1024]]))
        wait("tensor", qA, tf4)
        if len(og_w) >= 2:
            wait("vector", qS, og_w[-2])
        for hhf in range(2):
            pyb = PS[6] if hhf == 0 else PS[7]
            for quad in range(2):
                pe_m = None
                for gq in range(4):
                    hh = hhf * 8 + quad * 4 + gq
                    pe_m = mm(PS[5][:, 128 * gq:128 * gq + 128],
                              LW[:, 128 * hh:128 * hh + 128],
                              RW[:, 136 * hh:136 * hh + 128],
                              True, True,
                              sem=sPE if gq == 3 else None, skip=True)
                wait("vector", sPE, pe_m)
                wtb = Wt_t[quad]
                dvm = TTo(wtb[:], PS[5][:], mask_s[:], OP.mult, sem=sDV)
                wait("tensor", sDV, dvm)
                for gq in range(4):
                    hh = hhf * 8 + quad * 4 + gq
                    gi = quad * 4 + gq
                    mm(pyb[:, 64 * gi:64 * gi + 64],
                       wtb[:, 128 * gq:128 * gq + 128],
                       urd[G % 2][:, 64 * hh:64 * hh + 64],
                       True, False, skip=True)
                    lw_prev_pe = mm(
                        pyb[:, 64 * gi:64 * gi + 64],
                        L2[:, 128 * hh:128 * hh + 128],
                        R2[:, 64 * hh:64 * hh + 64],
                        False, True,
                        sem=sPE if (quad == 1 and gq == 3) else None,
                        skip=True)
            wait("vector", sPE, lw_prev_pe)
            qsl = qrd[:, 512 * hhf:512 * hhf + 512]
            gsl = grd[:, 512 * hhf:512 * hhf + 512]
            TTo(ot_f[:], qsl, pyb[:], OP.mult)
            og_dv = TTo(ogt[G % 2][:, 512 * hhf:512 * hhf + 512],
                        ot_f[:], gsl, OP.mult, sem=sDV)
            wait("tensor", sDV, og_dv)
        wait("sync", sDV, og_dv)
        og_w.append(dma("sync", qS, d_og.ap()[128 * G:128 * G + 128, :],
                        ogt[G % 2][:]))
    t_ogw = qS.n
    t_lateW_pe = lw_prev_pe

    gw.close()

    ga = stack0.enter_context(contextlib.ExitStack())
    attn_tm = sb(ga, "attn_tm", [128, NT * D], BF16)
    gp8 = stack0.enter_context(contextlib.ExitStack())
    og_fm = sb(gp8, "og_fm", [128, 8 * T], BF16)
    woT_s = sb(gp8, "woT_s", [128, 8 * D], BF16)

    wait("sync", qS, t_ogw)
    for b in range(8):
        dma("sync", qS, og_fm[:, T * b:T * b + T],
            AP(d_og, 128 * b, [[1024, 2048], [1, 128]]), transpose=True,
            sp=True)
    t_ogfm = qS.n
    wo_cast = []
    wo_ld = []
    for b in range(8):
        if b >= 2:
            wait("gpsimd", sDV, wo_cast[b - 2])
        wo_ld.append(dma("gpsimd", qG, xt_f[b % 2][:],
                         woT_i[128 * b:128 * b + 128, :]))
        wait("vector", qG, max(wo_ld[min(1, b)], wo_ld[b]))
        wo_cast.append(CP(woT_s[:, D * b:D * b + D], xt_f[b % 2][:], sem=sDV))
    wait("tensor", qS, t_ogfm)
    wait("tensor", sDV, wo_cast[-1])
    for i in range(NT):
        for b in range(8):
            lhs = og_fm[:, T * b + 128 * i: T * b + 128 * i + 128]
            mm(PS[0][:], lhs, woT_s[:, D * b:D * b + 512], b == 0, b == 7)
            wo_pe = mm(PS[1][:], lhs, woT_s[:, D * b + 512:D * b + 1024],
                       b == 0, b == 7, sem=sPE if b == 7 else None)
        wait("scalar", sPE, wo_pe)
        E("scalar", lambda e, ii=i: e.activation(
            attn_tm[:, D * ii:D * ii + 512], PS[0][:], AF.Copy))
        aac = op("scalar", lambda e, ii=i: e.activation(
            attn_tm[:, D * ii + 512:D * ii + 1024], PS[1][:], AF.Copy), sAC)
        wait("tensor", sAC, aac)
    t_wo_pe = sPE.n

    gp8.close()
    gp9 = stack0.enter_context(contextlib.ExitStack())
    wdT_s = sb(gp9, "wdT_s", [128, 11 * 1024], BF16)
    hq_fm = sb(gp9, "hq_fm", [128, 11 * 1024], BF16)

    out_w, res_dv, dn_pe_l, ds_w = [], [], [], []
    wait("vector", qS, t_hq)
    wait("scalar", qG, t_repack)
    wait("tensor", sDV, sDV.n)
    wait("tensor", sAC, sAC.n)
    for ihalf in range(2):
        wait("sync", sPE, t_wo_pe if ihalf == 0 else dn_pe_l[-1])
        if ihalf == 1:
            wait("sync", qS, ds_w[-1])
        wdl = []
        for ib in range(11):
            ib2 = ihalf * 11 + ib
            wdl.append(dma("sync", qS,
                           wdT_s[:, 1024 * ib:1024 * ib + 1024],
                           AP(d_wdf, 128 * ib2, [[2816, 1024], [1, 128]]),
                           transpose=True, sp=True))
        for thalf in range(2):
            if ihalf == 0 and thalf == 0:
                wait("sync", qS, t_hq)
                wait("sync", sPE, t_wo_pe)
            if not (ihalf == 0 and thalf == 0):
                wait("sync", sPE, dn_pe_l[-1])
            hql = []
            for ib in range(11):
                ib2 = ihalf * 11 + ib
                hql.append(dma("sync", qS,
                               hq_fm[:, 1024 * ib:1024 * ib + 1024],
                               AP(d_hq, thalf * 1024 * I + 128 * ib2,
                                  [[2816, 1024], [1, 128]]), transpose=True,
                               sp=True))
            wait("tensor", qS, hql[-1])
            if thalf == 0:
                wait("tensor", qS, wdl[-1])
            for it in range(8):
                i = thalf * 8 + it
                nbuf = len(out_w) + len(ds_w)
                if nbuf >= 2:
                    if len(res_dv) >= 2:
                        wait("sync", sDV, res_dv[-2])
                    if ihalf == 0 and len(ds_w) >= 2:
                        wait("vector", qS, ds_w[-2])
                    if ihalf == 1 and len(out_w) >= 2:
                        wait("vector", qS, out_w[-2])
                if len(res_dv) >= 2:
                    wait("tensor", sDV, res_dv[-2])
                pba, pbb = PS[2 * (it % 2)], PS[2 * (it % 2) + 1]
                for ib in range(11):
                    lhs = hq_fm[:, 1024 * ib + 128 * it:
                                1024 * ib + 128 * it + 128]
                    mm(pba[:], lhs, wdT_s[:, 1024 * ib:1024 * ib + 512],
                       ib == 0, ib == 10)
                    dn_pe = mm(pbb[:], lhs,
                               wdT_s[:, 1024 * ib + 512:1024 * ib + 1024],
                               ib == 0, ib == 10,
                               sem=sPE if ib == 10 else None)
                dn_pe_l.append(dn_pe)
                if ihalf == 0:
                    wait("vector", sPE, dn_pe)
                    oh9 = (it % 2) * 1024
                    CP(qout[:, oh9:oh9 + 512], pba[:])
                    dvm = CP(qout[:, oh9 + 512:oh9 + 1024], pbb[:], sem=sDV)
                    res_dv.append(dvm)
                    wait("sync", sDV, dvm)
                    ds_w.append(dma("sync", qS,
                                    AP(d_ds, 128 * i * 1024,
                                       [[1024, 128], [1, 1024]]),
                                    qout[:, oh9:oh9 + 1024]))
                else:
                    xr = dma("sync", qS, xt_f[it % 2][:],
                             x_in[128 * i:128 * i + 128, :], sp=False)
                    oh9 = (it % 2) * 1024
                    dsr = dma("sync", qS, qout[:, oh9:oh9 + 1024],
                              AP(d_ds, 128 * i * 1024,
                                 [[1024, 128], [1, 1024]]))
                    wait("vector", qS, dsr)
                    wait("vector", sPE, dn_pe)
                    E("vector", lambda e, x=xt_f[it % 2], ii=i:
                      e.tensor_tensor(res_f[:], x[:],
                                      attn_tm[:, D * ii:D * ii + D], OP.add))
                    TTo(res_f[:], res_f[:], qout[:, oh9:oh9 + 1024], OP.add)
                    TTo(out_f[it % 2][:, 0:512], res_f[:, 0:512], pba[:],
                        OP.add)
                    dvm = TTo(out_f[it % 2][:, 512:1024], res_f[:, 512:1024],
                              pbb[:], OP.add, sem=sDV)
                    res_dv.append(dvm)
                    wait("sync", sDV, dvm)
                    out_w.append(dma("sync", qS,
                                     out_t[128 * i:128 * i + 128, :],
                                     out_f[it % 2][:]))
    wait("sync", qS, qS.n)
    wait("gpsimd", qS, qS.n)
    gp9.close()
    ga.close()

    with nc.Block() as blk:
        @blk.sync
        def _(e):
            for f in plan["sync"]:
                f(e)

        @blk.tensor
        def _(e):
            for f in plan["tensor"]:
                f(e)

        @blk.vector
        def _(e):
            for f in plan["vector"]:
                f(e)

        @blk.scalar
        def _(e):
            for f in plan["scalar"]:
                f(e)

        @blk.gpsimd
        def _(e):
            for f in plan["gpsimd"]:
                f(e)

    stack0.close()
    return nc


def _consts():
    c = {}
    m32 = 0.5 * np.triu(np.ones((32, 32), np.float32), 1)
    m128 = np.zeros((128, 128), np.float32)
    for m in range(4):
        m128[32 * m:32 * m + 32, 32 * m:32 * m + 32] = m32
    c["c_mask"] = np.tile(m128, (1, 4)).astype(bf16)
    c["c_eye"] = np.eye(128, dtype=np.float32)
    c["c_ones"] = np.ones((1, 128), np.float32).astype(bf16)
    qt = np.zeros((128, 32), np.float32)
    qt[:, 0:16] = 1.0
    qb = np.zeros((128, 32), np.float32)
    qb[:, 16:32] = 1.0
    c["c_qtop0"] = qt
    c["c_qbot0"] = qb
    c["c_lwz"] = np.zeros((32, 8192), np.float32)
    c["c_zb16"] = np.zeros((1, 32768), np.float32).astype(bf16)
    rws = np.zeros((32, 2176), np.float32)
    for gb in range(16):
        for m in range(4):
            for s in range(2):
                col = gb * 136 + 128 + 4 * s + m
                rws[8 * m + 2 + s, col] = 1.0
                rws[8 * m + 6 + s, col] = 1.0
    c["c_rws"] = rws
    return c


def kernel(L0_attn_x, L0_attn_Wq, L0_attn_Wk, L0_attn_Wv, L0_attn_Wo,
           L0_attn_Wb, L0_attn_Wfd, L0_attn_bfd, L0_attn_Wsd, L0_attn_bsd,
           L0_attn_Wg, L0_ffn_w_gate, L0_ffn_w_up, L0_ffn_w_down):
    if "nc" not in _CACHE:
        _CACHE["nc"] = build_program()
        _CACHE["consts"] = _consts()
    nc = _CACHE["nc"]
    consts = _CACHE["consts"]

    def f32(a):
        return np.ascontiguousarray(np.asarray(a, np.float32))

    x = f32(L0_attn_x)
    wqT = f32(np.asarray(L0_attn_Wq).T)
    wkT = f32(np.asarray(L0_attn_Wk).T)
    wvT = f32(np.asarray(L0_attn_Wv).T)
    wgT = f32(np.asarray(L0_attn_Wg).T)
    woT = f32(np.asarray(L0_attn_Wo).T)
    wbfs = np.concatenate([f32(L0_attn_Wb), f32(L0_attn_Wfd),
                           f32(L0_attn_Wsd)], 0)
    wbfsT = np.ascontiguousarray(wbfs.T)
    bias = np.concatenate([np.zeros(16, np.float32),
                           f32(L0_attn_bfd).ravel(),
                           f32(L0_attn_bsd).ravel()]).reshape(1, 48)
    wg_f = f32(L0_ffn_w_gate)
    wu_f = f32(L0_ffn_w_up)
    wd_f = f32(L0_ffn_w_down)

    in_maps = []
    for c in range(8):
        b, half = c // 2, c % 2
        gsh = np.zeros((384, D), np.float32)
        gsh[0:352] = wg_f[352 * c:352 * c + 352]
        ush = np.zeros((384, D), np.float32)
        ush[0:352] = wu_f[352 * c:352 * c + 352]
        dsh = np.ascontiguousarray(wd_f[128 * c:128 * c + 128])
        zm = np.full((16, 1), float(half), np.float32)
        m = {"x_s": np.ascontiguousarray(x[b, 2048 * half:2048 * half + 2048]),
             "wqT": wqT, "wkT": wkT, "wvT": wvT, "wgT": wgT, "woT": woT,
             "wbfsT": wbfsT, "bias_bfs": bias,
             "wg_sh": gsh, "wu_sh": ush, "wd_sh": dsh,
             "c_zmask": zm}
        m.update(consts)
        in_maps.append(m)

    res = run_bass_kernel_spmd(nc, in_maps, core_ids=list(range(8)))
    out = np.zeros((4, 4096, 1024), np.float32)
    for c in range(8):
        b, half = c // 2, c % 2
        out[b, 2048 * half:2048 * half + 2048] = res.results[c]["out"]
    return out


# version 29 (final)
# speedup vs baseline: 1.0240x; 1.0010x over previous
import numpy as np
import ml_dtypes
import contextlib
import concourse.bass as bass
import concourse.mybir as mybir
from concourse.bass_utils import run_bass_kernel_spmd

bf16 = ml_dtypes.bfloat16
F32 = mybir.dt.float32
BF16 = mybir.dt.bfloat16
OP = mybir.AluOpType
AF = mybir.ActivationFunctionType
AX = mybir.AxisListType
AP = bass.AP

T, D, H, HD, I = 2048, 1024, 16, 64, 2816
NT = 16
EPS, CPc = 0.05, 1.05

_CACHE = {}


class Sem:
    def __init__(self, h):
        self.h = h
        self.n = 0
        self.max_wait = 0
        self.drained = 0

    def inc(self, k=1):
        self.n += k
        return self.n


def build_program():
    nc = bass.Bass("TRN2", target_bir_lowering=False)

    def ein(name, shape, dt=F32):
        return nc.dram_tensor(name, shape, dt, kind="ExternalInput").ap()

    x_in = ein("x_s", [T, D])
    wqT_i = ein("wqT", [D, D]); wkT_i = ein("wkT", [D, D])
    wvT_i = ein("wvT", [D, D]); wgT_i = ein("wgT", [D, D])
    woT_i = ein("woT", [D, D])
    wbfsT_i = ein("wbfsT", [D, 48])
    bias_in = ein("bias_bfs", [1, 48])
    wg_sh = ein("wg_sh", [384, D])
    wu_sh = ein("wu_sh", [384, D])
    wd_sh = ein("wd_sh", [128, I])
    c_mask = ein("c_mask", [128, 512], BF16)
    c_eye = ein("c_eye", [128, 128])
    c_ones = ein("c_ones", [1, 128], BF16)
    c_qtop0 = ein("c_qtop0", [128, 32])
    c_qbot0 = ein("c_qbot0", [128, 32])
    c_lwz = ein("c_lwz", [32, 8192])
    c_rws = ein("c_rws", [32, 2176])
    c_zb16 = ein("c_zb16", [1, 32768], BF16)
    c_zmask = ein("c_zmask", [16, 1])
    out_t = nc.dram_tensor("out", [T, D], F32, kind="ExternalOutput").ap()

    def scr(name, shape, dt):
        return nc.dram_tensor(name, shape, dt)

    d_xnb = scr("d_xnb", [T, D], BF16)
    d_xq = scr("d_xq", [T, D], BF16)
    d_u = scr("d_u", [T, D], BF16)
    d_q = scr("d_q", [T, D], BF16)
    d_g = scr("d_g", [T, D], BF16)
    d_fs = scr("d_fs", [48, T], F32)
    d_og = scr("d_og", [T, D], BF16)
    d_hq = scr("d_hq", [T, I], BF16)
    d_ds = scr("d_ds", [T, D], BF16)
    d_tg = scr("d_tg", [2, 32768], F32)
    d_th = scr("d_th", [2, 32768], F32)
    d_tr = scr("d_tr", [2, 32768], F32)
    d_tHfA = scr("d_tHfA", [2, 16384], F32)
    d_trBt = scr("d_trBt", [2, 16384], F32)
    d_trfl = scr("d_trfl", [2, 32768], BF16)
    d_q32 = scr("d_q32", [128, 32], F32)
    d_Braw = scr("d_Braw", [2, 65536], F32)
    d_zc = scr("d_zc", [2, 16 * 65 * 66], BF16)
    d_wqin = scr("d_wqin", [1056, 1024], BF16)
    d_wqout = scr("d_wqout", [8448, 1024], BF16)
    d_zin = scr("d_zin", [32, 64], BF16)
    d_zout = scr("d_zout", [64, 64], BF16)
    d_wgf = scr("d_wgf", [I, D], BF16)
    d_wuf = scr("d_wuf", [I, D], BF16)
    d_wdf = scr("d_wdf", [D, I], BF16)
    d_LWf = scr("d_LWf", [32, 32768], F32)
    d_RWf = scr("d_RWf", [32, 34816], F32)
    d_L2f = scr("d_L2f", [8, 32768], BF16)
    d_R2f = scr("d_R2f", [8, 16384], BF16)

    plan = {k: [] for k in ["sync", "tensor", "vector", "scalar", "gpsimd"]}
    stack0 = contextlib.ExitStack()

    sems = {s: Sem(stack0.enter_context(nc.semaphore(s)))
            for s in ["qS", "qG", "qA", "qV", "sPE", "sDV", "sAC", "sCC"]}
    qS, qG, qA, qV = sems["qS"], sems["qG"], sems["qA"], sems["qV"]
    sPE, sDV, sAC, sCC = sems["sPE"], sems["sDV"], sems["sAC"], sems["sCC"]

    def E(eng, fn):
        if eng == "vector" or eng == "scalar":
            s = sDV if eng == "vector" else sAC
            v = s.n
            plan[eng].append(lambda e, ss=s.h, vv=v: e.wait_ge(ss, vv))
            tgt = s.inc(1)
            plan[eng].append(lambda e, f=fn, ss=s.h: f(e).then_inc(ss, 1))
            return tgt
        plan[eng].append(fn)
        return None

    def dma(eng, q, dst, src, transpose=False, sp=True):
        tgt = q.inc(16)
        if transpose:
            plan[eng].append(
                lambda e, d=dst, s=src: e.dma_start(d, s, transpose=True)
                .then_inc(q.h, 16))
        else:
            plan[eng].append(lambda e, d=dst, s=src: e.dma_start(d, s)
                             .then_inc(q.h, 16))
        if sp:
            plan[eng].append(lambda e, s=q.h, vv=tgt: e.wait_ge(s, vv))
            q.drained = tgt
        return tgt

    def wait(eng, sem, val):
        sem.max_wait = max(sem.max_wait, val)
        plan[eng].append(lambda e, s=sem.h, v=val: e.wait_ge(s, v))

    def op(eng, fn, sem=None):
        if eng == "vector" or eng == "scalar":
            return E(eng, fn)
        if sem is None:
            E(eng, fn)
            return None
        tgt = sem.inc(1)
        plan[eng].append(lambda e, f=fn, s=sem.h: f(e).then_inc(s, 1))
        return tgt

    def mm(psum_ap, lhsT_ap, rhs_ap, start, stop, sem=None, skip=False):
        if sem is None:
            E("tensor", lambda e, o=psum_ap, l=lhsT_ap, r=rhs_ap, st=start,
              sp=stop, sk=skip: e.matmul(o, l, r, start=st, stop=sp,
                                         skip_group_check=sk))
            return None
        tgt = sem.inc(1)
        E("tensor", lambda e, o=psum_ap, l=lhsT_ap, r=rhs_ap, st=start,
          sp=stop, s=sem.h, sk=skip: e.matmul(
              o, l, r, start=st, stop=sp, skip_group_check=sk).then_inc(s, 1))
        return tgt

    def TTo(o, a, b, alu, sem=None):
        return op("vector", lambda e, oo=o, aa=a, bb=b, al=alu:
                  e.tensor_tensor(oo, aa, bb, al), sem)

    def STT(o, a, s, b, op0, op1, sem=None):
        return op("vector", lambda e, oo=o, aa=a, ss=s, bb=b, o0=op0, o1=op1:
                  e.scalar_tensor_tensor(oo, aa, ss, bb, o0, o1), sem)

    def TS(o, a, s1, s2, op0, op1=None, sem=None):
        if op1 is None:
            return op("vector", lambda e, oo=o, aa=a, a1=s1, o0=op0:
                      e.tensor_scalar(oo, aa, a1, None, o0), sem)
        return op("vector", lambda e, oo=o, aa=a, a1=s1, a2=s2, o0=op0,
                  o1=op1: e.tensor_scalar(oo, aa, a1, a2, o0, o1), sem)

    def CP(o, a, sem=None):
        return op("vector", lambda e, oo=o, aa=a: e.tensor_copy(oo, aa), sem)

    def sb(st, name, shape, dt):
        return st.enter_context(nc.sbuf_tensor(name, shape, dt))

    g0 = stack0
    xt_f = [sb(g0, f"xt{i}", [128, D], F32) for i in range(2)]
    xn_f = sb(g0, "xn_f", [128, D], F32)
    kvb = [sb(g0, f"kvb{i}", [128, D], BF16) for i in range(4)]
    res_f = sb(g0, "res_f", [128, D], F32)
    out_f = [sb(g0, f"outf{i}", [128, D], F32) for i in range(2)]
    ds_rd = sb(g0, "ds_rd", [128, D], F32)
    wsh_f = sb(g0, "wsh_f", [128, 1024], F32)
    col_a = sb(g0, "col_a", [128, 1], F32)
    col_b = sb(g0, "col_b", [128, 1], F32)
    col_c = sb(g0, "col_c", [128, 1], F32)
    col_d = sb(g0, "col_d", [128, 1], F32)
    qy = sb(g0, "qy", [128, 1024], F32)
    qv = sb(g0, "qv", [128, 1024], F32)
    qsg = sb(g0, "qsg", [128, 1024], BF16)
    qa_ = sb(g0, "qa_", [128, 1024], BF16)
    qb_ = sb(g0, "qb_", [128, 1024], BF16)
    qrB = sb(g0, "qrB", [128, 1024], BF16)
    qu2 = sb(g0, "qu2", [128, 1024], BF16)
    qz_ = sb(g0, "qz_", [128, 1024], BF16)
    qout = sb(g0, "qout", [128, 2048], BF16)
    amax = sb(g0, "amax", [128, 64], F32)
    scl = sb(g0, "scl", [128, 64], F32)
    rscl = sb(g0, "rscl", [128, 64], F32)
    scl16 = sb(g0, "scl16", [128, 64], BF16)
    mask_s = sb(g0, "mask_s", [128, 512], BF16)
    eye_s = sb(g0, "eye_s", [128, 128], F32)
    ones_s = sb(g0, "ones_s", [1, 128], BF16)
    bias_f = sb(g0, "bias_f", [1, 48], F32)
    bias_s = sb(g0, "bias_s", [1, 48], BF16)
    zmask_s = sb(g0, "zmask_s", [16, 1], F32)
    q32ch = sb(g0, "q32ch", [16, 256], F32)

    PS = [stack0.enter_context(nc.psum_tensor(f"PS{i}", [128, 512], F32))
          for i in range(8)]

    dma("sync", qS, eye_s[:], c_eye, sp=False)
    dma("sync", qS, mask_s[:], c_mask, sp=False)
    dma("sync", qS, ones_s[:], c_ones, sp=False)
    dma("sync", qS, bias_f[:], bias_in, sp=False)
    dma("sync", qS, zmask_s[:], c_zmask)
    t_consts0 = qS.n
    wait("sync", qS, t_consts0)
    xload = [dma("sync", qS, xt_f[0][:], x_in[0:128, :])]

    quant_state = {}

    def quant_tile(xin_ap, w, oq):
        nblk = w // 16
        x3 = xin_ap.rearrange("p (n k) -> p n k", k=16)
        am, sc_ = amax[:, 0:nblk], scl[:, 0:nblk]
        rs_, s16 = rscl[:, 0:nblk], scl16[:, 0:nblk]
        E("vector", lambda e, a=x3, b=am: e.tensor_reduce(
            b, a, AX.X, OP.max, apply_absolute_value=True))
        TS(sc_, am, 1.0 / 6.0, 1e-12, OP.mult, OP.max)
        E("vector", lambda e, a=sc_, b=rs_: e.reciprocal(b, a))
        rs_b = AP(rscl, 0, [[64, 128], [1, nblk], [0, 16]])
        sc_b = AP(scl, 0, [[64, 128], [1, nblk], [0, 16]])
        y3 = qy[:, 0:w].rearrange("p (n k) -> p n k", k=16)
        dvm = TTo(y3, x3, rs_b, OP.mult, sem=sDV)
        quant_state["y_mark"] = dvm
        yw, vw, sg = qy[:, 0:w], qv[:, 0:w], qsg[:, 0:w]
        wait("scalar", sDV, dvm)
        absm = op("scalar", lambda e, a=yw, o=vw: e.activation(o, a, AF.Abs),
                  sAC)
        acm = op("scalar", lambda e, a=yw, o=sg: e.activation(o, a, AF.Sign),
                 sAC)
        wait("vector", sAC, absm)
        aw, bw, zw = qa_[:, 0:w], qb_[:, 0:w], qz_[:, 0:w]
        TS(aw, vw, 96.0, None, OP.add)
        TS(bw, aw, -96.0, 2.0, OP.add, OP.min)
        STT(aw, vw, 2.5, bw, OP.is_gt, OP.add)
        STT(bw, vw, 3.5, aw, OP.is_gt, OP.add)
        STT(aw, vw, 5.0, bw, OP.is_gt, OP.add)
        STT(bw, vw, 5.0, aw, OP.is_gt, OP.add)
        wait("vector", sAC, acm)
        TTo(zw, bw, sg, OP.mult)
        z3 = qz_[:, 0:w].rearrange("p (n k) -> p n k", k=16)
        qo3 = qout[:, oq:oq + w].rearrange("p (n k) -> p n k", k=16)
        return TTo(qo3, z3, sc_b, OP.mult, sem=sDV)

    prev_qm = None
    wq_w = []
    shi = 0
    jobs = []
    for ti in range(3):
        jobs.append((wg_sh[128 * ti:128 * ti + 128, :], 128 * ti, 1024,
                     96 if ti == 2 else 128))
    for ti in range(3):
        jobs.append((wu_sh[128 * ti:128 * ti + 128, :], 352 + 128 * ti, 1024,
                     96 if ti == 2 else 128))
    for cc3 in range(3):
        cw3 = 768 if cc3 == 2 else 1024
        jobs.append((wd_sh[:, 1024 * cc3:1024 * cc3 + cw3],
                     ("wd", cc3, cw3), cw3, 128))
    for (srcap, dsti, wc, nrow) in jobs:
        if prev_qm is not None:
            wait("scalar", sDV, quant_state["y_mark"])
        t = dma("scalar", qA, wsh_f[:, 0:wc], srcap)
        wait("vector", qA, t)
        oq = (shi % 2) * 1024
        if shi >= 2:
            wait("vector", qS, wq_w[shi - 2])
        qm = quant_tile(wsh_f[:, 0:wc], wc, oq)
        prev_qm = qm
        wait("sync", sDV, qm)
        if isinstance(dsti, tuple):
            _, cc3, cw3 = dsti
            wq_w.append(dma("sync", qS,
                            AP(d_wqin, 704 * 1024 + 1024 * cc3,
                               [[2816, 128], [1, cw3]]),
                            qout[:, oq:oq + cw3]))
        else:
            wq_w.append(dma("sync", qS,
                            AP(d_wqin, dsti * 1024, [[1024, nrow], [1, 1024]]),
                            qout[0:nrow, oq:oq + 1024]))
        shi += 1
    t_wqin = qS.n

    gp = stack0.enter_context(contextlib.ExitStack())
    xn_fm = sb(gp, "xn_fm", [128, 8 * T], BF16)
    wmovA = sb(gp, "wmovA", [128, 8 * D], BF16)
    wmovB = sb(gp, "wmovB", [128, 8 * D], BF16)
    wbfs_s = sb(gp, "wbfs_s", [128, 8 * 48], BF16)
    bfs_tm = sb(gp, "bfs_tm", [128, NT * 48], F32)
    fs_st = sb(gp, "fs_st", [48, T], F32)
    wtmp = [sb(gp, f"wtmp{i}", [128, D], F32) for i in range(4)]
    ustg = sb(gp, "ustg", [128, 2048], BF16)
    qstg = sb(gp, "qstg", [128, 2048], BF16)
    gstg = sb(gp, "gstg", [128, 2048], BF16)
    f_sc = sb(gp, "f_sc", [128, 256], F32)
    s_sc = sb(gp, "s_sc", [128, 256], F32)
    QTop = sb(gp, "QTop", [128, 544], F32)
    QBot = sb(gp, "QBot", [128, 544], F32)
    SC0 = sb(gp, "SC0", [128, 544], F32)
    SC1 = sb(gp, "SC1", [128, 544], F32)
    t_g = sb(gp, "t_g", [128, 512], F32)
    t_h = sb(gp, "t_h", [128, 512], F32)
    t_r = sb(gp, "t_r", [128, 512], F32)
    t_HfA = sb(gp, "t_HfA", [128, 256], F32)
    t_rBt = sb(gp, "t_rBt", [128, 256], F32)
    t_rfl = sb(gp, "t_rfl", [128, 512], F32)
    t_det = sb(gp, "t_det", [128, 256], F32)
    t_rdet = sb(gp, "t_rdet", [128, 256], F32)
    t_w1 = sb(gp, "t_w1", [128, 544], F32)
    t_w2 = sb(gp, "t_w2", [128, 544], F32)
    t_bf = sb(gp, "t_bf", [128, 512], BF16)
    q32 = sb(gp, "q32", [128, 32], F32)

    def qslot(tsr, j):
        return AP(tsr, j, [[544, 128], [272, 2], [17, 16]])

    def qhalf(tsr, j, pair):
        return AP(tsr, j + 272 * pair, [[544, 128], [17, 16]])

    E("vector", lambda e: e.memset(qhalf(QTop, 0, 0), 1.0))
    E("vector", lambda e: e.memset(qhalf(QTop, 0, 1), 0.0))
    E("vector", lambda e: e.memset(qhalf(QBot, 0, 0), 0.0))
    E("vector", lambda e: e.memset(qhalf(QBot, 0, 1), 1.0))
    E("vector", lambda e: e.memset(qhalf(SC0, 16, 0), 1.0))
    E("vector", lambda e: e.memset(qhalf(SC0, 16, 1), 0.0))
    E("vector", lambda e: e.memset(qhalf(SC1, 16, 0), 0.0))
    E("vector", lambda e: e.memset(qhalf(SC1, 16, 1), 1.0))

    wjobs = []
    for src_, dst, ncol in [(wkT_i, wmovA, D), (wvT_i, wmovB, D),
                            (wbfsT_i, wbfs_s, 48)]:
        for b in range(8):
            wjobs.append((src_, dst, ncol, b))
    for src_, dst, ncol in [(wqT_i, wmovA, D), (wgT_i, wmovB, D)]:
        for b in range(8):
            wjobs.append((src_, dst, ncol, b))
    NW = len(wjobs)
    wload_t, wcast_t = [], []

    def emit_wjob(j):
        src_, dst, ncol, b = wjobs[j]
        if j >= 4:
            wait("gpsimd", sDV, wcast_t[j - 4])
        if ncol == D:
            t = dma("gpsimd", qG, wtmp[j % 4][:],
                    src_[128 * b:128 * b + 128, :])
        else:
            t = dma("gpsimd", qG, wtmp[j % 4][:, 0:48],
                    src_[128 * b:128 * b + 128, :])
        wload_t.append(t)
        wait("vector", qG, max(wload_t[min(3, j)], t))
        if ncol == D:
            wcast_t.append(CP(dst[:, D * b:D * b + D], wtmp[j % 4][:],
                              sem=sDV))
        else:
            wcast_t.append(CP(dst[:, 48 * b:48 * b + 48],
                              wtmp[j % 4][:, 0:48], sem=sDV))

    for j in range(24):
        emit_wjob(j)
    wait("vector", qS, t_consts0)
    bias_cast = CP(bias_s[:], bias_f[:])
    wait("gpsimd", qS, t_wqin)
    cc1 = sCC.inc(1)
    E("gpsimd", lambda e: e.collective_compute(
        "AllGather", OP.bypass, replica_groups=[[0, 1, 2, 3, 4, 5, 6, 7]],
        ins=[d_wqin.ap().opt()], outs=[d_wqout.ap().opt()]).then_inc(sCC.h, 1))
    wait("gpsimd", sCC, cc1)
    for r in range(8):
        dma("gpsimd", qG, AP(d_wgf, 352 * 1024 * r, [[1024, 352], [1, 1024]]),
            AP(d_wqout, 1056 * 1024 * r, [[1024, 352], [1, 1024]]), sp=False)
        dma("gpsimd", qG, AP(d_wuf, 352 * 1024 * r, [[1024, 352], [1, 1024]]),
            AP(d_wqout, (1056 * r + 352) * 1024, [[1024, 352], [1, 1024]]),
            sp=False)
        dma("gpsimd", qG, AP(d_wdf, 128 * 2816 * r, [[2816, 128], [1, 2816]]),
            AP(d_wqout, (1056 * r + 704) * 1024, [[2816, 128], [1, 2816]]),
            sp=(r == 7))
    t_repack = qG.n

    ph1_marks = []
    xnb_w, xq_w = [], []
    for i in range(NT + 1):
        if i < NT:
            xt = xt_f[i % 2]
            xbuf = xn_f if i % 2 == 0 else ds_rd
            cb = col_b if i % 2 == 0 else col_d
            wait("vector", qS, xload[i])
            E("vector", lambda e, a=xt: e.tensor_reduce(
                col_a[:], a[:], AX.X, OP.add))
            TS(col_a[:], col_a[:], -1.0 / D, None, OP.mult)
            dv_mean = TS(xbuf[:], xt[:], col_a[:], None, OP.add, sem=sDV)
            wait("scalar", sDV, dv_mean)
            ac_sq = op("scalar", lambda e, xb=xbuf, c=cb: e.activation(
                qrB[:], xb[:], AF.Square, accum_out=c[:]), sAC)
            ph1_marks.append((dv_mean, ac_sq))
            if i + 1 < NT:
                if i >= 1:
                    wait("sync", sDV, ph1_marks[i - 1][0])
                xload.append(dma("sync", qS, xt_f[(i + 1) % 2][:],
                                 x_in[128 * (i + 1):128 * (i + 2), :]))
        if i >= 1:
            j = i - 1
            xbufj = xn_f if j % 2 == 0 else ds_rd
            cbj = col_b if j % 2 == 0 else col_d
            wait("vector", sAC, ph1_marks[j][1])
            dvm = TS(cbj[:], cbj[:], 1.0 / D, 1e-5, OP.mult, OP.add,
                     sem=sDV)
            wait("scalar", sDV, dvm)
            E("scalar", lambda e, c=cbj: e.activation(col_c[:], c[:], AF.Ln))
            acm = op("scalar", lambda e: e.activation(
                col_c[:], col_c[:], AF.Exp, scale=-0.5), sAC)
            wait("vector", sAC, acm)
            TS(xbufj[:], xbufj[:], col_c[:], None, OP.mult)
            oh = (j % 2) * 1024
            if j >= 1:
                wait("vector", qS, xnb_w[j - 1])
            dvm = CP(qu2[:], xbufj[:], sem=sDV)
            wait("sync", sDV, dvm)
            xnb_w.append(dma("sync", qS,
                             d_xnb.ap()[128 * j:128 * j + 128, :], qu2[:]))
            if j >= 2:
                wait("vector", qS, xq_w[j - 2])
            else:
                wait("vector", qS, wq_w[7 + j])
            qm = quant_tile(xbufj[:], D, oh)
            wait("sync", sDV, qm)
            xq_w.append(dma("sync", qS,
                            d_xq.ap()[128 * j:128 * j + 128, :],
                            qout[:, oh:oh + 1024]))
    ph1_qs = qS.n



    wait("sync", qS, ph1_qs)
    for b in range(8):
        dma("sync", qS, xn_fm[:, T * b:T * b + T],
            AP(d_xnb, 128 * b, [[1024, 2048], [1, 128]]), transpose=True,
            sp=True)
    t_xnfm = qS.n

    wait("tensor", qS, t_xnfm)
    wait("tensor", sDV, bias_cast)
    bfs_ac = []
    for i in range(NT):
        pb_ = PS[4] if i % 2 == 0 else PS[5]
        if i >= 2:
            wait("tensor", sAC, bfs_ac[i - 2])
        for b in range(8):
            lhs = xn_fm[:, T * b + 128 * i: T * b + 128 * i + 128]
            mm(pb_[:, 0:48], lhs, wbfs_s[:, 48 * b:48 * b + 48],
               b == 0, False, skip=True)
        bfs_pe = mm(pb_[:, 0:48], ones_s[:], bias_s[:], False,
                    True, sem=sPE, skip=True)
        wait("scalar", sPE, bfs_pe)
        bac = op("scalar", lambda e, ii=i, pp=pb_: e.activation(
            bfs_tm[:, 48 * ii:48 * ii + 48], pp[:, 0:48], AF.Tanh,
            scale=0.5), sAC)
        bfs_ac.append(bac)
    wait("vector", sAC, bfs_ac[-1])
    bdv = TS(bfs_tm[:], bfs_tm[:], 0.5, 0.5, OP.mult, OP.add, sem=sDV)
    fs_dv = []
    wait("tensor", sDV, bdv)
    for i in range(NT):
        pt_ = PS[6] if i % 2 == 0 else PS[7]
        if i >= 2:
            wait("tensor", sDV, fs_dv[i - 2])
        trm = op("tensor", lambda e, ii=i, pp=pt_: e.transpose(
            pp[0:48, 0:128], bfs_tm[:, 48 * ii:48 * ii + 48],
            eye_s[:]), sPE)
        wait("vector", sPE, trm)
        dvm = CP(fs_st[:, 128 * i:128 * i + 128], pt_[0:48, 0:128],
                 sem=sDV)
        fs_dv.append(dvm)
    wait("sync", sDV, fs_dv[-1])
    dma("sync", qS, d_fs.ap()[:, :], fs_st[:])
    wait("sync", qS, qS.n)
    for hh in range(16):
        dma("sync", qS, f_sc[8 * hh:8 * hh + 8, :],
            AP(d_fs, (16 + hh) * 2048, [[256, 8], [1, 256]]), sp=False)
        dma("sync", qS, s_sc[8 * hh:8 * hh + 8, :],
            AP(d_fs, (32 + hh) * 2048, [[256, 8], [1, 256]]), sp=(hh == 15))
    t_fsc = qS.n
    wait("tensor", sAC, bfs_ac[-1])
    wait("tensor", sDV, fs_dv[-1])
    u_w, u_dv, kv_ac = [], [], []
    for i in range(NT):
        pk = (PS[0], PS[1]) if i % 2 == 0 else (PS[4], PS[5])
        pv = (PS[2], PS[3]) if i % 2 == 0 else (PS[6], PS[7])
        if i >= 2:
            wait("tensor", sAC, kv_ac[i - 2])
        for b in range(8):
            lhs = xn_fm[:, T * b + 128 * i: T * b + 128 * i + 128]
            mm(pk[0][:], lhs, wmovA[:, D * b:D * b + 512], b == 0, b == 7)
            k_pe = mm(pk[1][:], lhs, wmovA[:, D * b + 512:D * b + 1024],
                      b == 0, b == 7, sem=sPE if b == 7 else None)
        for b in range(8):
            lhs = xn_fm[:, T * b + 128 * i: T * b + 128 * i + 128]
            mm(pv[0][:], lhs, wmovB[:, D * b:D * b + 512], b == 0, b == 7)
            v_pe = mm(pv[1][:], lhs, wmovB[:, D * b + 512:D * b + 1024],
                      b == 0, b == 7, sem=sPE if b == 7 else None)
        kb_, vb_ = kvb[i % 2], kvb[2 + i % 2]
        if i >= 2:
            wait("scalar", sDV, u_dv[i - 2])
        wait("scalar", sPE, k_pe)
        E("scalar", lambda e, kk=kb_, p0=pk[0]: e.activation(
            kk[:, 0:512], p0[:], AF.Silu))
        E("scalar", lambda e, kk=kb_, p1=pk[1]: e.activation(
            kk[:, 512:1024], p1[:], AF.Silu))
        wait("scalar", sPE, v_pe)
        E("scalar", lambda e, vv=vb_, p0=pv[0]: e.activation(
            vv[:, 0:512], p0[:], AF.Silu))
        vac = op("scalar", lambda e, vv=vb_, p1=pv[1]: e.activation(
            vv[:, 512:1024], p1[:], AF.Silu), sAC)
        kv_ac.append(vac)
        wait("vector", sAC, vac)
        if i >= 2:
            wait("vector", qS, u_w[i - 2])
        bbc = AP(bfs_tm, 48 * i, [[NT * 48, 128], [1, 16], [0, 64]])
        k3 = kb_[:].rearrange("p (h d) -> p h d", d=64)
        v3 = vb_[:].rearrange("p (h d) -> p h d", d=64)
        oh = (i % 2) * 1024
        u3 = ustg[:, oh:oh + 1024].rearrange("p (h d) -> p h d", d=64)
        y3d = qy[:].rearrange("p (h d) -> p h d", d=64)
        TTo(y3d, k3, bbc, OP.mult)
        udv = TTo(u3, y3d, v3, OP.mult, sem=sDV)
        u_dv.append(udv)
        wait("sync", sDV, udv)
        u_w.append(dma("sync", qS, d_u.ap()[128 * i:128 * i + 128, :],
                       ustg[:, oh:oh + 1024]))
    wait("vector", sPE, sPE.n)
    for j in range(24, NW):
        emit_wjob(j)
    wait("tensor", sDV, wcast_t[-1])
    wait("tensor", sAC, kv_ac[-1])
    wait("scalar", sDV, u_dv[-1])
    q_w, g_w = [], []
    qg_ac, qg_dv = [], []
    for i in range(NT):
        pq = (PS[0], PS[1]) if i % 2 == 0 else (PS[4], PS[5])
        pg = (PS[2], PS[3]) if i % 2 == 0 else (PS[6], PS[7])
        if i >= 2:
            wait("tensor", sAC, qg_ac[i - 2])
        for b in range(8):
            lhs = xn_fm[:, T * b + 128 * i: T * b + 128 * i + 128]
            mm(pq[0][:], lhs, wmovA[:, D * b:D * b + 512], b == 0, b == 7)
            q_pe = mm(pq[1][:], lhs, wmovA[:, D * b + 512:D * b + 1024],
                      b == 0, b == 7, sem=sPE if b == 7 else None)
        for b in range(8):
            lhs = xn_fm[:, T * b + 128 * i: T * b + 128 * i + 128]
            mm(pg[0][:], lhs, wmovB[:, D * b:D * b + 512], b == 0, b == 7)
            g_pe = mm(pg[1][:], lhs, wmovB[:, D * b + 512:D * b + 1024],
                      b == 0, b == 7, sem=sPE if b == 7 else None)
        oh = (i % 2) * 1024
        gscr = kvb[i % 2]
        if i >= 2:
            wait("scalar", qS, q_w[i - 2])
            wait("scalar", sDV, qg_dv[i - 2])
        wait("scalar", sPE, q_pe)
        E("scalar", lambda e, o=oh, p0=pq[0]: e.activation(
            qstg[:, o:o + 512], p0[:], AF.Silu))
        E("scalar", lambda e, o=oh, p1=pq[1]: e.activation(
            qstg[:, o + 512:o + 1024], p1[:], AF.Silu))
        wait("scalar", sPE, g_pe)
        E("scalar", lambda e, gg=gscr, p0=pg[0]: e.activation(
            gg[:, 0:512], p0[:], AF.Tanh, scale=0.5))
        gac = op("scalar", lambda e, gg=gscr, p1=pg[1]: e.activation(
            gg[:, 512:1024], p1[:], AF.Tanh, scale=0.5), sAC)
        qg_ac.append(gac)
        wait("vector", sAC, gac)
        if i >= 2:
            wait("vector", qS, g_w[i - 2])
        gdv = TS(gstg[:, oh:oh + 1024], gscr[:], 0.5, 0.5,
                 OP.mult, OP.add, sem=sDV)
        qg_dv.append(gdv)
        wait("sync", sAC, gac)
        q_w.append(dma("sync", qS, d_q.ap()[128 * i:128 * i + 128, :],
                       qstg[:, oh:oh + 1024]))
        wait("sync", sDV, gdv)
        g_w.append(dma("sync", qS, d_g.ap()[128 * i:128 * i + 128, :],
                       gstg[:, oh:oh + 1024]))
    t_passB = sPE.n

    wait("vector", qS, t_fsc)

    def fsl(t2, j):
        return AP(t2, j, [[256, 128], [0, 2], [16, 16]])

    tmpA = AP(t_w1, 0, [[544, 128], [272, 2], [17, 16]])
    tmpB = AP(t_w2, 0, [[544, 128], [272, 2], [17, 16]])
    for j in range(1, 17):
        TTo(tmpA, fsl(f_sc, j - 1), qslot(QTop, j - 1), OP.mult)
        TTo(tmpB, fsl(s_sc, j - 1), qslot(QBot, j - 1), OP.mult)
        STT(qslot(QTop, j), tmpB, EPS, tmpA, OP.mult, OP.add)
        STT(qslot(QBot, j), tmpA, EPS, tmpB, OP.mult, OP.add)
    for jj in range(16):
        i2 = 15 - jj
        STT(tmpA, qslot(SC1, i2 + 1), EPS, qslot(SC0, i2 + 1),
            OP.mult, OP.add)
        STT(tmpB, qslot(SC0, i2 + 1), EPS, qslot(SC1, i2 + 1),
            OP.mult, OP.add)
        TTo(qslot(SC0, i2), fsl(f_sc, i2), tmpA, OP.mult)
        TTo(qslot(SC1, i2), fsl(s_sc, i2), tmpB, OP.mult)

    def slots16(tsr, j0):
        return AP(tsr, j0, [[544, 128], [272, 2], [17, 16], [1, 16]])

    th_ap = AP(t_h, 0, [[512, 128], [256, 2], [16, 16], [1, 16]])
    TTo(th_ap, slots16(SC0, 1), slots16(SC1, 1), OP.add)
    TS(th_ap, th_ap, CPc, None, OP.mult)

    def pp0(tsr, j0):
        return AP(tsr, j0, [[544, 128], [17, 16], [1, 16]])

    def pp1(tsr, j0):
        return AP(tsr, j0 + 272, [[544, 128], [17, 16], [1, 16]])

    tdet_ap = AP(t_det, 0, [[256, 128], [16, 16], [1, 16]])
    trd_ap = AP(t_rdet, 0, [[256, 128], [16, 16], [1, 16]])
    w1s = AP(t_w1, 0, [[544, 128], [16, 16], [1, 16]])
    w2s = AP(t_w2, 0, [[544, 128], [16, 16], [1, 16]])
    TTo(w1s, pp0(QTop, 1), pp1(QBot, 1), OP.mult)
    TTo(w2s, pp1(QTop, 1), pp0(QBot, 1), OP.mult)
    TTo(tdet_ap, w1s, w2s, OP.subtract)
    E("vector", lambda e: e.reciprocal(trd_ap, tdet_ap))
    tg_F = AP(t_g, 0, [[512, 128], [16, 16], [1, 16]])
    tg_S = AP(t_g, 256, [[512, 128], [16, 16], [1, 16]])
    TTo(w1s, pp1(QBot, 1), pp1(QTop, 1), OP.subtract)
    TTo(w1s, w1s, trd_ap, OP.mult)
    TS(tg_F, w1s, CPc, None, OP.mult)
    TTo(w2s, pp0(QTop, 1), pp0(QBot, 1), OP.subtract)
    TTo(w2s, w2s, trd_ap, OP.mult)
    TS(tg_S, w2s, CPc, None, OP.mult)
    f16a = AP(f_sc, 0, [[256, 128], [0, 2], [16, 16], [1, 16]])
    s16a = AP(s_sc, 0, [[256, 128], [0, 2], [16, 16], [1, 16]])
    tr_ap = AP(t_r, 0, [[512, 128], [256, 2], [16, 16], [1, 16]])
    w14 = AP(t_w1, 0, [[544, 128], [272, 2], [17, 16], [1, 16]])
    w24 = AP(t_w2, 0, [[544, 128], [272, 2], [17, 16], [1, 16]])
    TTo(w14, f16a, slots16(QTop, 0), OP.mult)
    TTo(w24, s16a, slots16(QBot, 0), OP.mult)
    TTo(tr_ap, AP(t_w1, 0, [[544, 128], [272, 2], [17, 16], [1, 16]]),
        AP(t_w2, 0, [[544, 128], [272, 2], [17, 16], [1, 16]]), OP.add)
    qb_e, qa_e = {}, {}
    for (nm, tsr, offp) in [("00", QTop, 0), ("01", QTop, 272),
                            ("10", QBot, 0), ("11", QBot, 272)]:
        qb_e[nm] = AP(tsr, offp + 17 + 16, [[544, 128], [34, 8]])
        qa_e[nm] = AP(tsr, offp + 16, [[544, 128], [34, 8]])

    def bc16(a2):
        return AP(a2.tensor, a2.offset, [a2.ap[0], a2.ap[1], [0, 16]])

    hA_F = AP(t_h, 0, [[512, 128], [32, 8], [1, 16]])
    hA_S = AP(t_h, 256, [[512, 128], [32, 8], [1, 16]])
    Hf_F = AP(t_HfA, 0, [[256, 128], [16, 8], [1, 16]])
    Hf_S = AP(t_HfA, 128, [[256, 128], [16, 8], [1, 16]])
    w1h = AP(t_w1, 0, [[544, 128], [16, 8], [1, 16]])
    w2h = AP(t_w2, 0, [[544, 128], [16, 8], [1, 16]])
    TTo(w1h, bc16(qb_e["00"]), hA_F, OP.mult)
    TTo(w2h, bc16(qb_e["01"]), hA_S, OP.mult)
    TTo(Hf_F, w1h, w2h, OP.add)
    TTo(w1h, bc16(qb_e["10"]), hA_F, OP.mult)
    TTo(w2h, bc16(qb_e["11"]), hA_S, OP.mult)
    TTo(Hf_S, w1h, w2h, OP.add)
    rdB = AP(t_rdet, 31, [[256, 128], [32, 8]])
    rB_F = AP(t_r, 16, [[512, 128], [32, 8], [1, 16]])
    rB_S = AP(t_r, 256 + 16, [[512, 128], [32, 8], [1, 16]])
    rt_F = AP(t_rBt, 0, [[256, 128], [16, 8], [1, 16]])
    rt_S = AP(t_rBt, 128, [[256, 128], [16, 8], [1, 16]])
    TTo(w1h, bc16(qb_e["11"]), rB_F, OP.mult)
    TTo(w2h, bc16(qb_e["10"]), rB_S, OP.mult)
    TTo(w1h, w1h, w2h, OP.subtract)
    TTo(rt_F, w1h, bc16(rdB), OP.mult)
    TTo(w1h, bc16(qb_e["00"]), rB_S, OP.mult)
    TTo(w2h, bc16(qb_e["01"]), rB_F, OP.mult)
    TTo(w1h, w1h, w2h, OP.subtract)
    TTo(rt_S, w1h, bc16(rdB), OP.mult)
    rA_F = AP(t_r, 0, [[512, 128], [32, 8], [1, 16]])
    rA_S = AP(t_r, 256, [[512, 128], [32, 8], [1, 16]])
    rf_FA = AP(t_rfl, 0, [[512, 128], [32, 8], [1, 16]])
    rf_SA = AP(t_rfl, 256, [[512, 128], [32, 8], [1, 16]])
    rf_FB = AP(t_rfl, 16, [[512, 128], [32, 8], [1, 16]])
    rf_SB = AP(t_rfl, 256 + 16, [[512, 128], [32, 8], [1, 16]])
    TS(rf_FA, rA_F, 0.5, None, OP.mult)
    TS(rf_SA, rA_S, 0.5, None, OP.mult)
    TTo(w1h, bc16(qa_e["00"]), rB_F, OP.mult)
    TTo(w2h, bc16(qa_e["10"]), rB_S, OP.mult)
    TTo(w1h, w1h, w2h, OP.add)
    TS(rf_FB, w1h, 0.5, None, OP.mult)
    TTo(w1h, bc16(qa_e["01"]), rB_F, OP.mult)
    TTo(w2h, bc16(qa_e["11"]), rB_S, OP.mult)
    TTo(w1h, w1h, w2h, OP.add)
    TS(rf_SB, w1h, 0.5, None, OP.mult)
    for k, (aa, bb2, cc, dd) in enumerate([("00", "00", "01", "10"),
                                           ("00", "01", "01", "11"),
                                           ("10", "00", "11", "10"),
                                           ("10", "01", "11", "11")]):
        TTo(t_w1[:, 0:8], qb_e[aa], qa_e[bb2], OP.mult)
        TTo(t_w2[:, 0:8], qb_e[cc], qa_e[dd], OP.mult)
        TTo(q32[:, 8 * k:8 * k + 8], t_w1[:, 0:8], t_w2[:, 0:8], OP.add)
    dump_jobs = [(d_tg, t_g, 512), (d_th, t_h, 512), (d_tr, t_r, 512),
                 (d_tHfA, t_HfA, 256), (d_trBt, t_rBt, 256)]
    wait("sync", sDV, sDV.n)
    for dstt, srct, wdt in dump_jobs:
        hw = wdt // 2
        for comp in range(2):
            dma("sync", qS,
                AP(dstt, comp * 128 * hw, [[hw, 128], [1, hw]]),
                srct[:, comp * hw:comp * hw + hw], sp=False)
    dvm = CP(t_bf[:, 0:512], t_rfl[:, 0:512], sem=sDV)
    wait("sync", sDV, dvm)
    for comp in range(2):
        t_trfl_dump = dma("sync", qS,
                          AP(d_trfl, comp * 128 * 256,
                             [[256, 128], [1, 256]]),
                          t_bf[:, comp * 256:comp * 256 + 256])
    t_q32w = dma("sync", qS, d_q32.ap()[:, :], q32[:])
    wait("sync", qS, qS.n)
    for ent in range(4):
        tq32 = dma("sync", qS, q32ch[:, 64 * ent:64 * ent + 64],
                   AP(d_q32, 8 * ent, [[256, 16], [32, 8], [1, 8]]),
                   sp=(ent == 3))
    t_dumps = qS.n

    for b4 in range(4):
        dma("sync", qS, AP(d_LWf, b4 * 8192, [[32768, 32], [1, 8192]]),
            c_lwz, sp=False)
    for b16 in range(16):
        dma("sync", qS, AP(d_RWf, b16 * 2176, [[34816, 32], [1, 2176]]),
            c_rws, sp=False)
    for r in range(8):
        dma("sync", qS, AP(d_L2f, r * 32768, [[1, 32768]]),
            AP(c_zb16.tensor, 0, [[1, 32768]]), sp=(r == 7))
    lwk = [("gA", d_tg, 0, 0), ("gA", d_tg, 1, 0),
           ("HfA", d_tHfA, 0, 0), ("HfA", d_tHfA, 1, 0),
           ("gB", d_tg, 0, 16), ("gB", d_tg, 1, 16),
           ("hB", d_th, 0, 16), ("hB", d_th, 1, 16)]
    for m in range(4):
        for k in range(8):
            nm, srcd, comp, off = lwk[k]
            if srcd is d_tHfA:
                soff = comp * 16384 + m * 16
                gst, hst = 64, 1024
            else:
                soff = comp * 32768 + m * 32 + (16 if off == 16 else 0)
                gst, hst = 128, 2048
            dst = AP(d_LWf, (8 * m + k) * 32768 + 32 * m + off,
                     [[2048, 16], [128, 16], [1, 16]])
            src2 = AP(srcd, soff, [[gst, 16], [hst, 16], [1, 16]])
            dma("sync", qS, dst, src2, sp=False)
    rwk = [("rA", d_tr, 0, 0), ("rA", d_tr, 1, 0),
           ("rBt", d_trBt, 0, 16), ("rBt", d_trBt, 1, 16),
           ("rB", d_tr, 0, 16), ("rB", d_tr, 1, 16)]
    for m in range(4):
        for k in range(6):
            nm, srcd, comp, off = rwk[k]
            if srcd is d_trBt:
                soff = comp * 16384 + m * 16
                gst, hst = 64, 1024
            else:
                soff = comp * 32768 + m * 32 + (16 if nm == "rB" else 0)
                gst, hst = 128, 2048
            dst = AP(d_RWf, (8 * m + k) * 34816 + 32 * m + off,
                     [[2176, 16], [136, 16], [1, 16]])
            src2 = AP(srcd, soff, [[gst, 16], [hst, 16], [1, 16]])
            dma("sync", qS, dst, src2, sp=False)
    for m in range(4):
        for s in range(2):
            dst = AP(d_L2f, (2 * m + s) * 32768 + 32 * m,
                     [[2048, 16], [128, 16], [1, 32]])
            src2 = AP(d_trfl, s * 32768 + m * 32,
                      [[128, 16], [2048, 16], [1, 32]])
            dma("sync", qS, dst, src2, sp=(m == 3 and s == 1))
    t_asm = qS.n

    gp.close()

    gw = stack0.enter_context(contextlib.ExitStack())
    LW = sb(gw, "LW", [32, 2048], F32)
    RW = sb(gw, "RW", [32, 2176], F32)
    L2 = sb(gw, "L2", [8, 2048], BF16)
    R2 = sb(gw, "R2", [8, 1024], BF16)
    Hf8 = sb(gw, "Hf8", [128, 64], BF16)
    Wt_t = [sb(gw, f"Wt{i2}", [128, 512], BF16) for i2 in range(2)]
    Bst = sb(gw, "Bst", [8, 1024], F32)
    ot_f = sb(gw, "ot_f", [128, 512], F32)
    ogt = [sb(gw, f"ogt{i2}", [128, D], BF16) for i2 in range(2)]
    urd = [sb(gw, f"urd{i2}", [128, D], BF16) for i2 in range(2)]
    qrd = sb(gw, "qrd", [128, D], BF16)
    grd = sb(gw, "grd", [128, D], BF16)
    zhF = sb(gw, "zhF", [16, 65 * 66], BF16)
    zhS = sb(gw, "zhS", [16, 65 * 66], BF16)

    zgF = sb(gw, "zgF", [16, 64], BF16)
    zgS = sb(gw, "zgS", [16, 64], BF16)
    zi_F = sb(gw, "zi_F", [16, 64], F32)
    zi_S = sb(gw, "zi_S", [16, 64], F32)
    ztmp = sb(gw, "ztmp", [16, 1040], BF16)
    cht = [sb(gw, f"cht{i2}", [16, 66], F32) for i2 in range(2)]
    xq_fm = sb(gw, "xq_fm", [128, 8 * T], BF16)
    wgc = sb(gw, "wgc", [128, 8 * 1024], BF16)
    wuc = sb(gw, "wuc", [128, 8 * 1024], BF16)

    for b in range(8):
        dma("sync", qS, xq_fm[:, T * b:T * b + T],
            AP(d_xq, 128 * b, [[1024, 2048], [1, 128]]), transpose=True,
            sp=True)
    t_xqfm = qS.n

    BcrF = [sb(gw, f"BcrF{i2}", [16, 528], F32) for i2 in range(2)]
    BcrS = [sb(gw, f"BcrS{i2}", [16, 528], F32) for i2 in range(2)]
    bcr_t = [None, None]

    def bcr_load(kb, buf):
        wait("scalar", qS, braw_w[2 * kb + 1])
        dma("scalar", qA, AP(BcrF[buf], 0, [[528, 16], [66, 8], [1, 64]]),
            AP(d_Braw, 8 * kb * 1024, [[64, 16], [1024, 8], [1, 64]]),
            sp=False)
        t2 = dma("scalar", qA,
                 AP(BcrS[buf], 0, [[528, 16], [66, 8], [1, 64]]),
                 AP(d_Braw, 65536 + 8 * kb * 1024,
                    [[64, 16], [1024, 8], [1, 64]]))
        return t2

    E("vector", lambda e: e.memset(zhF[:, 0:66], 0.0))
    E("vector", lambda e: e.memset(zhS[:, 0:66], 0.0))
    E("vector", lambda e: e.memset(zhF[:, 64:65], 1.0))
    E("vector", lambda e: e.memset(zhS[:, 65:66], 1.0))
    wait("vector", qS, tq32)

    chain_state = {"c": 0, "loads": 8}

    def chain_step(sem=None):
        c = chain_state["c"]
        kb, cin = c // 8, c % 8
        buf = kb % 2
        if cin == 0:
            wait("vector", qA, bcr_t[buf])
        zFc = zhF[:, 66 * c:66 * c + 66]
        zSc = zhS[:, 66 * c:66 * c + 66]
        bF = BcrF[buf][:, 66 * cin:66 * cin + 66]
        bS = BcrS[buf][:, 66 * cin:66 * cin + 66]

        def qc(k):
            return q32ch[:, 64 * k + c:64 * k + c + 1]

        TS(cht[0][:], zFc, qc(0), None, OP.mult)
        TS(cht[1][:], zSc, qc(1), None, OP.mult)
        TTo(cht[0][:], cht[0][:], cht[1][:], OP.add)
        TTo(zhF[:, 66 * (c + 1):66 * (c + 1) + 66], cht[0][:], bF, OP.add)
        TS(cht[0][:], zFc, qc(2), None, OP.mult)
        TS(cht[1][:], zSc, qc(3), None, OP.mult)
        TTo(cht[0][:], cht[0][:], cht[1][:], OP.add)
        sem_eff = sem if sem is not None else (sDV if cin == 7 else None)
        lastm = TTo(zhS[:, 66 * (c + 1):66 * (c + 1) + 66], cht[0][:], bS,
                    OP.add, sem=sem_eff)
        chain_state["c"] = c + 1
        if cin == 7 and chain_state["loads"] < 8:
            kb2 = chain_state["loads"]
            wait("scalar", sDV, lastm)
            bcr_t[kb2 % 2] = bcr_load(kb2, kb2 % 2)
            chain_state["loads"] += 1
        return lastm

    wait("vector", qS, t_q32w)
    for buf2 in range(2):
        E("vector", lambda e, b2=buf2: e.memset(BcrF[b2][:], 0.0))
        E("vector", lambda e, b2=buf2: e.memset(BcrS[b2][:], 0.0))
    bm = op("vector", lambda e: e.engine_nop(), sDV)
    wait("scalar", sDV, bm)

    pre_chain_ac = sAC.n
    wait("scalar", qS, t_asm)
    wait("scalar", qS, u_w[-1])
    braw_w = []
    eb_pe = []
    chain_marks = []
    for G in range(16):
        if G > 0:
            wait("scalar", sPE, eb_pe[-1])
        dma("scalar", qA, LW[:, :], AP(d_LWf, G * 2048,
                                       [[32768, 32], [1, 2048]]), sp=False)
        dma("scalar", qA, RW[:, :], AP(d_RWf, G * 2176,
                                       [[34816, 32], [1, 2176]]), sp=False)
        tu = dma("scalar", qA, urd[G % 2][:],
                 AP(d_u, G * 128 * 1024, [[1024, 128], [1, 1024]]))
        wait("tensor", qA, tu)
        for sb8 in range(2):
            pe_m = None
            for gi in range(8):
                hh = sb8 * 8 + gi
                pe_m = mm(PS[5][:, 8 * gi:8 * gi + 8],
                          LW[:, 128 * hh:128 * hh + 128],
                          RW[:, 136 * hh + 128:136 * hh + 136],
                          True, True, sem=sPE if gi == 7 else None, skip=True)
            wait("vector", sPE, pe_m)
            dvm = CP(Hf8[:], PS[5][:, 0:64], sem=sDV)
            wait("tensor", sDV, dvm)
            pb = PS[6] if sb8 % 2 == 0 else PS[7]
            pe_m = None
            for gi in range(8):
                hh = sb8 * 8 + gi
                pe_m = mm(pb[0:8, 64 * gi:64 * gi + 64],
                          Hf8[:, 8 * gi:8 * gi + 8],
                          urd[G % 2][:, 64 * hh:64 * hh + 64],
                          True, True, sem=sPE if gi == 7 else None, skip=True)
            eb_pe.append(pe_m)
            wait("vector", sPE, pe_m)
            if len(braw_w) >= 1 and sb8 == 0:
                wait("vector", qS, braw_w[-1])
            dvm = CP(Bst[:, 512 * sb8:512 * sb8 + 512], pb[0:8, 0:512],
                     sem=sDV)
            if sb8 == 1:
                eb_last_dv = dvm
                wait("sync", sDV, dvm)
                for s in range(2):
                    tb = dma("sync", qS,
                             AP(d_Braw, s * 65536 + 4 * G * 1024,
                                [[1024, 4], [1, 1024]]),
                             AP(Bst, 4096 * s, [[1024, 4], [1, 1024]]),
                             sp=(s == 1))
                braw_w.append(tb)
        if G % 2 == 1:
            m_ = (G - 1) // 2
            if m_ >= 2:
                wait("scalar", sDV, chain_marks[m_ - 2])
            bcr_t[m_ % 2] = bcr_load(m_, m_ % 2)
            if m_ >= 1:
                for _ in range(8):
                    sem2 = sDV if chain_state["c"] == 63 else None
                    cm = chain_step(sem=sem2)
                chain_marks.append(cm)
    for _ in range(8):
        sem2 = sDV if chain_state["c"] == 63 else None
        cm = chain_step(sem=sem2)
    chain_marks.append(cm)
    t_earlyB_pe = eb_pe[-1]

    wait("gpsimd", sDV, cm)
    dma("gpsimd", qG, AP(d_zin, 0, [[64, 16], [1, 64]]),
        AP(zhF, 66 * 64, [[4290, 16], [1, 64]]), sp=False)
    tz = dma("gpsimd", qG, AP(d_zin, 16 * 64, [[64, 16], [1, 64]]),
             AP(zhS, 66 * 64, [[4290, 16], [1, 64]]))
    wait("gpsimd", qG, tz)
    ccz = sCC.inc(1)
    E("gpsimd", lambda e: e.collective_compute(
        "AllGather", OP.bypass,
        replica_groups=[[0, 1], [2, 3], [4, 5], [6, 7]],
        ins=[d_zin.ap().opt()], outs=[d_zout.ap().opt()]).then_inc(sCC.h, 1))
    wait("gpsimd", sCC, ccz)
    dma("gpsimd", qG, zgF[:], d_zout.ap()[0:16, :], sp=False)
    tz2 = dma("gpsimd", qG, zgS[:], d_zout.ap()[16:32, :])

    IC = [(0, 1024), (1024, 1024), (2048, 768)]
    wait("scalar", qG, t_repack)
    wait("scalar", sPE, t_passB)
    wait("tensor", sAC, pre_chain_ac)
    wait("tensor", sDV, eb_last_dv)
    hq_w = []
    hq_i = 0
    gu_last_pe = None
    hdone_l, sg_l = [], []
    for ic, (i0_, cw) in enumerate(IC):
        if ic > 0:
            wait("sync", sPE, gu_last_pe)
        last = None
        for b in range(8):
            dma("sync", qS, wgc[:, 1024 * b:1024 * b + cw],
                AP(d_wgf, i0_ * 1024 + 128 * b, [[1024, cw], [1, 128]]),
                transpose=True, sp=True)
            last = dma("sync", qS, wuc[:, 1024 * b:1024 * b + cw],
                       AP(d_wuf, i0_ * 1024 + 128 * b,
                          [[1024, cw], [1, 128]]),
                       transpose=True, sp=True)
        if ic == 0:
            wait("tensor", qS, t_xqfm)
        wait("tensor", qS, last)
        for i in range(NT):
            pp4 = 4 * (hq_i % 2)
            pga, pgb = PS[pp4], PS[pp4 + 1]
            pua, pub = PS[pp4 + 2], PS[pp4 + 3]
            if len(hdone_l) >= 2:
                wait("tensor", sDV, hdone_l[-2])
                wait("tensor", sAC, sg_l[-2])
            for b in range(8):
                lhs = xq_fm[:, T * b + 128 * i: T * b + 128 * i + 128]
                mm(pga[:, 0:512], lhs, wgc[:, 1024 * b:1024 * b + 512],
                   b == 0, b == 7)
                gate_pe = mm(pgb[:, 0:cw - 512], lhs,
                             wgc[:, 1024 * b + 512:1024 * b + cw],
                             b == 0, b == 7, sem=sPE if b == 7 else None)
            for b in range(8):
                lhs = xq_fm[:, T * b + 128 * i: T * b + 128 * i + 128]
                mm(pua[:, 0:512], lhs, wuc[:, 1024 * b:1024 * b + 512],
                   b == 0, b == 7)
                up_pe = mm(pub[:, 0:cw - 512], lhs,
                           wuc[:, 1024 * b + 512:1024 * b + cw],
                           b == 0, b == 7, sem=sPE if b == 7 else None)
            gu_last_pe = up_pe
            scr = xn_f if hq_i % 2 == 0 else ds_rd
            wait("scalar", sPE, gate_pe)
            E("scalar", lambda e, ss=scr, pa=pga: e.activation(
                ss[:, 0:512], pa[:, 0:512], AF.Silu))
            sg_ac = op("scalar", lambda e, o2=scr[:, 512:cw],
                       p2=pgb[:, 0:cw - 512]: e.activation(
                           o2, p2, AF.Silu), sAC)
            sg_l.append(sg_ac)
            wait("vector", sAC, sg_ac)
            wait("vector", sPE, up_pe)
            TTo(res_f[:, 0:512], scr[:, 0:512], pua[:, 0:512],
                OP.mult)
            hdone = TTo(res_f[:, 512:cw], scr[:, 512:cw],
                        pub[:, 0:cw - 512], OP.mult, sem=sDV)
            hdone_l.append(hdone)
            oq = (hq_i % 2) * 1024
            if hq_i >= 2:
                wait("vector", qS, hq_w[hq_i - 2])
            qm = quant_tile(res_f[:, 0:cw], cw, oq)
            wait("sync", sDV, qm)
            hq_w.append(dma("sync", qS,
                            AP(d_hq, 128 * i * I + i0_,
                               [[2816, 128], [1, cw]]),
                            qout[:, oq:oq + cw]))
            hq_i += 1

    wait("vector", qG, tz2)
    TS(zi_F[:], zgF[:], zmask_s[:, 0:1], None, OP.mult)
    TS(zi_S[:], zgS[:], zmask_s[:, 0:1], None, OP.mult)
    for ch5 in range(5):
        c0 = ch5 * 13
        ncc = 13
        zh3F = AP(zhF, 66 * c0, [[4290, 16], [66, ncc], [1, 64]])
        zh3S = AP(zhS, 66 * c0, [[4290, 16], [66, ncc], [1, 64]])
        zt3 = AP(ztmp, 0, [[1040, 16], [64, ncc], [1, 64]])
        pbF = AP(zhF, 66 * c0 + 64, [[4290, 16], [66, ncc], [0, 64]])
        pbF2 = AP(zhF, 66 * c0 + 65, [[4290, 16], [66, ncc], [0, 64]])
        pbS = AP(zhS, 66 * c0 + 64, [[4290, 16], [66, ncc], [0, 64]])
        pbS2 = AP(zhS, 66 * c0 + 65, [[4290, 16], [66, ncc], [0, 64]])
        zbF = AP(zi_F, 0, [[64, 16], [0, ncc], [1, 64]])
        zbS = AP(zi_S, 0, [[64, 16], [0, ncc], [1, 64]])
        TTo(zt3, pbF, zbF, OP.mult)
        TTo(zh3F, zh3F, zt3, OP.add)
        TTo(zt3, pbF2, zbS, OP.mult)
        TTo(zh3F, zh3F, zt3, OP.add)
        TTo(zt3, pbS, zbF, OP.mult)
        TTo(zh3S, zh3S, zt3, OP.add)
        TTo(zt3, pbS2, zbS, OP.mult)
        zc_dv = TTo(zh3S, zh3S, zt3, OP.add, sem=sDV)
    wait("gpsimd", sDV, zc_dv)
    dma("gpsimd", qG, AP(d_zc, 0, [[4290, 16], [1, 4290]]), zhF[:],
        sp=False)
    tzc = dma("gpsimd", qG,
              AP(d_zc, 16 * 65 * 66, [[4290, 16], [1, 4290]]), zhS[:])
    wait("gpsimd", qG, tzc)
    for m in range(4):
        for s in range(2):
            dst = AP(d_R2f, (2 * m + s) * 16384,
                     [[0, 1], [1024, 16], [64, 16], [1, 64]])
            src3 = AP(d_zc, s * 68640 + m * 66,
                      [[0, 1], [264, 16], [4290, 16], [1, 64]])
            dma("gpsimd", qG, dst, src3, sp=(m == 3 and s == 1))
    wait("gpsimd", qG, qG.n)
    t_r2f = qG.n
    t_hq = qS.n

    lw_prev_pe = t_earlyB_pe
    og_w = []
    wait("scalar", qS, g_w[-1])
    og_dv = None
    for G in range(16):
        wait("scalar", sPE, lw_prev_pe)
        if og_dv is not None:
            wait("scalar", sDV, og_dv)
        dma("scalar", qA, LW[:, :],
            AP(d_LWf, G * 2048, [[32768, 32], [1, 2048]]), sp=False)
        dma("scalar", qA, RW[:, :],
            AP(d_RWf, G * 2176, [[34816, 32], [1, 2176]]), sp=False)
        dma("scalar", qA, L2[:, :],
            AP(d_L2f, G * 2048, [[32768, 8], [1, 2048]]), sp=False)
        if G == 0:
            wait("scalar", qG, t_r2f)
        dma("scalar", qA, R2[:, :],
            AP(d_R2f, G * 1024, [[16384, 8], [1, 1024]]), sp=False)
        dma("scalar", qA, urd[G % 2][:],
            AP(d_u, G * 128 * 1024, [[1024, 128], [1, 1024]]), sp=False)
        dma("scalar", qA, qrd[:],
            AP(d_q, G * 128 * 1024, [[1024, 128], [1, 1024]]), sp=False)
        tf4 = dma("scalar", qA, grd[:],
                  AP(d_g, G * 128 * 1024, [[1024, 128], [1, 1024]]))
        wait("tensor", qA, tf4)
        if len(og_w) >= 2:
            wait("vector", qS, og_w[-2])
        for hhf in range(2):
            pyb = PS[6] if hhf == 0 else PS[7]
            for quad in range(2):
                pe_m = None
                for gq in range(4):
                    hh = hhf * 8 + quad * 4 + gq
                    pe_m = mm(PS[5][:, 128 * gq:128 * gq + 128],
                              LW[:, 128 * hh:128 * hh + 128],
                              RW[:, 136 * hh:136 * hh + 128],
                              True, True,
                              sem=sPE if gq == 3 else None, skip=True)
                wait("vector", sPE, pe_m)
                wtb = Wt_t[quad]
                dvm = TTo(wtb[:], PS[5][:], mask_s[:], OP.mult, sem=sDV)
                wait("tensor", sDV, dvm)
                for gq in range(4):
                    hh = hhf * 8 + quad * 4 + gq
                    gi = quad * 4 + gq
                    mm(pyb[:, 64 * gi:64 * gi + 64],
                       wtb[:, 128 * gq:128 * gq + 128],
                       urd[G % 2][:, 64 * hh:64 * hh + 64],
                       True, False, skip=True)
                    lw_prev_pe = mm(
                        pyb[:, 64 * gi:64 * gi + 64],
                        L2[:, 128 * hh:128 * hh + 128],
                        R2[:, 64 * hh:64 * hh + 64],
                        False, True,
                        sem=sPE if (quad == 1 and gq == 3) else None,
                        skip=True)
            wait("vector", sPE, lw_prev_pe)
            qsl = qrd[:, 512 * hhf:512 * hhf + 512]
            gsl = grd[:, 512 * hhf:512 * hhf + 512]
            TTo(ot_f[:], qsl, pyb[:], OP.mult)
            og_dv = TTo(ogt[G % 2][:, 512 * hhf:512 * hhf + 512],
                        ot_f[:], gsl, OP.mult, sem=sDV)
            wait("tensor", sDV, og_dv)
        wait("sync", sDV, og_dv)
        og_w.append(dma("sync", qS, d_og.ap()[128 * G:128 * G + 128, :],
                        ogt[G % 2][:]))
    t_ogw = qS.n
    t_lateW_pe = lw_prev_pe

    gw.close()

    ga = stack0.enter_context(contextlib.ExitStack())
    attn_tm = sb(ga, "attn_tm", [128, NT * D], BF16)
    gp8 = stack0.enter_context(contextlib.ExitStack())
    og_fm = sb(gp8, "og_fm", [128, 8 * T], BF16)
    woT_s = sb(gp8, "woT_s", [128, 8 * D], BF16)

    wait("sync", qS, t_ogw)
    for b in range(8):
        dma("sync", qS, og_fm[:, T * b:T * b + T],
            AP(d_og, 128 * b, [[1024, 2048], [1, 128]]), transpose=True,
            sp=True)
    t_ogfm = qS.n
    wo_cast = []
    wo_ld = []
    for b in range(8):
        if b >= 2:
            wait("gpsimd", sDV, wo_cast[b - 2])
        wo_ld.append(dma("gpsimd", qG, xt_f[b % 2][:],
                         woT_i[128 * b:128 * b + 128, :]))
        wait("vector", qG, max(wo_ld[min(1, b)], wo_ld[b]))
        wo_cast.append(CP(woT_s[:, D * b:D * b + D], xt_f[b % 2][:], sem=sDV))
    wait("tensor", qS, t_ogfm)
    wait("tensor", sDV, wo_cast[-1])
    for i in range(NT):
        for b in range(8):
            lhs = og_fm[:, T * b + 128 * i: T * b + 128 * i + 128]
            mm(PS[0][:], lhs, woT_s[:, D * b:D * b + 512], b == 0, b == 7)
            wo_pe = mm(PS[1][:], lhs, woT_s[:, D * b + 512:D * b + 1024],
                       b == 0, b == 7, sem=sPE if b == 7 else None)
        wait("scalar", sPE, wo_pe)
        E("scalar", lambda e, ii=i: e.activation(
            attn_tm[:, D * ii:D * ii + 512], PS[0][:], AF.Copy))
        aac = op("scalar", lambda e, ii=i: e.activation(
            attn_tm[:, D * ii + 512:D * ii + 1024], PS[1][:], AF.Copy), sAC)
        wait("tensor", sAC, aac)
    t_wo_pe = sPE.n

    gp8.close()
    gp9 = stack0.enter_context(contextlib.ExitStack())
    wdT_s = sb(gp9, "wdT_s", [128, 11 * 1024], BF16)
    hq_fm = sb(gp9, "hq_fm", [128, 11 * 1024], BF16)

    out_w, res_dv, dn_pe_l, ds_w = [], [], [], []
    wait("vector", qS, t_hq)
    wait("scalar", qG, t_repack)
    wait("tensor", sDV, sDV.n)
    wait("tensor", sAC, sAC.n)
    for ihalf in range(2):
        wait("sync", sPE, t_wo_pe if ihalf == 0 else dn_pe_l[-1])
        if ihalf == 1:
            wait("sync", qS, ds_w[-1])
        wdl = []
        for ib in range(11):
            ib2 = ihalf * 11 + ib
            wdl.append(dma("sync", qS,
                           wdT_s[:, 1024 * ib:1024 * ib + 1024],
                           AP(d_wdf, 128 * ib2, [[2816, 1024], [1, 128]]),
                           transpose=True, sp=True))
        for thalf in range(2):
            if ihalf == 0 and thalf == 0:
                wait("sync", qS, t_hq)
                wait("sync", sPE, t_wo_pe)
            if not (ihalf == 0 and thalf == 0):
                wait("sync", sPE, dn_pe_l[-1])
            hql = []
            for ib in range(11):
                ib2 = ihalf * 11 + ib
                hql.append(dma("sync", qS,
                               hq_fm[:, 1024 * ib:1024 * ib + 1024],
                               AP(d_hq, thalf * 1024 * I + 128 * ib2,
                                  [[2816, 1024], [1, 128]]), transpose=True,
                               sp=True))
            wait("tensor", qS, hql[-1])
            if thalf == 0:
                wait("tensor", qS, wdl[-1])
            for it in range(8):
                i = thalf * 8 + it
                nbuf = len(out_w) + len(ds_w)
                if nbuf >= 2:
                    if len(res_dv) >= 2:
                        wait("sync", sDV, res_dv[-2])
                    if ihalf == 0 and len(ds_w) >= 2:
                        wait("vector", qS, ds_w[-2])
                    if ihalf == 1 and len(out_w) >= 2:
                        wait("vector", qS, out_w[-2])
                if len(res_dv) >= 2:
                    wait("tensor", sDV, res_dv[-2])
                pba, pbb = PS[2 * (it % 2)], PS[2 * (it % 2) + 1]
                for ib in range(11):
                    lhs = hq_fm[:, 1024 * ib + 128 * it:
                                1024 * ib + 128 * it + 128]
                    mm(pba[:], lhs, wdT_s[:, 1024 * ib:1024 * ib + 512],
                       ib == 0, ib == 10)
                    dn_pe = mm(pbb[:], lhs,
                               wdT_s[:, 1024 * ib + 512:1024 * ib + 1024],
                               ib == 0, ib == 10,
                               sem=sPE if ib == 10 else None)
                dn_pe_l.append(dn_pe)
                if ihalf == 0:
                    wait("vector", sPE, dn_pe)
                    oh9 = (it % 2) * 1024
                    CP(qout[:, oh9:oh9 + 512], pba[:])
                    dvm = CP(qout[:, oh9 + 512:oh9 + 1024], pbb[:], sem=sDV)
                    res_dv.append(dvm)
                    wait("sync", sDV, dvm)
                    ds_w.append(dma("sync", qS,
                                    AP(d_ds, 128 * i * 1024,
                                       [[1024, 128], [1, 1024]]),
                                    qout[:, oh9:oh9 + 1024]))
                else:
                    xr = dma("sync", qS, xt_f[it % 2][:],
                             x_in[128 * i:128 * i + 128, :], sp=False)
                    oh9 = (it % 2) * 1024
                    dsr = dma("sync", qS, qout[:, oh9:oh9 + 1024],
                              AP(d_ds, 128 * i * 1024,
                                 [[1024, 128], [1, 1024]]))
                    wait("vector", qS, dsr)
                    wait("vector", sPE, dn_pe)
                    E("vector", lambda e, x=xt_f[it % 2], ii=i:
                      e.tensor_tensor(res_f[:], x[:],
                                      attn_tm[:, D * ii:D * ii + D], OP.add))
                    TTo(res_f[:], res_f[:], qout[:, oh9:oh9 + 1024], OP.add)
                    TTo(out_f[it % 2][:, 0:512], res_f[:, 0:512], pba[:],
                        OP.add)
                    dvm = TTo(out_f[it % 2][:, 512:1024], res_f[:, 512:1024],
                              pbb[:], OP.add, sem=sDV)
                    res_dv.append(dvm)
                    wait("sync", sDV, dvm)
                    out_w.append(dma("sync", qS,
                                     out_t[128 * i:128 * i + 128, :],
                                     out_f[it % 2][:]))
    wait("sync", qS, qS.n)
    wait("gpsimd", qS, qS.n)
    gp9.close()
    ga.close()

    with nc.Block() as blk:
        @blk.sync
        def _(e):
            for f in plan["sync"]:
                f(e)

        @blk.tensor
        def _(e):
            for f in plan["tensor"]:
                f(e)

        @blk.vector
        def _(e):
            for f in plan["vector"]:
                f(e)

        @blk.scalar
        def _(e):
            for f in plan["scalar"]:
                f(e)

        @blk.gpsimd
        def _(e):
            for f in plan["gpsimd"]:
                f(e)

    stack0.close()
    return nc


def _consts():
    c = {}
    m32 = 0.5 * np.triu(np.ones((32, 32), np.float32), 1)
    m128 = np.zeros((128, 128), np.float32)
    for m in range(4):
        m128[32 * m:32 * m + 32, 32 * m:32 * m + 32] = m32
    c["c_mask"] = np.tile(m128, (1, 4)).astype(bf16)
    c["c_eye"] = np.eye(128, dtype=np.float32)
    c["c_ones"] = np.ones((1, 128), np.float32).astype(bf16)
    qt = np.zeros((128, 32), np.float32)
    qt[:, 0:16] = 1.0
    qb = np.zeros((128, 32), np.float32)
    qb[:, 16:32] = 1.0
    c["c_qtop0"] = qt
    c["c_qbot0"] = qb
    c["c_lwz"] = np.zeros((32, 8192), np.float32)
    c["c_zb16"] = np.zeros((1, 32768), np.float32).astype(bf16)
    rws = np.zeros((32, 2176), np.float32)
    for gb in range(16):
        for m in range(4):
            for s in range(2):
                col = gb * 136 + 128 + 4 * s + m
                rws[8 * m + 2 + s, col] = 1.0
                rws[8 * m + 6 + s, col] = 1.0
    c["c_rws"] = rws
    return c


def kernel(L0_attn_x, L0_attn_Wq, L0_attn_Wk, L0_attn_Wv, L0_attn_Wo,
           L0_attn_Wb, L0_attn_Wfd, L0_attn_bfd, L0_attn_Wsd, L0_attn_bsd,
           L0_attn_Wg, L0_ffn_w_gate, L0_ffn_w_up, L0_ffn_w_down):
    if "nc" not in _CACHE:
        _CACHE["nc"] = build_program()
        _CACHE["consts"] = _consts()
    nc = _CACHE["nc"]
    consts = _CACHE["consts"]

    def f32(a):
        return np.ascontiguousarray(np.asarray(a, np.float32))

    x = f32(L0_attn_x)
    wqT = f32(np.asarray(L0_attn_Wq).T)
    wkT = f32(np.asarray(L0_attn_Wk).T)
    wvT = f32(np.asarray(L0_attn_Wv).T)
    wgT = f32(np.asarray(L0_attn_Wg).T)
    woT = f32(np.asarray(L0_attn_Wo).T)
    wbfs = np.concatenate([f32(L0_attn_Wb), f32(L0_attn_Wfd),
                           f32(L0_attn_Wsd)], 0)
    wbfsT = np.ascontiguousarray(wbfs.T)
    bias = np.concatenate([np.zeros(16, np.float32),
                           f32(L0_attn_bfd).ravel(),
                           f32(L0_attn_bsd).ravel()]).reshape(1, 48)
    wg_f = f32(L0_ffn_w_gate)
    wu_f = f32(L0_ffn_w_up)
    wd_f = f32(L0_ffn_w_down)

    in_maps = []
    for c in range(8):
        b, half = c // 2, c % 2
        gsh = np.zeros((384, D), np.float32)
        gsh[0:352] = wg_f[352 * c:352 * c + 352]
        ush = np.zeros((384, D), np.float32)
        ush[0:352] = wu_f[352 * c:352 * c + 352]
        dsh = np.ascontiguousarray(wd_f[128 * c:128 * c + 128])
        zm = np.full((16, 1), float(half), np.float32)
        m = {"x_s": np.ascontiguousarray(x[b, 2048 * half:2048 * half + 2048]),
             "wqT": wqT, "wkT": wkT, "wvT": wvT, "wgT": wgT, "woT": woT,
             "wbfsT": wbfsT, "bias_bfs": bias,
             "wg_sh": gsh, "wu_sh": ush, "wd_sh": dsh,
             "c_zmask": zm}
        m.update(consts)
        in_maps.append(m)

    res = run_bass_kernel_spmd(nc, in_maps, core_ids=list(range(8)))
    out = np.zeros((4, 4096, 1024), np.float32)
    for c in range(8):
        b, half = c // 2, c % 2
        out[b, 2048 * half:2048 * half + 2048] = res.results[c]["out"]
    return out
